# version 02
# baseline (speedup 1.0000x reference)
import sys
sys.path.insert(0, "/opt/trn_rl_repo")
from contextlib import ExitStack

import numpy as np
import concourse.bass as bass
import concourse.mybir as mybir
from concourse.bass_utils import run_bass_kernel_spmd

F32 = mybir.dt.float32
F32R = mybir.dt.float32r
BF16 = mybir.dt.bfloat16
I32 = mybir.dt.int32
AF = mybir.ActivationFunctionType
OP = mybir.AluOpType
AX = mybir.AxisListType

MAGIC = 12582912.0
EPS = 1e-5
D = 1024
E = 1024
EH = 512
NCH = 8
MCH = 4
NST = 16
T = 8192
ST = 512
NSLAB = T // ST
DMA_INC = 16


class Buf:
    __slots__ = ("writer", "readers", "name")

    def __init__(self, name=""):
        self.writer = None
        self.readers = []
        self.name = name


class _Op:
    __slots__ = ("engine", "fn", "inc", "deps", "semkey", "need_inc",
                 "val_after", "idx", "nsk")


class Prog:
    ENGINES = ("sync", "tensor", "vector", "scalar", "gpsimd")

    def __init__(self, nc):
        self.nc = nc
        self.ops = []
        self.barriers = []

    def barrier(self):
        self.barriers.append(len(self.ops))

    def _op(self, engine, fn, reads, writes, deps, inc, semkey, sreads=None):
        o = _Op()
        o.engine, o.fn, o.inc, o.semkey = engine, fn, inc, semkey
        o.need_inc = False
        o.val_after = None
        o.idx = len(self.ops)
        dep = set(deps or ())
        nsk = set()
        for b in reads or ():
            if b.writer is not None:
                dep.add(b.writer)
        for b in sreads or ():
            if b.writer is not None:
                dep.add(b.writer)
                nsk.add(b.writer)
        for b in writes or ():
            if b.writer is not None:
                dep.add(b.writer)
            dep.update(b.readers)
        o.deps = dep
        o.nsk = nsk
        self.ops.append(o)
        for b in reads or ():
            b.readers.append(o)
        for b in sreads or ():
            b.readers.append(o)
        for b in writes or ():
            b.writer = o
            b.readers = []
        return o

    def op(self, engine, fn, reads=None, writes=None, deps=None, sreads=None):
        return self._op(engine, fn, reads, writes, deps, 1, engine,
                        sreads=sreads)

    def dma(self, queue, fn, reads=None, writes=None, deps=None,
            private=False):
        key = queue + "dma"
        o = self._op(queue, fn, reads, writes, deps, DMA_INC, key)
        if private:
            o.semkey = "pvt%d" % o.idx
        return o

    def finalize(self, st, serial=False, noskip=False):
        import os
        trace_f = open("/tmp/prog.txt", "w") if os.environ.get("PROGTRACE") \
            else None
        nc = self.nc
        for o in self.ops:
            for d in o.deps:
                d.need_inc = True
        for o in self.ops:
            if o.inc == DMA_INC:
                o.need_inc = True
        cnt = {}
        run_cnt = []
        for o in self.ops:
            if o.need_inc:
                cnt[o.semkey] = cnt.get(o.semkey, 0) + o.inc
                o.val_after = cnt[o.semkey]
            run_cnt.append(dict(cnt))
        sems = {k: st.enter_context(nc.semaphore("s_" + k)) for k in cnt}
        per_engine = {e: [] for e in self.ENGINES}
        for o in self.ops:
            per_engine[o.engine].append(o)
        final_vals = dict(cnt)
        self_barriers = self.barriers
        block = st.enter_context(nc.Block())

        def section(name):
            def _(eng):
                hw = {}
                bar_i = 0
                for o in per_engine[name]:
                    nonlocal_barriers = self_barriers
                    while bar_i < len(nonlocal_barriers) and \
                            o.idx >= nonlocal_barriers[bar_i]:
                        bpos = nonlocal_barriers[bar_i]
                        if bpos > 0:
                            for k, v in run_cnt[bpos - 1].items():
                                if hw.get(k, 0) < v:
                                    eng.wait_ge(sems[k], v)
                                    hw[k] = v
                        bar_i += 1
                    if serial and o.idx > 0:
                        for k, v in run_cnt[o.idx - 1].items():
                            if hw.get(k, 0) < v:
                                eng.wait_ge(sems[k], v)
                                hw[k] = v
                    for d in sorted(o.deps, key=lambda x: x.idx):
                        if (not noskip) and d.engine == o.engine \
                                and d.inc != DMA_INC and d not in o.nsk:
                            continue
                        if d.val_after is None:
                            continue
                        if d.inc == DMA_INC:
                            val = run_cnt[o.idx - 1].get(d.semkey, 0) \
                                if o.idx > 0 else d.val_after
                            val = max(val, d.val_after)
                        else:
                            val = d.val_after
                        if (not noskip) and hw.get(d.semkey, 0) >= val:
                            continue
                        if trace_f is not None:
                            trace_f.write("%s   wait %s >= %d (dep #%d)\n" % (
                                name, d.semkey, val, d.idx))
                        eng.wait_ge(sems[d.semkey], val)
                        hw[d.semkey] = val
                    if trace_f is not None:
                        trace_f.write("%s #%d L%d sem=%s inc=%s\n" % (
                            name, o.idx, o.fn.__code__.co_firstlineno,
                            o.semkey,
                            o.val_after if o.need_inc else "-"))
                    inst = o.fn(eng)
                    if o.need_inc:
                        inst.then_inc(sems[o.semkey], o.inc)
                if name == "sync":
                    for k, v in final_vals.items():
                        eng.wait_ge(sems[k], v)
            return _

        for name in self.ENGINES:
            if per_engine[name] or name == "sync":
                getattr(block, name)(section(name))


def p1_drams(nc, xa_kind="ExternalOutput", debug=False):
    dram = lambda n, s, dt=F32, k="ExternalInput", **kw: nc.dram_tensor(
        n, s, dt, kind=k, **kw).ap()
    X = {}
    xb = X["xb"] = dram("xb", [T, D])
    inpw_x = X["inpw_x"] = dram("inpw_x", [D, E])
    inpw_z = X["inpw_z"] = dram("inpw_z", [D, EH])
    ln1_g = X["ln1_g"] = dram("ln1_g", [D])
    ln1_b = X["ln1_b"] = dram("ln1_b", [D])
    rms_w = X["rms_w"] = dram("rms_w", [D])
    conv_w = X["conv_w"] = dram("conv_w", [E, 4])
    conv_b = X["conv_b"] = dram("conv_b", [E])
    xprojw = X["xprojw"] = dram("xprojw", [E, 96])
    dtw_h = X["dtw_h"] = dram("dtw_h", [64, EH])
    dtb_h = X["dtb_h"] = dram("dtb_h", [EH])
    alog_h = X["alog_h"] = dram("alog_h", [EH, NST])
    dp_h = X["dp_h"] = dram("dp_h", [EH])
    opw = X["opw"] = dram("opw", [E, D])
    x0h = X["x0h"] = dram("x0h", [T // 2, D])
    blendm = X["blendm"] = dram("blendm", [1])
    xa_out = X["xa_out"] = dram("xa_out", [T // 2, D], k=xa_kind)
    bc_dram = X["bc_dram"] = dram("bc_dram", [32, T], k="Internal")
    NG = NSLAB // 2
    X["dbg"] = dram("dbg", [16, 128, 1024], k="ExternalOutput") if debug \
        else None
    X["agy_in"] = dram("agy_in", [NG, MCH, 128, 1024], BF16, k="Internal")
    X["agy_out"] = dram("agy_out", [NG, 2, MCH, 128, 1024], BF16,
                        k="Internal", addr_space="Local")
    return X


def emit_p1(nc, q, X, debug=False):
    (xb, inpw_x, inpw_z, ln1_g, ln1_b, rms_w, conv_w, conv_b, xprojw, dtw_h,
     dtb_h, alog_h, dp_h, opw, x0h, blendm, xa_out, bc_dram, dbg, agy_in,
     agy_out) = (X[k] for k in (
        "xb", "inpw_x", "inpw_z", "ln1_g", "ln1_b", "rms_w", "conv_w",
        "conv_b", "xprojw", "dtw_h", "dtb_h", "alog_h", "dp_h", "opw",
        "x0h", "blendm", "xa_out", "bc_dram", "dbg", "agy_in", "agy_out"))
    NG = NSLAB // 2

    st = ExitStack()
    sb = lambda name, shape, dt=F32: st.enter_context(
        nc.sbuf_tensor("A_" + name, shape, dt)).ap()
    mkpsum = lambda name, shape: st.enter_context(
        nc.psum_tensor("A_" + name, shape, F32)).ap()
    B = Buf

    with st:
        w_inp = [sb(f"w_inp{k}", [128, E + EH], BF16) for k in range(NCH)]
        w_xp = [sb(f"w_xp{k}", [128, 96]) for k in range(NCH)]
        w_dt = [sb(f"w_dt{c}", [64, 128], F32R) for c in range(MCH)]
        w_op = [sb(f"w_op{k}", [128, D], BF16) for k in range(NCH)]
        a_sb = sb("a_sb", [128, MCH * NST])
        cw_sb = [sb(f"cw{k}", [128, 4]) for k in range(NCH)]
        cb_sb = sb("cb_sb", [128, NCH])
        dtb_sb = sb("dtb_sb", [128, MCH])
        dp_sb = sb("dp_sb", [128, MCH])
        rmsw_sb = sb("rmsw_sb", [128, NCH])
        g1rep = sb("g1rep", [128, D], BF16)
        b1rep = sb("b1rep", [128, D], BF16)
        ident = sb("ident", [128, 128])
        identr = sb("identr", [128, 128], F32R)
        blm = sb("blm", [128, 2])
        eps_sb = sb("eps_sb", [128, 1])
        row_t = sb("row_t", [1, 1024])
        x0t = [sb(f"x0t{i}", [128, D]) for i in range(2)]
        aint = sb("aint", [128, D])
        ln1t = sb("ln1t", [128, D])
        rmst = sb("rmst", [128, D])
        stat = sb("stat", [128, 12])
        mvt = sb("mvt", [128, 6])
        smallt = sb("smallt", [128, 8])
        rms_fm = [sb(f"rms_fm{k}", [128, ST], BF16) for k in range(NCH)]
        xi = [sb(f"xi{k}", [128, ST + 3], BF16) for k in range(NCH)]
        cv = sb("cv", [128, ST])
        dt_r = sb("dt_r", [64, ST], F32R)
        bc_sb = sb("bc_sb", [32, ST])
        sp_t = sb("sp_t", [128, ST])
        sp_t2 = sb("sp_t2", [128, ST])
        xcm = [[sb(f"xcm{c}_{i}", [128, ST]) for c in range(MCH)]
               for i in range(2)]
        xcp = [sb(f"xcp{c}", [128, ST]) for c in range(MCH)]
        delta = [[sb(f"delta{c}_{i}", [128, ST]) for c in range(MCH)]
                 for i in range(2)]
        sz = [[sb(f"sz{c}_{i}", [128, ST], BF16) for c in range(MCH)]
              for i in range(2)]
        u_scr = sb("u_scr", [128, ST])
        da = [sb(f"da{i}", [128, ST]) for i in range(2)]
        bxt = [sb(f"bxt{i}", [128, ST]) for i in range(2)]
        hsc = [sb(f"hsc{i}", [128, ST]) for i in range(2)]
        prod = [sb(f"prod{i}", [128, ST], F32R) for i in range(2)]
        bbt = [sb(f"bbt{i}", [128, ST]) for i in range(2)]
        cbt = [sb(f"cbt{i}", [128, ST]) for i in range(2)]
        state = sb("state", [128, MCH * NST])
        t1 = sb("t1", [128, ST])
        y3t = sb("y3t", [128, ST], BF16)
        ye = sb("ye", [128, NCH * 128], BF16)
        yo = sb("yo", [128, NCH * 128], BF16)
        yb = sb("yb", [128, NCH * 128], BF16)
        x0p = sb("x0p", [128, D])
        stage = x0p
        pa = sb("pa", [128, D])
        pb = sb("pb", [128, D])
        pc = sb("pc", [128, D])
        xar = x0p
        stat2 = sb("stat2", [128, 12])
        mvt2 = sb("mvt2", [128, 6])
        small2 = sb("small2", [128, 8])
        pm = [mkpsum(f"pm{i}", [128, 512]) for i in range(2)]
        pt = mkpsum("pt", [128, 512])
        pd = mkpsum("pd", [128, 512])
        pys = mkpsum("pys", [128, 512])
        pop = mkpsum("pop", [128, D])

        b_winp = [B() for _ in range(NCH)]
        b_wxp = [B() for _ in range(NCH)]
        b_wdt = [B() for _ in range(MCH)]
        b_wop = [B() for _ in range(NCH)]
        b_a = B(); b_cw = [B() for _ in range(NCH)]
        b_cb = B(); b_dtb = B(); b_dp = B(); b_rmsw = B()
        b_g1 = B(); b_b1 = B(); b_id = B(); b_idr = B()
        b_blm = B(); b_row = B(); b_eps = B()
        b_x0t = [B(), B()]
        b_aint = B(); b_ln1t = B(); b_rmst = B()
        b_stat = B(); b_mvt = B(); b_small = B()
        b_rmsfm = [B() for _ in range(NCH)]
        b_xi = [B() for _ in range(NCH)]
        b_cv = B(); b_dtr = B(); b_bcsb = B()
        b_spt = B(); b_spt2 = B()
        b_xcm = [[B() for _ in range(MCH)] for _ in range(2)]
        b_xcp = [B() for _ in range(MCH)]
        b_delta = [[B() for _ in range(MCH)] for _ in range(2)]
        b_sz = [[B() for _ in range(MCH)] for _ in range(2)]
        b_uscr = B()
        b_da = [B(), B()]; b_bx = [B(), B()]; b_h = [B(), B()]
        b_prod = [B(), B()]; b_bbt = [B(), B()]; b_cbt = [B(), B()]
        b_state = B(); b_t1 = B(); b_y3t = B()
        b_ye = B(); b_yo = B(); b_yb = B()
        b_x0p = B(); b_pa = B(); b_pb = B(); b_pc = B()
        b_stat2 = B(); b_mvt2 = B(); b_small2 = B()
        b_pm = [B(), B()]
        b_pt = B(); b_pd = B(); b_pys = B(); b_pop = B()
        b_bcdram = B(); b_agyin = B(); b_agyout = B()

        def colload(vec, out_ap, out_buf, n):
            q.dma("sync", lambda e: e.dma_start(out=row_t[:, 0:n * 128],
                                                in_=vec[None, :]),
                  writes=[b_row], private=True)
            for c in range(n):
                q.op("tensor", lambda e, c=c: e.transpose(
                    pt[:, c:c + 1], in_=row_t[:, c * 128:(c + 1) * 128],
                    identity=ident[0:1, 0:1]),
                    reads=[b_row, b_id], writes=[b_pt])
            q.op("scalar", lambda e: e.activation(
                out=out_ap, in_=pt[:, 0:n], func=AF.Copy),
                reads=[b_pt], writes=[out_buf])

        def lnstats(xap, xbuf, sap, sbuf, mvap, mvbuf, mu_sl, r_sl, sc,
                    smbuf):
            s_d, s_d2, s_v, s_sd = sc
            for i in range(2):
                q.op("vector", lambda e, i=i: e.bn_stats(
                    out=sap[:, 6 * i:6 * i + 6],
                    in_=xap[:, 512 * i:512 * (i + 1)]),
                    reads=[xbuf], writes=[sbuf])
            q.op("vector", lambda e: e.bn_aggr(out=mvap[:, 0:2],
                                               in_=sap[:, 0:6]),
                 reads=[sbuf], writes=[mvbuf])
            q.op("vector", lambda e: e.bn_aggr(out=mvap[:, 2:4],
                                               in_=sap[:, 6:12]),
                 reads=[sbuf], writes=[mvbuf])
            q.op("vector", lambda e: e.tensor_scalar(
                out=mu_sl, in0=mvap[:, 0:1], scalar1=mvap[:, 2:3],
                scalar2=0.5, op0=OP.add, op1=OP.mult),
                reads=[mvbuf], sreads=[mvbuf], writes=[smbuf])
            q.op("vector", lambda e: e.tensor_scalar(
                out=s_d, in0=mvap[:, 0:1], scalar1=mvap[:, 2:3],
                scalar2=None, op0=OP.subtract),
                reads=[mvbuf], sreads=[mvbuf], writes=[smbuf])
            q.op("scalar", lambda e: e.activation(
                out=s_d2, in_=s_d, func=AF.Square, scale=0.5),
                reads=[smbuf], writes=[smbuf])
            q.op("vector", lambda e: e.tensor_scalar(
                out=s_v, in0=mvap[:, 1:2], scalar1=mvap[:, 3:4],
                scalar2=0.5, op0=OP.add, op1=OP.mult),
                reads=[mvbuf], sreads=[mvbuf], writes=[smbuf])
            q.op("vector", lambda e: e.tensor_scalar(
                out=s_v, in0=s_v, scalar1=s_d2, scalar2=None, op0=OP.add),
                reads=[smbuf], sreads=[smbuf], writes=[smbuf])
            q.op("scalar", lambda e: e.activation(
                out=s_sd, in_=s_v, func=AF.Sqrt, bias=eps_sb),
                reads=[smbuf], sreads=[b_eps], writes=[smbuf])
            q.op("vector", lambda e: e.reciprocal(out=r_sl, in_=s_sd),
                 reads=[smbuf], writes=[smbuf])

        q.op("vector", lambda e: e.memset(eps_sb, EPS), writes=[b_eps])
        q.op("gpsimd", lambda e: e.memset(ident, 0.0), writes=[b_id])
        q.op("gpsimd", lambda e: e.affine_select(
            out=ident, in_=ident, compare_op=OP.not_equal, fill=1.0, base=0,
            pattern=[[-1, 128]], channel_multiplier=1),
            reads=[b_id], writes=[b_id])
        q.op("vector", lambda e: e.tensor_copy(out=identr, in_=ident),
             reads=[b_id], writes=[b_idr])
        q.dma("gpsimd", lambda e: e.dma_start(
            out=g1rep, in_=ln1_g.unsqueeze(0).broadcast_to([128, D])),
            writes=[b_g1])
        q.dma("gpsimd", lambda e: e.dma_start(
            out=b1rep, in_=ln1_b.unsqueeze(0).broadcast_to([128, D])),
            writes=[b_b1])
        q.dma("sync", lambda e: e.dma_start(
            out=blm[:, 0:1], in_=blendm.unsqueeze(0).broadcast_to([128, 1])),
            writes=[b_blm])
        q.op("vector", lambda e: e.tensor_scalar(
            out=blm[:, 1:2], in0=blm[:, 0:1], scalar1=-1.0, scalar2=1.0,
            op0=OP.mult, op1=OP.add), reads=[b_blm], writes=[b_blm])
        colload(rms_w, rmsw_sb, b_rmsw, NCH)
        colload(conv_b, cb_sb, b_cb, NCH)
        colload(dtb_h, dtb_sb, b_dtb, MCH)
        colload(dp_h, dp_sb, b_dp, MCH)
        for c in range(MCH):
            q.dma("sync", lambda e, c=c: e.dma_start(
                out=stage[:, 0:NST], in_=alog_h[c * 128:(c + 1) * 128, :]),
                writes=[b_x0p])
            q.op("scalar", lambda e, c=c: e.activation(
                out=sp_t[:, 0:NST], in_=stage[:, 0:NST],
                func=AF.Exp), reads=[b_x0p], writes=[b_spt])
            q.op("vector", lambda e, c=c: e.tensor_single_scalar(
                out=a_sb[:, c * NST:(c + 1) * NST], in_=sp_t[:, 0:NST],
                scalar=-1.0, op=OP.mult),
                reads=[b_spt], writes=[b_a])
        for k in range(NCH):
            q.dma("sync", lambda e, k=k: e.dma_start(
                out=cw_sb[k], in_=conv_w[k * 128:(k + 1) * 128, :]),
                writes=[b_cw[k]])
            q.dma("sync", lambda e, k=k: e.dma_start(
                out=stage, in_=inpw_x[k * 128:(k + 1) * 128, :]),
                writes=[b_x0p])
            q.op("vector", lambda e, k=k: e.tensor_scalar(
                out=w_inp[k][:, 0:E], in0=stage,
                scalar1=rmsw_sb[:, k:k + 1], scalar2=None, op0=OP.mult),
                reads=[b_x0p], sreads=[b_rmsw], writes=[b_winp[k]])
            q.dma("sync", lambda e, k=k: e.dma_start(
                out=stage[:, 0:EH], in_=inpw_z[k * 128:(k + 1) * 128, :]),
                writes=[b_x0p])
            q.op("vector", lambda e, k=k: e.tensor_scalar(
                out=w_inp[k][:, E:E + EH], in0=stage[:, 0:EH],
                scalar1=rmsw_sb[:, k:k + 1], scalar2=None, op0=OP.mult),
                reads=[b_x0p], sreads=[b_rmsw], writes=[b_winp[k]])
            q.dma("sync", lambda e, k=k: e.dma_start(
                out=w_xp[k], in_=xprojw[k * 128:(k + 1) * 128, :]),
                writes=[b_wxp[k]])
            q.dma("sync", lambda e, k=k: e.dma_start(
                out=stage, in_=opw[k * 128:(k + 1) * 128, :]),
                writes=[b_x0p])
            q.op("scalar", lambda e, k=k: e.activation(
                out=w_op[k], in_=stage, func=AF.Copy),
                reads=[b_x0p], writes=[b_wop[k]])
        for c in range(MCH):
            q.dma("sync", lambda e, c=c: e.dma_start(
                out=stage[0:64, 0:128], in_=dtw_h[:, c * 128:(c + 1) * 128]),
                writes=[b_x0p])
            q.op("scalar", lambda e, c=c: e.activation(
                out=w_dt[c], in_=stage[0:64, 0:128], func=AF.Copy),
                reads=[b_x0p], writes=[b_wdt[c]])
        for k in range(NCH):
            q.op("vector", lambda e, k=k: e.memset(xi[k][:, 0:3], 0.0),
                 writes=[b_xi[k]])
        q.op("vector", lambda e: e.memset(state, 0.0), writes=[b_state])

        def emit_prep(s):
            xp = s % 2
            for tt in range(4):
                tok0 = s * ST + tt * 128
                q.dma("sync", lambda e, tok0=tok0, xp=xp: e.dma_start(
                    out=x0t[xp], in_=xb[tok0:tok0 + 128, :]),
                    writes=[b_x0t[xp]])
                lnstats(x0t[xp], b_x0t[xp], stat, b_stat, mvt, b_mvt,
                        smallt[:, 0:1], smallt[:, 1:2],
                        [smallt[:, 4:5], smallt[:, 5:6], smallt[:, 6:7],
                         smallt[:, 7:8]], b_small)
                q.op("vector", lambda e, xp=xp: e.tensor_scalar(
                    out=aint, in0=x0t[xp], scalar1=smallt[:, 0:1],
                    scalar2=smallt[:, 1:2], op0=OP.subtract, op1=OP.mult),
                    reads=[b_x0t[xp]], sreads=[b_small], writes=[b_aint])
                lnstats(aint, b_aint, stat, b_stat, mvt, b_mvt,
                        smallt[:, 2:3], smallt[:, 3:4],
                        [smallt[:, 4:5], smallt[:, 5:6], smallt[:, 6:7],
                         smallt[:, 7:8]], b_small)
                q.op("vector", lambda e: e.tensor_scalar(
                    out=ln1t, in0=aint, scalar1=smallt[:, 2:3],
                    scalar2=smallt[:, 3:4], op0=OP.subtract, op1=OP.mult),
                    reads=[b_aint], sreads=[b_small], writes=[b_ln1t])
                q.op("vector", lambda e: e.tensor_mul(
                    out=rmst, in0=ln1t, in1=g1rep),
                    reads=[b_ln1t, b_g1], writes=[b_rmst])
                q.op("vector", lambda e: e.tensor_add(
                    out=ln1t, in0=rmst, in1=b1rep),
                    reads=[b_rmst, b_b1], writes=[b_ln1t])
                for i in range(2):
                    q.op("vector", lambda e, i=i: e.bn_stats(
                        out=stat[:, 6 * i:6 * i + 6],
                        in_=ln1t[:, 512 * i:512 * (i + 1)]),
                        reads=[b_ln1t], writes=[b_stat])
                q.op("vector", lambda e: e.bn_aggr(out=mvt[:, 0:2],
                                                   in_=stat[:, 0:6]),
                     reads=[b_stat], writes=[b_mvt])
                q.op("vector", lambda e: e.bn_aggr(out=mvt[:, 2:4],
                                                   in_=stat[:, 6:12]),
                     reads=[b_stat], writes=[b_mvt])
                q.op("vector", lambda e: e.tensor_scalar(
                    out=mvt[:, 4:5], in0=mvt[:, 0:1], scalar1=mvt[:, 2:3],
                    scalar2=0.5, op0=OP.add, op1=OP.mult),
                    reads=[b_mvt], sreads=[b_mvt], writes=[b_mvt])
                q.op("vector", lambda e: e.tensor_scalar(
                    out=smallt[:, 4:5], in0=mvt[:, 0:1], scalar1=mvt[:, 2:3],
                    scalar2=None, op0=OP.subtract),
                    reads=[b_mvt], sreads=[b_mvt], writes=[b_small])
                q.op("scalar", lambda e: e.activation(
                    out=smallt[:, 5:6], in_=smallt[:, 4:5], func=AF.Square,
                    scale=0.5), reads=[b_small], writes=[b_small])
                q.op("vector", lambda e: e.tensor_scalar(
                    out=smallt[:, 6:7], in0=mvt[:, 1:2], scalar1=mvt[:, 3:4],
                    scalar2=0.5, op0=OP.add, op1=OP.mult),
                    reads=[b_mvt], sreads=[b_mvt], writes=[b_small])
                q.op("vector", lambda e: e.tensor_scalar(
                    out=smallt[:, 6:7], in0=smallt[:, 6:7],
                    scalar1=smallt[:, 5:6], scalar2=None, op0=OP.add),
                    reads=[b_small], sreads=[b_small], writes=[b_small])
                q.op("scalar", lambda e: e.activation(
                    out=smallt[:, 4:5], in_=mvt[:, 4:5], func=AF.Square),
                    reads=[b_mvt], writes=[b_small])
                q.op("vector", lambda e: e.tensor_scalar(
                    out=smallt[:, 6:7], in0=smallt[:, 6:7],
                    scalar1=smallt[:, 4:5], scalar2=None, op0=OP.add),
                    reads=[b_small], sreads=[b_small], writes=[b_small])
                q.op("scalar", lambda e: e.activation(
                    out=smallt[:, 7:8], in_=smallt[:, 6:7], func=AF.Sqrt,
                    bias=eps_sb),
                    reads=[b_small], sreads=[b_eps], writes=[b_small])
                q.op("vector", lambda e: e.reciprocal(
                    out=mvt[:, 5:6], in_=smallt[:, 7:8]),
                    reads=[b_small], writes=[b_mvt])
                q.op("scalar", lambda e: e.activation(
                    out=rmst, in_=ln1t, func=AF.Copy, scale=mvt[:, 5:6]),
                    reads=[b_ln1t], sreads=[b_mvt], writes=[b_rmst])
                for k in range(NCH):
                    q.op("tensor", lambda e, k=k: e.transpose(
                        pt[:, 0:128], in_=rmst[:, k * 128:(k + 1) * 128],
                        identity=ident),
                        reads=[b_rmst, b_id], writes=[b_pt])
                    q.op("scalar", lambda e, k=k, tt=tt: e.activation(
                        out=rms_fm[k][:, tt * 128:(tt + 1) * 128],
                        in_=pt[:, 0:128], func=AF.Copy),
                        reads=[b_pt], writes=[b_rmsfm[k]])
            for dc in range(12):
                pmx = dc % 2
                for k in range(NCH):
                    q.op("tensor", lambda e, dc=dc, k=k, pmx=pmx: e.matmul(
                        pm[pmx], lhsT=w_inp[k][:, dc * 128:(dc + 1) * 128],
                        rhs=rms_fm[k], start=(k == 0), stop=(k == NCH - 1)),
                        reads=[b_winp[k], b_rmsfm[k]], writes=[b_pm[pmx]])
                if dc < 8:
                    if s > 0:
                        q.op("vector", lambda e, dc=dc: e.tensor_copy(
                            out=xi[dc][:, 0:3], in_=xi[dc][:, ST:ST + 3]),
                            reads=[b_xi[dc]], writes=[b_xi[dc]])
                    q.op("scalar", lambda e, dc=dc, pmx=pmx: e.activation(
                        out=xi[dc][:, 3:ST + 3], in_=pm[pmx], func=AF.Copy),
                        reads=[b_pm[pmx]], writes=[b_xi[dc]])
                else:
                    c = dc - 8
                    q.op("scalar", lambda e, c=c, pmx=pmx, xp=xp: e.activation(
                        out=sz[xp][c], in_=pm[pmx], func=AF.Silu),
                        reads=[b_pm[pmx]], writes=[b_sz[xp][c]])
            for k in range(NCH):
                q.op("vector", lambda e, k=k: e.tensor_scalar(
                    out=cv, in0=xi[k][:, 0:ST], scalar1=cw_sb[k][:, 0:1],
                    scalar2=None, op0=OP.mult),
                    reads=[b_xi[k]], sreads=[b_cw[k]], writes=[b_cv])
                for j in (1, 2, 3):
                    q.op("vector", lambda e, k=k, j=j: e.scalar_tensor_tensor(
                        out=cv, in0=xi[k][:, j:ST + j],
                        scalar=cw_sb[k][:, j:j + 1], in1=cv,
                        op0=OP.mult, op1=OP.add),
                        reads=[b_xi[k], b_cv], sreads=[b_cw[k]], writes=[b_cv])
                if k < MCH:
                    dst, dstb = xcm[xp][k], b_xcm[xp][k]
                else:
                    dst, dstb = xcp[k - MCH], b_xcp[k - MCH]
                q.op("scalar", lambda e, k=k, dst=dst: e.activation(
                    out=dst, in_=cv, func=AF.Silu, bias=cb_sb[:, k:k + 1]),
                    reads=[b_cv], sreads=[b_cb], writes=[dstb])
            for k in range(NCH):
                src = (xcm[xp][k], b_xcm[xp][k]) if k < MCH else \
                      (xcp[k - MCH], b_xcp[k - MCH])
                q.op("tensor", lambda e, k=k, src=src: e.matmul(
                    pd[0:96, :], lhsT=w_xp[k], rhs=src[0],
                    start=(k == 0), stop=(k == NCH - 1)),
                    reads=[b_wxp[k], src[1]], writes=[b_pd])
            q.op("scalar", lambda e: e.activation(
                out=dt_r, in_=pd[0:64, :], func=AF.Copy),
                reads=[b_pd], writes=[b_dtr])
            q.op("scalar", lambda e: e.activation(
                out=bc_sb, in_=pd[64:96, :], func=AF.Copy),
                reads=[b_pd], writes=[b_bcsb])
            q.dma("sync", lambda e, s=s: e.dma_start(
                out=bc_dram[:, s * ST:(s + 1) * ST], in_=bc_sb),
                reads=[b_bcsb], writes=[b_bcdram])
            for c in range(MCH):
                pmx = c % 2
                q.op("tensor", lambda e, c=c, pmx=pmx: e.matmul(
                    pm[pmx], lhsT=w_dt[c], rhs=dt_r, start=True, stop=True),
                    reads=[b_wdt[c], b_dtr], writes=[b_pm[pmx]])
                q.op("scalar", lambda e, c=c, pmx=pmx: e.activation(
                    out=sp_t, in_=pm[pmx], func=AF.Relu,
                    bias=dtb_sb[:, c:c + 1]),
                    reads=[b_pm[pmx]], sreads=[b_dtb], writes=[b_spt])
                q.op("scalar", lambda e, c=c, pmx=pmx: e.activation(
                    out=sp_t2, in_=pm[pmx], func=AF.Abs,
                    bias=dtb_sb[:, c:c + 1]),
                    reads=[b_pm[pmx]], sreads=[b_dtb], writes=[b_spt2])
                q.op("scalar", lambda e: e.activation(
                    out=sp_t2, in_=sp_t2, func=AF.Exp, scale=-1.0),
                    reads=[b_spt2], writes=[b_spt2])
                q.op("scalar", lambda e: e.activation(
                    out=sp_t2, in_=sp_t2, func=AF.Ln, bias=1.0),
                    reads=[b_spt2], writes=[b_spt2])
                q.op("vector", lambda e, c=c, xp=xp: e.tensor_add(
                    out=delta[xp][c], in0=sp_t, in1=sp_t2),
                    reads=[b_spt, b_spt2], writes=[b_delta[xp][c]])

        def emit_scan(s):
            xp = s % 2
            for c in range(MCH):
                q.op("vector", lambda e, c=c, xp=xp: e.tensor_mul(
                    out=u_scr, in0=delta[xp][c], in1=xcm[xp][c]),
                    reads=[b_delta[xp][c], b_xcm[xp][c]], writes=[b_uscr])
                for n in range(NST):
                    idx = c * NST + n
                    p2 = n % 2
                    q.op("scalar", lambda e, c=c, n=n, idx=idx, p2=p2, xp=xp:
                         e.activation(out=da[p2], in_=delta[xp][c],
                                      func=AF.Exp,
                                      scale=a_sb[:, idx:idx + 1]),
                         reads=[b_delta[xp][c]], sreads=[b_a],
                         writes=[b_da[p2]])
                    q.dma("scalar", lambda e, n=n, p2=p2, s=s: e.dma_start(
                        out=bbt[p2],
                        in_=bc_dram[n, s * ST:(s + 1) * ST]
                        .unsqueeze(0).broadcast_to([128, ST])),
                        reads=[b_bcdram], writes=[b_bbt[p2]])
                    q.op("gpsimd", lambda e, p2=p2: e.tensor_mul(
                        out=bxt[p2], in0=u_scr, in1=bbt[p2]),
                        reads=[b_uscr, b_bbt[p2]], writes=[b_bx[p2]])
                    q.op("vector", lambda e, p2=p2, idx=idx: e.tensor_tensor_scan(
                        out=hsc[p2], data0=da[p2], data1=bxt[p2],
                        initial=state[:, idx:idx + 1],
                        op0=OP.mult, op1=OP.add),
                        reads=[b_da[p2], b_bx[p2]], sreads=[b_state],
                        writes=[b_h[p2]])
                    q.op("vector", lambda e, p2=p2, idx=idx: e.tensor_copy(
                        out=state[:, idx:idx + 1], in_=hsc[p2][:, ST - 1:ST]),
                        reads=[b_h[p2]], writes=[b_state])
                    q.dma("scalar", lambda e, n=n, p2=p2, s=s: e.dma_start(
                        out=cbt[p2],
                        in_=bc_dram[16 + n, s * ST:(s + 1) * ST]
                        .unsqueeze(0).broadcast_to([128, ST])),
                        reads=[b_bcdram], writes=[b_cbt[p2]])
                    peng = "vector" if n % 2 == 0 else "gpsimd"
                    q.op(peng, lambda e, p2=p2: e.tensor_mul(
                        out=prod[p2], in0=hsc[p2], in1=cbt[p2]),
                        reads=[b_h[p2], b_cbt[p2]], writes=[b_prod[p2]])
                    q.op("tensor", lambda e, p2=p2, n=n: e.matmul(
                        pys, lhsT=identr, rhs=prod[p2],
                        start=(n == 0), stop=(n == NST - 1)),
                        reads=[b_idr, b_prod[p2]], writes=[b_pys])
                q.op("vector", lambda e, c=c, xp=xp: e.scalar_tensor_tensor(
                    out=t1, in0=xcm[xp][c], scalar=dp_sb[:, c:c + 1], in1=pys,
                    op0=OP.mult, op1=OP.add),
                    reads=[b_xcm[xp][c], b_pys], sreads=[b_dp],
                    writes=[b_t1])
                q.op("vector", lambda e, c=c, xp=xp: e.tensor_mul(
                    out=y3t, in0=t1, in1=sz[xp][c]),
                    reads=[b_t1, b_sz[xp][c]], writes=[b_y3t])
                q.dma("sync", lambda e, c=c, s=s: e.dma_start(
                    out=agy_in[s // 2, c, :, (s % 2) * ST:(s % 2) * ST + ST],
                    in_=y3t),
                    reads=[b_y3t], writes=[b_agyin])

        coll_ops = []

        def emit_ag(g):
            o = q.op("gpsimd", lambda e, g=g: e.collective_compute(
                "AllGather", OP.bypass,
                replica_groups=[[0, 1], [2, 3], [4, 5], [6, 7]],
                ins=[agy_in[g]],
                outs=[agy_out[g]]),
                reads=[b_agyin], writes=[b_agyout])
            coll_ops.append(o)

        def emit_post(ps):
            agdep = coll_ops[ps]
            for tt in range(4):
                q.dma("sync", lambda e, ps=ps, tt=tt: e.dma_start(
                    out=ye.rearrange("p (k f) -> p k f", f=128),
                    in_=agy_out[ps, :, :, :, tt * 128:tt * 128 + 128]
                    .rearrange("s c p f -> p (s c) f")),
                    deps=[agdep], writes=[b_ye])
                q.dma("sync", lambda e, ps=ps, tt=tt: e.dma_start(
                    out=yo.rearrange("p (k f) -> p k f", f=128),
                    in_=agy_out[ps, :, :, :,
                                512 + tt * 128:640 + tt * 128]
                    .rearrange("s c p f -> p (s c) f")),
                    deps=[agdep], writes=[b_yo])
                q.op("vector", lambda e: e.tensor_scalar(
                    out=yb, in0=yo, scalar1=blm[:, 0:1], scalar2=None,
                    op0=OP.mult), reads=[b_yo], sreads=[b_blm],
                    writes=[b_yb])
                q.op("vector", lambda e: e.scalar_tensor_tensor(
                    out=yb, in0=ye, scalar=blm[:, 1:2], in1=yb,
                    op0=OP.mult, op1=OP.add),
                    reads=[b_ye, b_yb], sreads=[b_blm], writes=[b_yb])
                for hh in range(2):
                    for k in range(NCH):
                        q.op("tensor", lambda e, k=k, hh=hh: e.matmul(
                            pop[:, hh * 512:(hh + 1) * 512],
                            lhsT=yb[:, k * 128:(k + 1) * 128],
                            rhs=w_op[k][:, hh * 512:(hh + 1) * 512],
                            start=(k == 0), stop=(k == NCH - 1)),
                            reads=[b_yb, b_wop[k]], writes=[b_pop])
                r0 = ps * ST + tt * 128
                q.dma("sync", lambda e, r0=r0: e.dma_start(
                    out=x0p, in_=x0h[r0:r0 + 128, :]), writes=[b_x0p])
                lnstats(x0p, b_x0p, stat2, b_stat2, mvt2, b_mvt2,
                        small2[:, 0:1], small2[:, 1:2],
                        [small2[:, 4:5], small2[:, 5:6], small2[:, 6:7],
                         small2[:, 7:8]], b_small2)
                q.op("vector", lambda e: e.tensor_scalar(
                    out=pa, in0=x0p, scalar1=small2[:, 0:1],
                    scalar2=small2[:, 1:2], op0=OP.subtract, op1=OP.mult),
                    reads=[b_x0p], sreads=[b_small2], writes=[b_pa])
                lnstats(pa, b_pa, stat2, b_stat2, mvt2, b_mvt2,
                        small2[:, 2:3], small2[:, 3:4],
                        [small2[:, 4:5], small2[:, 5:6], small2[:, 6:7],
                         small2[:, 7:8]], b_small2)
                q.op("vector", lambda e: e.tensor_scalar(
                    out=pb, in0=pa, scalar1=small2[:, 2:3],
                    scalar2=small2[:, 3:4], op0=OP.subtract, op1=OP.mult),
                    reads=[b_pa], sreads=[b_small2], writes=[b_pb])
                q.op("gpsimd", lambda e: e.tensor_mul(
                    out=pc, in0=pb, in1=g1rep),
                    reads=[b_pb, b_g1], writes=[b_pc])
                q.op("gpsimd", lambda e: e.tensor_add(
                    out=pb, in0=pc, in1=b1rep),
                    reads=[b_pc, b_b1], writes=[b_pb])
                q.op("gpsimd", lambda e: e.tensor_add(
                    out=pc, in0=pa, in1=pb),
                    reads=[b_pa, b_pb], writes=[b_pc])
                q.op("vector", lambda e: e.tensor_add(
                    out=xar, in0=pc, in1=pop),
                    reads=[b_pc, b_pop], writes=[b_x0p])
                q.dma("sync", lambda e, r0=r0: e.dma_start(
                    out=xa_out[r0:r0 + 128, :], in_=xar),
                    reads=[b_x0p])

        def emit_debug0():
            q.dma("sync", lambda e: e.dma_start(out=dbg[0], in_=aint),
                  reads=[b_aint])
            q.dma("sync", lambda e: e.dma_start(out=dbg[1], in_=ln1t),
                  reads=[b_ln1t])
            q.dma("sync", lambda e: e.dma_start(out=dbg[2], in_=rmst),
                  reads=[b_rmst])
            q.dma("gpsimd", lambda e: e.dma_start(
                out=dbg[3][:, 0:515], in_=xi[0]), reads=[b_xi[0]])
            q.dma("sync", lambda e: e.dma_start(
                out=dbg[4][:, 0:512], in_=xcm[0][0]), reads=[b_xcm[0][0]])
            q.dma("gpsimd", lambda e: e.dma_start(
                out=dbg[5][:, 0:512], in_=xcp[0]), reads=[b_xcp[0]])
            q.dma("sync", lambda e: e.dma_start(
                out=dbg[6][:, 0:512], in_=delta[0][0]), reads=[b_delta[0][0]])
            q.dma("gpsimd", lambda e: e.dma_start(
                out=dbg[7][:, 0:512], in_=sz[0][0]), reads=[b_sz[0][0]])
            q.dma("sync", lambda e: e.dma_start(
                out=dbg[8][0:32, 0:512], in_=bc_sb), reads=[b_bcsb])
            q.dma("sync", lambda e: e.dma_start(
                out=dbg[8][:, 600:606], in_=mvt), reads=[b_mvt])
            q.dma("sync", lambda e: e.dma_start(
                out=dbg[8][:, 610:618], in_=smallt), reads=[b_small])

        def emit_debug1():
            q.dma("sync", lambda e: e.dma_start(
                out=dbg[9][:, 0:64], in_=state), reads=[b_state])
            q.dma("gpsimd", lambda e: e.dma_start(
                out=dbg[10][:, 0:512], in_=y3t), reads=[b_y3t])
            q.dma("sync", lambda e: e.dma_start(
                out=dbg[11][:, 0:512], in_=t1), reads=[b_t1])
            q.dma("sync", lambda e: e.dma_start(
                out=dbg[12][:, 0:512], in_=hsc[1]), reads=[b_h[1]])
            q.dma("sync", lambda e: e.dma_start(
                out=dbg[13][:, 0:512], in_=da[1]), reads=[b_da[1]])
            q.dma("sync", lambda e: e.dma_start(
                out=dbg[14][:, 0:512], in_=bxt[1]), reads=[b_bx[1]])
            q.dma("sync", lambda e: e.dma_start(
                out=dbg[15][:, 0:64], in_=a_sb), reads=[b_a])

        for s in range(NSLAB):
            emit_prep(s)
            if debug and s == 0:
                emit_debug0()
            if s > 0:
                emit_scan(s - 1)
                if debug and s == 1:
                    emit_debug1()
                if s % 2 == 0:
                    emit_ag(s // 2 - 1)
                    emit_post(s // 2 - 1)
        emit_scan(NSLAB - 1)
        emit_ag(NSLAB // 2 - 1)
        emit_post(NSLAB // 2 - 1)



def build_p1(xa_kind="ExternalOutput", debug=False, serial=False,
             noskip=False):
    nc = bass.Bass("TRN2", target_bir_lowering=False)
    q = Prog(nc)
    X = p1_drams(nc, xa_kind, debug)
    emit_p1(nc, q, X, debug=debug)
    with ExitStack() as fst:
        q.finalize(fst, serial=serial, noskip=noskip)
    return nc


def stage_p1_inputs(ins, core):
    b, ch = core >> 1, core & 1
    x = np.asarray(ins["L0_attn_x"])
    mine = slice(ch * EH, (ch + 1) * EH)
    other = slice((1 - ch) * EH, (2 - ch) * EH)
    perm = np.r_[np.arange(ch * EH, (ch + 1) * EH),
                 np.arange((1 - ch) * EH, (2 - ch) * EH)]
    inpw = np.asarray(ins["L0_attn_in_proj_w"])
    d = {
        "xb": np.ascontiguousarray(x[b]),
        "inpw_x": np.ascontiguousarray(inpw[:, perm]),
        "inpw_z": np.ascontiguousarray(inpw[:, 1024 + ch * EH:1024 + (ch + 1) * EH]),
        "ln1_g": np.ascontiguousarray(ins["L0_attn_ln1_g"]),
        "ln1_b": np.ascontiguousarray(ins["L0_attn_ln1_b"]),
        "rms_w": np.ascontiguousarray(ins["L0_attn_rms_w"]),
        "conv_w": np.ascontiguousarray(np.asarray(ins["L0_attn_conv_w"])[perm]),
        "conv_b": np.ascontiguousarray(np.asarray(ins["L0_attn_conv_b"])[perm]),
        "xprojw": np.ascontiguousarray(np.asarray(ins["L0_attn_x_proj_w"])[perm]),
        "dtw_h": np.ascontiguousarray(np.asarray(ins["L0_attn_dt_proj_w"])[:, mine]),
        "dtb_h": np.ascontiguousarray(np.asarray(ins["L0_attn_dt_proj_b"])[mine]),
        "alog_h": np.ascontiguousarray(np.asarray(ins["L0_attn_A_log"])[mine]),
        "dp_h": np.ascontiguousarray(np.asarray(ins["L0_attn_D_param"])[mine]),
        "opw": np.ascontiguousarray(ins["L0_attn_out_proj_w"]),
        "x0h": np.ascontiguousarray(
            x[b].reshape(NSLAB, ST, D)[ch::2].reshape(T // 2, D)),
        "blendm": np.array([float(ch)], np.float32),
    }
    return {k: np.asarray(v, dtype=np.float32) for k, v in d.items()}




def p2_drams(nc, xa_kind="ExternalInput", x1_kind="ExternalOutput",
             x0h=None, xa_in=None):
    dram = lambda n, s, dt=F32, k="ExternalInput", **kw: nc.dram_tensor(
        n, s, dt, kind=k, **kw).ap()
    X = {}
    X["xa_in"] = xa_in if xa_in is not None else \
        dram("xa_in", [T // 2, D], k=xa_kind)
    X["x0h"] = x0h if x0h is not None else dram("x0h", [T // 2, D])
    X["w1d"] = dram("w1d", [D, 4 * D])
    X["b1d"] = dram("b1d", [4 * D])
    X["w2d"] = dram("w2d", [4 * D, D])
    X["b2d"] = dram("b2d", [D])
    X["g2d"] = dram("g2d", [D])
    X["bt2d"] = dram("bt2d", [D])
    X["x1_out"] = dram("x1_out", [T // 2, D], k=x1_kind)
    return X


def emit_p2(nc, q, X):
    (xa_in, x0h, w1d, b1d, w2d, b2d, g2d, bt2d, x1_out) = (X[k] for k in (
        "xa_in", "x0h", "w1d", "b1d", "w2d", "b2d", "g2d", "bt2d", "x1_out"))
    ST2 = 256
    NS2 = (T // 2) // ST2

    st = ExitStack()
    sb = lambda name, shape, dt=F32: st.enter_context(
        nc.sbuf_tensor("B_" + name, shape, dt)).ap()
    mkpsum = lambda name, shape: st.enter_context(
        nc.psum_tensor("B_" + name, shape, F32)).ap()
    B = Buf
    with st:
        w1 = [sb(f"w1_{k}", [128, 4 * D], BF16) for k in range(NCH)]
        w2 = [sb(f"w2_{k}", [128, D], BF16) for k in range(32)]
        b1c = sb("b1c", [128, 32])
        b2row = sb("b2row", [1, D], BF16)
        onesb = sb("onesb", [1, 128], BF16)
        g2c = sb("g2c", [128, NCH])
        bt2c = sb("bt2c", [128, NCH])
        identb = sb("identb", [128, 128], BF16)
        ident = sb("ident", [128, 128])
        eps_sb = sb("eps_sb", [128, 1])
        row_t = sb("row_t", [1, 1024])
        stage = sb("stage", [128, 2048])
        b1row = sb("b1row", [1, 4 * D])
        xat = sb("xat", [128, D])
        x0t = sb("x0t", [128, D])
        ln2t = sb("ln2t", [128, D])
        ln2fm = [sb(f"ln2fm{k}", [128, ST2], BF16) for k in range(NCH)]
        g1 = [sb(f"g1_{dc}", [128, ST2], BF16) for dc in range(32)]
        x1t = sb("x1t", [128, D])
        stat = sb("stat", [128, 12])
        mvt = sb("mvt", [128, 6])
        smallt = sb("smallt", [128, 8])
        pm = [mkpsum(f"pm{i}", [128, 512]) for i in range(2)]
        pt = mkpsum("pt", [128, 512])
        pop = mkpsum("pop", [128, D])
        pb1 = mkpsum("pb1", [128, 512])

        b_w1 = [B() for _ in range(NCH)]
        b_w2 = [B() for _ in range(32)]
        b_b1c = B(); b_b2r = B(); b_ones = B(); b_g2 = B(); b_bt2 = B()
        b_idb = B(); b_id = B(); b_eps = B(); b_row = B(); b_stage = B()
        b_b1row = B()
        b_xat = B(); b_x0t = B(); b_ln2t = B()
        b_ln2fm = [B() for _ in range(NCH)]
        b_g1 = [B() for _ in range(32)]
        b_x1t = B(); b_stat = B(); b_mvt = B(); b_small = B()
        b_pm = [B(), B()]; b_pt = B(); b_pop = B(); b_pb1 = B()

        def colload(vec, n, out_ap, out_buf):
            q.dma("sync", lambda e: e.dma_start(out=row_t[:, 0:n * 128],
                                                in_=vec[None, :]),
                  writes=[b_row], private=True)
            for c in range(n):
                q.op("tensor", lambda e, c=c: e.transpose(
                    pt[:, c:c + 1], in_=row_t[:, c * 128:(c + 1) * 128],
                    identity=ident[0:1, 0:1]),
                    reads=[b_row, b_id], writes=[b_pt])
            q.op("scalar", lambda e: e.activation(
                out=out_ap, in_=pt[:, 0:n], func=AF.Copy),
                reads=[b_pt], writes=[out_buf])

        def lnstats(xap, xbuf, mu_sl, r_sl):
            for i in range(2):
                q.op("vector", lambda e, i=i: e.bn_stats(
                    out=stat[:, 6 * i:6 * i + 6],
                    in_=xap[:, 512 * i:512 * (i + 1)]),
                    reads=[xbuf], writes=[b_stat])
            q.op("vector", lambda e: e.bn_aggr(out=mvt[:, 0:2],
                                               in_=stat[:, 0:6]),
                 reads=[b_stat], writes=[b_mvt])
            q.op("vector", lambda e: e.bn_aggr(out=mvt[:, 2:4],
                                               in_=stat[:, 6:12]),
                 reads=[b_stat], writes=[b_mvt])
            q.op("vector", lambda e: e.tensor_scalar(
                out=mu_sl, in0=mvt[:, 0:1], scalar1=mvt[:, 2:3],
                scalar2=0.5, op0=OP.add, op1=OP.mult),
                reads=[b_mvt], sreads=[b_mvt], writes=[b_small])
            q.op("vector", lambda e: e.tensor_scalar(
                out=smallt[:, 4:5], in0=mvt[:, 0:1], scalar1=mvt[:, 2:3],
                scalar2=None, op0=OP.subtract),
                reads=[b_mvt], sreads=[b_mvt], writes=[b_small])
            q.op("scalar", lambda e: e.activation(
                out=smallt[:, 5:6], in_=smallt[:, 4:5], func=AF.Square,
                scale=0.5), reads=[b_small], writes=[b_small])
            q.op("vector", lambda e: e.tensor_scalar(
                out=smallt[:, 6:7], in0=mvt[:, 1:2], scalar1=mvt[:, 3:4],
                scalar2=0.5, op0=OP.add, op1=OP.mult),
                reads=[b_mvt], sreads=[b_mvt], writes=[b_small])
            q.op("vector", lambda e: e.tensor_scalar(
                out=smallt[:, 6:7], in0=smallt[:, 6:7],
                scalar1=smallt[:, 5:6], scalar2=None, op0=OP.add),
                reads=[b_small], sreads=[b_small], writes=[b_small])
            q.op("scalar", lambda e: e.activation(
                out=smallt[:, 7:8], in_=smallt[:, 6:7], func=AF.Sqrt,
                bias=eps_sb),
                reads=[b_small], sreads=[b_eps], writes=[b_small])
            q.op("vector", lambda e: e.reciprocal(out=r_sl,
                                                  in_=smallt[:, 7:8]),
                 reads=[b_small], writes=[b_small])

        q.op("vector", lambda e: e.memset(eps_sb, EPS), writes=[b_eps])
        q.op("gpsimd", lambda e: e.memset(ident, 0.0), writes=[b_id])
        q.op("gpsimd", lambda e: e.affine_select(
            out=ident, in_=ident, compare_op=OP.not_equal, fill=1.0, base=0,
            pattern=[[-1, 128]], channel_multiplier=1),
            reads=[b_id], writes=[b_id])
        q.op("vector", lambda e: e.tensor_copy(out=identb, in_=ident),
             reads=[b_id], writes=[b_idb])
        q.op("vector", lambda e: e.memset(onesb, 1.0), writes=[b_ones])
        colload(g2d, NCH, g2c, b_g2)
        colload(bt2d, NCH, bt2c, b_bt2)
        q.dma("gpsimd", lambda e: e.dma_start(out=b2row, in_=b2d[None, :]),
              writes=[b_b2r], private=True)
        q.dma("sync", lambda e: e.dma_start(out=b1row, in_=b1d[None, :]),
              writes=[b_b1row], private=True)
        for k in range(NCH):
            for h in range(2):
                q.dma("sync", lambda e, k=k, h=h: e.dma_start(
                    out=stage, in_=w1d[k * 128:(k + 1) * 128,
                                       h * 2048:(h + 1) * 2048]),
                    writes=[b_stage])
                q.op("vector", lambda e, k=k, h=h: e.tensor_scalar(
                    out=w1[k][:, h * 2048:(h + 1) * 2048], in0=stage,
                    scalar1=g2c[:, k:k + 1], scalar2=None, op0=OP.mult),
                    reads=[b_stage], sreads=[b_g2], writes=[b_w1[k]])
        for g in range(8):
            for k in range(NCH):
                q.dma("sync", lambda e, k=k, g=g: e.dma_start(
                    out=stage[:, 0:512],
                    in_=w1d[k * 128:(k + 1) * 128, g * 512:(g + 1) * 512]),
                    writes=[b_stage])
                q.op("tensor", lambda e, k=k, g=g: e.matmul(
                    pb1[0:1, :], lhsT=bt2c[:, k:k + 1], rhs=stage[:, 0:512],
                    start=(k == 0), stop=(k == NCH - 1)),
                    reads=[b_stage, b_bt2], writes=[b_pb1])
            q.op("vector", lambda e, g=g: e.tensor_tensor(
                out=row_t[0:1, 0:512], in0=pb1[0:1, 0:512],
                in1=b1row[0:1, g * 512:(g + 1) * 512], op=OP.add),
                reads=[b_pb1, b_b1row], writes=[b_row])
            for c2 in range(4):
                q.op("tensor", lambda e, g=g, c2=c2: e.transpose(
                    pt[:, g * 4 + c2:g * 4 + c2 + 1],
                    in_=row_t[0:1, c2 * 128:(c2 + 1) * 128],
                    identity=ident[0:1, 0:1]),
                    reads=[b_row, b_id], writes=[b_pt])
        q.op("scalar", lambda e: e.activation(
            out=b1c, in_=pt[:, 0:32], func=AF.Copy),
            reads=[b_pt], writes=[b_b1c])
        for kc in range(32):
            q.dma("sync", lambda e, kc=kc: e.dma_start(
                out=stage[:, 0:1024], in_=w2d[kc * 128:(kc + 1) * 128, :]),
                writes=[b_stage])
            q.op("scalar", lambda e, kc=kc: e.activation(
                out=w2[kc], in_=stage[:, 0:1024], func=AF.Copy),
                reads=[b_stage], writes=[b_w2[kc]])

        for s2 in range(NS2):
            for tt in range(2):
                r0 = s2 * ST2 + tt * 128
                q.dma("sync", lambda e, r0=r0: e.dma_start(
                    out=xat, in_=xa_in[r0:r0 + 128, :]), writes=[b_xat])
                lnstats(xat, b_xat, smallt[:, 0:1], smallt[:, 1:2])
                q.op("vector", lambda e: e.tensor_scalar(
                    out=ln2t, in0=xat, scalar1=smallt[:, 0:1],
                    scalar2=smallt[:, 1:2], op0=OP.subtract, op1=OP.mult),
                    reads=[b_xat], sreads=[b_small], writes=[b_ln2t])
                for k in range(NCH):
                    q.op("tensor", lambda e, k=k: e.transpose(
                        pt[:, 0:128], in_=ln2t[:, k * 128:(k + 1) * 128],
                        identity=ident),
                        reads=[b_ln2t, b_id], writes=[b_pt])
                    q.op("scalar", lambda e, k=k, tt=tt: e.activation(
                        out=ln2fm[k][:, tt * 128:(tt + 1) * 128],
                        in_=pt[:, 0:128], func=AF.Copy),
                        reads=[b_pt], writes=[b_ln2fm[k]])
            for dc in range(32):
                pmx = dc % 2
                for k in range(NCH):
                    q.op("tensor", lambda e, dc=dc, k=k, pmx=pmx: e.matmul(
                        pm[pmx][:, 0:ST2],
                        lhsT=w1[k][:, dc * 128:(dc + 1) * 128],
                        rhs=ln2fm[k], start=(k == 0), stop=(k == NCH - 1)),
                        reads=[b_w1[k], b_ln2fm[k]], writes=[b_pm[pmx]])
                q.op("scalar", lambda e, dc=dc, pmx=pmx: e.activation(
                    out=g1[dc], in_=pm[pmx][:, 0:ST2], func=AF.Relu,
                    bias=b1c[:, dc:dc + 1]),
                    reads=[b_pm[pmx]], sreads=[b_b1c], writes=[b_g1[dc]])
            for tt in range(2):
                r0 = s2 * ST2 + tt * 128
                for hh in range(2):
                    for kc in range(32):
                        q.op("tensor", lambda e, kc=kc, tt=tt, hh=hh: e.matmul(
                            pop[:, hh * 512:(hh + 1) * 512],
                            lhsT=g1[kc][:, tt * 128:(tt + 1) * 128],
                            rhs=w2[kc][:, hh * 512:(hh + 1) * 512],
                            start=(kc == 0), stop=False),
                            reads=[b_g1[kc], b_w2[kc]], writes=[b_pop])
                    q.op("tensor", lambda e, tt=tt, hh=hh: e.matmul(
                        pop[:, hh * 512:(hh + 1) * 512], lhsT=onesb,
                        rhs=b2row[0:1, hh * 512:(hh + 1) * 512],
                        start=False, stop=True),
                        reads=[b_ones, b_b2r], writes=[b_pop])
                q.dma("sync", lambda e, r0=r0: e.dma_start(
                    out=xat, in_=xa_in[r0:r0 + 128, :]), writes=[b_xat])
                q.dma("sync", lambda e, r0=r0: e.dma_start(
                    out=x0t, in_=x0h[r0:r0 + 128, :]), writes=[b_x0t])
                q.op("vector", lambda e: e.tensor_add(
                    out=x1t, in0=xat, in1=pop),
                    reads=[b_xat, b_pop], writes=[b_x1t])
                q.op("vector", lambda e: e.tensor_add(
                    out=x1t, in0=x1t, in1=x0t),
                    reads=[b_x1t, b_x0t], writes=[b_x1t])
                q.dma("sync", lambda e, r0=r0: e.dma_start(
                    out=x1_out[r0:r0 + 128, :], in_=x1t),
                    reads=[b_x1t])


def build_p2(noskip=True):
    nc = bass.Bass("TRN2", target_bir_lowering=False)
    q = Prog(nc)
    X = p2_drams(nc)
    emit_p2(nc, q, X)
    with ExitStack() as fst:
        q.finalize(fst, noskip=noskip)
    return nc


def stage_p2_inputs(ins, core, xa=None):
    b, ch = core >> 1, core & 1
    x = np.asarray(ins["L0_attn_x"])
    d = {
        "x0h": np.ascontiguousarray(
            x[b].reshape(NSLAB, ST, D)[ch::2].reshape(T // 2, D)),
        "w1d": np.ascontiguousarray(ins["L0_attn_ffn_w1"]),
        "b1d": np.ascontiguousarray(ins["L0_attn_ffn_b1"]),
        "w2d": np.ascontiguousarray(ins["L0_attn_ffn_w2"]),
        "b2d": np.ascontiguousarray(ins["L0_attn_ffn_b2"]),
        "g2d": np.ascontiguousarray(ins["L0_attn_ln2_g"]),
        "bt2d": np.ascontiguousarray(ins["L0_attn_ln2_b"]),
    }
    if xa is not None:
        d["xa_in"] = np.ascontiguousarray(xa)
    return {k: np.asarray(v, dtype=np.float32) for k, v in d.items()}




IC3 = [(0, 512), (512, 512), (1024, 512), (1536, 512), (2048, 512),
       (2560, 256)]


def p3_drams(nc, x1_kind="ExternalInput", out_kind="ExternalOutput",
             x1_in=None):
    dram = lambda n, s, dt=F32, k="ExternalInput", **kw: nc.dram_tensor(
        n, s, dt, kind=k, **kw).ap()
    X = {}
    X["x1_in"] = x1_in if x1_in is not None else \
        dram("x1_in", [T // 2, D], k=x1_kind)
    X["wg_sl"] = dram("wg_sl", [3 * 128, D])
    X["wu_sl"] = dram("wu_sl", [3 * 128, D])
    X["wd_sl"] = dram("wd_sl", [128, 2816])
    X["out"] = dram("out", [T // 2, D], k=out_kind)
    X["agw1_in"] = dram("agw1_in", [6, 128, D], BF16, k="Internal")
    X["agw1_out"] = dram("agw1_out", [8, 6, 128, D], BF16, k="Internal",
                         addr_space="Local")
    X["agw2_in"] = dram("agw2_in", [128, 2816], BF16, k="Internal")
    X["agw2_out"] = dram("agw2_out", [8, 128, 2816], BF16, k="Internal",
                         addr_space="Local")
    return X


def emit_p3(nc, q, X):
    (x1_in, wg_sl, wu_sl, wd_sl, out_d, agw1_in, agw1_out, agw2_in,
     agw2_out) = (X[k] for k in (
        "x1_in", "wg_sl", "wu_sl", "wd_sl", "out", "agw1_in", "agw1_out",
        "agw2_in", "agw2_out"))
    ST3 = 128
    NS3 = (T // 2) // ST3

    st = ExitStack()
    sb = lambda name, shape, dt=F32: st.enter_context(
        nc.sbuf_tensor("C_" + name, shape, dt)).ap()
    mkpsum = lambda name, shape: st.enter_context(
        nc.psum_tensor("C_" + name, shape, F32)).ap()
    B = Buf
    with st:
        wgT = [sb(f"wgT{k}", [128, 2816], BF16) for k in range(NCH)]
        wuT = [sb(f"wuT{k}", [128, 2816], BF16) for k in range(NCH)]
        wdT = [sb(f"wdT{k}", [128, D], BF16) for k in range(22)]
        ident = sb("ident", [128, 128])
        identb = sb("identb", [128, 128], BF16)
        eps_sb = sb("eps_sb", [128, 1])
        stage = sb("stage", [128, 2816])
        stage_b = sb("stage_b", [128, 2816], BF16)
        qa = sb("qa", [128, 1024])
        qxn = sb("qxn", [128, 1024])
        qb2 = sb("qb2", [128, 1024])
        qc = sb("qc", [128, 1024])
        qp3 = sb("qp3", [128, 1024])
        qsg = sb("qsg", [128, 1024], I32)
        qam = sb("qam", [128, 64])
        qr6 = sb("qr6", [128, 64])
        qs = sb("qs", [128, 64])
        x1t = sb("x1t", [128, D])
        ft = sb("ft", [128, D])
        fq = sb("fq", [128, D])
        fqfm = [sb(f"fqfm{k}", [128, ST3], BF16) for k in range(NCH)]
        ht = sb("ht", [128, 2816])
        hq = stage
        sg_t = sb("sg_t", [128, 512])
        hqfm = [sb(f"hqfm{k}", [128, ST3], BF16) for k in range(22)]
        x2t = ft
        stat = sb("stat", [128, 12])
        mvt = sb("mvt", [128, 6])
        smallt = sb("smallt", [128, 8])
        pg = mkpsum("pg", [128, 512])
        pu = mkpsum("pu", [128, 512])
        pt = mkpsum("pt", [128, 512])
        pd2 = mkpsum("pd2", [128, D])

        b_wgT = [B() for _ in range(NCH)]
        b_wuT = [B() for _ in range(NCH)]
        b_wdT = [B() for _ in range(22)]
        b_id = B(); b_idb = B(); b_eps = B(); b_stage = B(); b_stageb = B()
        b_qa = B(); b_qxn = B(); b_qb2 = B(); b_qc = B(); b_qp3 = B()
        b_qsg = B(); b_qam = B(); b_qr6 = B(); b_qs = B()
        b_x1t = B(); b_ft = B(); b_fq = B()
        b_fqfm = [B() for _ in range(NCH)]
        b_ht = B(); b_sg = B()
        b_hq = b_stage
        b_hqfm = [B() for _ in range(22)]
        b_stat = B(); b_mvt = B(); b_small = B()
        b_x2t = b_ft
        b_pg = B(); b_pu = B(); b_pt = B(); b_pd2 = B()
        b_agw1i = B(); b_agw1o = B(); b_agw2i = B(); b_agw2o = B()

        def quant(x_ap, x_buf, out_ap, out_buf, nblk, gq="gpsimd"):
            scr = dict(a=(qa, b_qa), xn=(qxn, b_qxn), b2=(qb2, b_qb2),
                       c=(qc, b_qc), p3=(qp3, b_qp3), sg=(qsg, b_qsg),
                       am=(qam, b_qam), r6=(qr6, b_qr6), s=(qs, b_qs))
            n16 = nblk * 16
            q.op("vector", lambda e: e.tensor_scalar(
                out=qsg[:, 0:n16], in0=x_ap.bitcast(I32),
                scalar1=-0x80000000, scalar2=None, op0=OP.bitwise_and),
                reads=[x_buf], writes=[b_qsg])
            q.op("vector", lambda e: e.tensor_scalar(
                out=qa.bitcast(I32)[:, 0:n16], in0=x_ap.bitcast(I32),
                scalar1=0x7FFFFFFF, scalar2=None, op0=OP.bitwise_and),
                reads=[x_buf], writes=[b_qa])
            q.op("vector", lambda e: e.tensor_reduce(
                out=qam[:, 0:nblk],
                in_=qa[:, 0:n16].rearrange("p (b k) -> p b k", k=16),
                axis=AX.X, op=OP.max), reads=[b_qa], writes=[b_qam])
            q.op("vector", lambda e: e.reciprocal(
                out=qr6[:, 0:nblk], in_=qam[:, 0:nblk]),
                reads=[b_qam], writes=[b_qr6])
            q.op("vector", lambda e: e.tensor_single_scalar(
                out=qr6[:, 0:nblk], in_=qr6[:, 0:nblk], scalar=6.0,
                op=OP.mult), reads=[b_qr6], writes=[b_qr6])
            q.op("vector", lambda e: e.tensor_single_scalar(
                out=qs[:, 0:nblk], in_=qam[:, 0:nblk], scalar=1.0 / 6.0,
                op=OP.mult), reads=[b_qam], writes=[b_qs])
            q.op(gq, lambda e: e.tensor_tensor(
                out=qxn[:, 0:n16].rearrange("p (b k) -> p b k", k=16),
                in0=qa[:, 0:n16].rearrange("p (b k) -> p b k", k=16),
                in1=qr6[:, 0:nblk].unsqueeze(2).broadcast_to([128, nblk, 16]),
                op=OP.mult), reads=[b_qa, b_qr6], writes=[b_qxn])
            q.op("vector", lambda e: e.tensor_scalar(
                out=qb2[:, 0:n16], in0=qxn[:, 0:n16], scalar1=2.0,
                scalar2=MAGIC, op0=OP.mult, op1=OP.add),
                reads=[b_qxn], writes=[b_qb2])
            q.op("vector", lambda e: e.tensor_scalar(
                out=qb2[:, 0:n16], in0=qb2[:, 0:n16], scalar1=MAGIC,
                scalar2=0.5, op0=OP.subtract, op1=OP.mult),
                reads=[b_qb2], writes=[b_qb2])
            q.op("vector", lambda e: e.tensor_single_scalar(
                out=qb2[:, 0:n16], in_=qb2[:, 0:n16], scalar=2.0, op=OP.min),
                reads=[b_qb2], writes=[b_qb2])
            q.op("vector", lambda e: e.tensor_scalar(
                out=qc[:, 0:n16], in0=qxn[:, 0:n16], scalar1=MAGIC,
                scalar2=MAGIC, op0=OP.add, op1=OP.subtract),
                reads=[b_qxn], writes=[b_qc])
            q.op("vector", lambda e: e.tensor_scalar(
                out=qc[:, 0:n16], in0=qc[:, 0:n16], scalar1=4.0, scalar2=2.0,
                op0=OP.min, op1=OP.subtract), reads=[b_qc], writes=[b_qc])
            q.op("vector", lambda e: e.scalar_tensor_tensor(
                out=qc[:, 0:n16], in0=qc[:, 0:n16], scalar=0.0,
                in1=qb2[:, 0:n16], op0=OP.max, op1=OP.add),
                reads=[b_qc, b_qb2], writes=[b_qc])
            q.op("vector", lambda e: e.tensor_scalar(
                out=qp3[:, 0:n16], in0=qxn[:, 0:n16], scalar1=5.0,
                scalar2=2.0, op0=OP.is_gt, op1=OP.mult),
                reads=[b_qxn], writes=[b_qp3])
            q.op(gq, lambda e: e.tensor_tensor(
                out=qc[:, 0:n16], in0=qc[:, 0:n16], in1=qp3[:, 0:n16],
                op=OP.add), reads=[b_qc, b_qp3], writes=[b_qc])
            q.op(gq, lambda e: e.tensor_tensor(
                out=qxn[:, 0:n16].rearrange("p (b k) -> p b k", k=16),
                in0=qc[:, 0:n16].rearrange("p (b k) -> p b k", k=16),
                in1=qs[:, 0:nblk].unsqueeze(2).broadcast_to([128, nblk, 16]),
                op=OP.mult), reads=[b_qc, b_qs], writes=[b_qxn])
            q.op("vector", lambda e: e.tensor_tensor(
                out=out_ap.bitcast(I32), in0=qxn.bitcast(I32)[:, 0:n16],
                in1=qsg[:, 0:n16], op=OP.bitwise_or),
                reads=[b_qxn, b_qsg], writes=[out_buf])

        def lnstats(xap, xbuf, mu_sl, r_sl):
            for i in range(2):
                q.op("vector", lambda e, i=i: e.bn_stats(
                    out=stat[:, 6 * i:6 * i + 6],
                    in_=xap[:, 512 * i:512 * (i + 1)]),
                    reads=[xbuf], writes=[b_stat])
            q.op("vector", lambda e: e.bn_aggr(out=mvt[:, 0:2],
                                               in_=stat[:, 0:6]),
                 reads=[b_stat], writes=[b_mvt])
            q.op("vector", lambda e: e.bn_aggr(out=mvt[:, 2:4],
                                               in_=stat[:, 6:12]),
                 reads=[b_stat], writes=[b_mvt])
            q.op("vector", lambda e: e.tensor_scalar(
                out=mu_sl, in0=mvt[:, 0:1], scalar1=mvt[:, 2:3],
                scalar2=0.5, op0=OP.add, op1=OP.mult),
                reads=[b_mvt], sreads=[b_mvt], writes=[b_small])
            q.op("vector", lambda e: e.tensor_scalar(
                out=smallt[:, 4:5], in0=mvt[:, 0:1], scalar1=mvt[:, 2:3],
                scalar2=None, op0=OP.subtract),
                reads=[b_mvt], sreads=[b_mvt], writes=[b_small])
            q.op("scalar", lambda e: e.activation(
                out=smallt[:, 5:6], in_=smallt[:, 4:5], func=AF.Square,
                scale=0.5), reads=[b_small], writes=[b_small])
            q.op("vector", lambda e: e.tensor_scalar(
                out=smallt[:, 6:7], in0=mvt[:, 1:2], scalar1=mvt[:, 3:4],
                scalar2=0.5, op0=OP.add, op1=OP.mult),
                reads=[b_mvt], sreads=[b_mvt], writes=[b_small])
            q.op("vector", lambda e: e.tensor_scalar(
                out=smallt[:, 6:7], in0=smallt[:, 6:7],
                scalar1=smallt[:, 5:6], scalar2=None, op0=OP.add),
                reads=[b_small], sreads=[b_small], writes=[b_small])
            q.op("scalar", lambda e: e.activation(
                out=smallt[:, 7:8], in_=smallt[:, 6:7], func=AF.Sqrt,
                bias=eps_sb),
                reads=[b_small], sreads=[b_eps], writes=[b_small])
            q.op("vector", lambda e: e.reciprocal(out=r_sl,
                                                  in_=smallt[:, 7:8]),
                 reads=[b_small], writes=[b_small])

        q.op("vector", lambda e: e.memset(eps_sb, EPS), writes=[b_eps])
        q.op("gpsimd", lambda e: e.memset(ident, 0.0), writes=[b_id])
        q.op("gpsimd", lambda e: e.affine_select(
            out=ident, in_=ident, compare_op=OP.not_equal, fill=1.0, base=0,
            pattern=[[-1, 128]], channel_multiplier=1),
            reads=[b_id], writes=[b_id])
        q.op("vector", lambda e: e.tensor_copy(out=identb, in_=ident),
             reads=[b_id], writes=[b_idb])
        for sl in range(3):
            for wsrc, slot in ((wg_sl, sl), (wu_sl, 3 + sl)):
                q.dma("sync", lambda e, wsrc=wsrc, sl=sl: e.dma_start(
                    out=stage[:, 0:D], in_=wsrc[sl * 128:(sl + 1) * 128, :]),
                    writes=[b_stage])
                quant(stage[:, 0:D], b_stage, qp3[:, 0:D], b_qp3, 64)
                q.op("scalar", lambda e: e.activation(
                    out=stage_b[:, 0:D], in_=qp3[:, 0:D], func=AF.Copy),
                    reads=[b_qp3], writes=[b_stageb])
                q.dma("sync", lambda e, slot=slot: e.dma_start(
                    out=agw1_in[slot], in_=stage_b[:, 0:D]),
                    reads=[b_stageb], writes=[b_agw1i])
        q.dma("sync", lambda e: e.dma_start(out=stage, in_=wd_sl),
              writes=[b_stage])
        for off, w in ((0, 1024), (1024, 1024), (2048, 768)):
            quant(stage[:, off:off + w], b_stage, qp3[:, 0:w], b_qp3, w // 16)
            q.op("scalar", lambda e, off=off, w=w: e.activation(
                out=stage_b[:, off:off + w], in_=qp3[:, 0:w], func=AF.Copy),
                reads=[b_qp3], writes=[b_stageb])
        q.dma("sync", lambda e: e.dma_start(out=agw2_in, in_=stage_b),
              reads=[b_stageb], writes=[b_agw2i])
        ag1 = q.op("gpsimd", lambda e: e.collective_compute(
            "AllGather", OP.bypass,
            replica_groups=[list(range(8))],
            ins=[agw1_in], outs=[agw1_out]),
            reads=[b_agw1i], writes=[b_agw1o])
        ag2 = q.op("gpsimd", lambda e: e.collective_compute(
            "AllGather", OP.bypass,
            replica_groups=[list(range(8))],
            ins=[agw2_in], outs=[agw2_out]),
            reads=[b_agw2i], writes=[b_agw2o])
        for j in range(22):
            rk, sl = j % 8, j // 8
            for which, wt, bt, slot in (("g", wgT, b_wgT, sl),
                                        ("u", wuT, b_wuT, 3 + sl)):
                q.dma("sync", lambda e, rk=rk, slot=slot: e.dma_start(
                    out=stage_b[:, 0:D], in_=agw1_out[rk, slot]),
                    deps=[ag1], reads=[b_agw1o], writes=[b_stageb])
                for k in range(NCH):
                    q.op("tensor", lambda e, k=k: e.transpose(
                        pt[:, 0:128].bitcast(BF16)[:, 0:128],
                        in_=stage_b[:, k * 128:(k + 1) * 128],
                        identity=identb),
                        reads=[b_stageb, b_idb], writes=[b_pt])
                    q.op("scalar", lambda e, k=k, j=j, wt=wt, bt=bt:
                         e.activation(
                             out=wt[k][:, j * 128:(j + 1) * 128],
                             in_=pt[:, 0:128].bitcast(BF16)[:, 0:128],
                             func=AF.Copy),
                         reads=[b_pt], writes=[bt[k]])
        for j in range(8):
            q.dma("sync", lambda e, j=j: e.dma_start(
                out=stage_b, in_=agw2_out[j]),
                deps=[ag2], reads=[b_agw2o], writes=[b_stageb])
            for kc in range(22):
                q.op("tensor", lambda e, kc=kc: e.transpose(
                    pt[:, 0:128].bitcast(BF16)[:, 0:128],
                    in_=stage_b[:, kc * 128:(kc + 1) * 128],
                    identity=identb),
                    reads=[b_stageb, b_idb], writes=[b_pt])
                q.op("scalar", lambda e, kc=kc, j=j: e.activation(
                    out=wdT[kc][:, j * 128:(j + 1) * 128],
                    in_=pt[:, 0:128].bitcast(BF16)[:, 0:128],
                    func=AF.Copy),
                    reads=[b_pt], writes=[b_wdT[kc]])

        for s3 in range(NS3):
            r0 = s3 * ST3
            q.dma("sync", lambda e, r0=r0: e.dma_start(
                out=x1t, in_=x1_in[r0:r0 + 128, :]), writes=[b_x1t])
            lnstats(x1t, b_x1t, smallt[:, 0:1], smallt[:, 1:2])
            q.op("vector", lambda e: e.tensor_scalar(
                out=ft, in0=x1t, scalar1=smallt[:, 0:1],
                scalar2=smallt[:, 1:2], op0=OP.subtract, op1=OP.mult),
                reads=[b_x1t], sreads=[b_small], writes=[b_ft])
            quant(ft, b_ft, fq, b_fq, 64)
            for k in range(NCH):
                q.op("tensor", lambda e, k=k: e.transpose(
                    pt[:, 0:128], in_=fq[:, k * 128:(k + 1) * 128],
                    identity=ident),
                    reads=[b_fq, b_id], writes=[b_pt])
                q.op("scalar", lambda e, k=k: e.activation(
                    out=fqfm[k], in_=pt[:, 0:128], func=AF.Copy),
                    reads=[b_pt], writes=[b_fqfm[k]])
            for ic, (off, w) in enumerate(IC3):
                for k in range(NCH):
                    q.op("tensor", lambda e, k=k, off=off, w=w: e.matmul(
                        pg[:, 0:w], lhsT=fqfm[k],
                        rhs=wgT[k][:, off:off + w],
                        start=(k == 0), stop=(k == NCH - 1)),
                        reads=[b_fqfm[k], b_wgT[k]], writes=[b_pg])
                for k in range(NCH):
                    q.op("tensor", lambda e, k=k, off=off, w=w: e.matmul(
                        pu[:, 0:w], lhsT=fqfm[k],
                        rhs=wuT[k][:, off:off + w],
                        start=(k == 0), stop=(k == NCH - 1)),
                        reads=[b_fqfm[k], b_wuT[k]], writes=[b_pu])
                q.op("scalar", lambda e, off=off, w=w: e.activation(
                    out=sg_t[:, 0:w], in_=pg[:, 0:w], func=AF.Silu),
                    reads=[b_pg], writes=[b_sg])
                q.op("vector", lambda e, off=off, w=w: e.tensor_tensor(
                    out=ht[:, off:off + w], in0=sg_t[:, 0:w],
                    in1=pu[:, 0:w], op=OP.mult),
                    reads=[b_sg, b_pu], writes=[b_ht])
            for off, w in ((0, 1024), (1024, 1024), (2048, 768)):
                quant(ht[:, off:off + w], b_ht, hq[:, off:off + w], b_hq,
                      w // 16)
            for kc in range(22):
                q.op("tensor", lambda e, kc=kc: e.transpose(
                    pt[:, 0:128], in_=hq[:, kc * 128:(kc + 1) * 128],
                    identity=ident),
                    reads=[b_hq, b_id], writes=[b_pt])
                q.op("scalar", lambda e, kc=kc: e.activation(
                    out=hqfm[kc], in_=pt[:, 0:128], func=AF.Copy),
                    reads=[b_pt], writes=[b_hqfm[kc]])
            for hh in range(2):
                for kc in range(22):
                    q.op("tensor", lambda e, kc=kc, hh=hh: e.matmul(
                        pd2[:, hh * 512:(hh + 1) * 512], lhsT=hqfm[kc],
                        rhs=wdT[kc][:, hh * 512:(hh + 1) * 512],
                        start=(kc == 0), stop=(kc == 21)),
                        reads=[b_hqfm[kc], b_wdT[kc]], writes=[b_pd2])
            q.op("vector", lambda e: e.tensor_add(
                out=x2t, in0=x1t, in1=pd2),
                reads=[b_x1t, b_pd2], writes=[b_x2t])
            q.dma("sync", lambda e, r0=r0: e.dma_start(
                out=out_d[r0:r0 + 128, :], in_=x2t),
                reads=[b_x2t])


def build_p3(noskip=True):
    nc = bass.Bass("TRN2", target_bir_lowering=False)
    q = Prog(nc)
    X = p3_drams(nc)
    emit_p3(nc, q, X)
    with ExitStack() as fst:
        q.finalize(fst, noskip=noskip)
    return nc


def stage_p3_inputs(ins, core, x1=None):
    wg = np.asarray(ins["L0_ffn_w_gate"])
    wu = np.asarray(ins["L0_ffn_w_up"])
    wd = np.asarray(ins["L0_ffn_w_down"])
    c = core
    gsl = [min(c + 8 * i, 21) for i in range(3)]
    d = {
        "wg_sl": np.ascontiguousarray(np.concatenate(
            [wg[j * 128:(j + 1) * 128] for j in gsl], axis=0)),
        "wu_sl": np.ascontiguousarray(np.concatenate(
            [wu[j * 128:(j + 1) * 128] for j in gsl], axis=0)),
        "wd_sl": np.ascontiguousarray(wd[c * 128:(c + 1) * 128]),
    }
    if x1 is not None:
        d["x1_in"] = np.ascontiguousarray(x1)
    return {k: np.asarray(v, dtype=np.float32) for k, v in d.items()}


def build_fused(noskip=True):
    nc = bass.Bass("TRN2", target_bir_lowering=False)
    q = Prog(nc)
    X1 = p1_drams(nc, xa_kind="Internal")
    emit_p1(nc, q, X1)
    q.barrier()
    X2 = p2_drams(nc, x1_kind="Internal", x0h=X1["x0h"],
                  xa_in=X1["xa_out"])
    emit_p2(nc, q, X2)
    q.barrier()
    X3 = p3_drams(nc, x1_in=X2["x1_out"])
    emit_p3(nc, q, X3)
    with ExitStack() as fst:
        q.finalize(fst, noskip=noskip)
    return nc


def stage_fused_inputs(ins, core):
    d = stage_p1_inputs(ins, core)
    d.update(stage_p2_inputs(ins, core))
    d.update(stage_p3_inputs(ins, core))
    return d


_BUILD_CACHE = {}


def _get(name, fn):
    if name not in _BUILD_CACHE:
        _BUILD_CACHE[name] = fn()
    return _BUILD_CACHE[name]


def kernel(**inputs):
    cores = list(range(8))
    nc = _get("fused", lambda: build_fused(noskip=True))
    maps = [stage_fused_inputs(inputs, c) for c in cores]
    r = run_bass_kernel_spmd(nc, maps, core_ids=cores)
    out = np.empty((4, T, D), np.float32)
    for c in cores:
        b, ch = c >> 1, c & 1
        out[b].reshape(NSLAB, ST, D)[ch::2] = \
            np.asarray(r.results[c]["out"]).reshape(NSLAB // 2, ST, D)
    return out


# version 04
# speedup vs baseline: 1.3125x; 1.3125x over previous
import sys
sys.path.insert(0, "/opt/trn_rl_repo")
from contextlib import ExitStack

import numpy as np
import concourse.bass as bass
import concourse.mybir as mybir
from concourse.bass_utils import run_bass_kernel_spmd

F32 = mybir.dt.float32
F32R = mybir.dt.float32r
BF16 = mybir.dt.bfloat16
I32 = mybir.dt.int32
AF = mybir.ActivationFunctionType
OP = mybir.AluOpType
AX = mybir.AxisListType

MAGIC = 12582912.0
EPS = 1e-5
D = 1024
E = 1024
EH = 512
NCH = 8
MCH = 4
NST = 16
T = 8192
ST = 512
NSLAB = T // ST
DMA_INC = 16


class Buf:
    __slots__ = ("writer", "readers", "name")

    def __init__(self, name=""):
        self.writer = None
        self.readers = []
        self.name = name


class _Op:
    __slots__ = ("engine", "fn", "inc", "deps", "semkey", "need_inc",
                 "val_after", "idx", "nsk")


class Prog:
    ENGINES = ("sync", "tensor", "vector", "scalar", "gpsimd")

    def __init__(self, nc):
        self.nc = nc
        self.ops = []
        self.barriers = []

    def barrier(self):
        self.barriers.append(len(self.ops))

    def _op(self, engine, fn, reads, writes, deps, inc, semkey, sreads=None):
        o = _Op()
        o.engine, o.fn, o.inc, o.semkey = engine, fn, inc, semkey
        o.need_inc = False
        o.val_after = None
        o.idx = len(self.ops)
        dep = set(deps or ())
        nsk = set()
        for b in reads or ():
            if b.writer is not None:
                dep.add(b.writer)
        for b in sreads or ():
            if b.writer is not None:
                dep.add(b.writer)
                nsk.add(b.writer)
        for b in writes or ():
            if b.writer is not None:
                dep.add(b.writer)
            dep.update(b.readers)
        o.deps = dep
        o.nsk = nsk
        self.ops.append(o)
        for b in reads or ():
            b.readers.append(o)
        for b in sreads or ():
            b.readers.append(o)
        for b in writes or ():
            b.writer = o
            b.readers = []
        return o

    def op(self, engine, fn, reads=None, writes=None, deps=None, sreads=None):
        return self._op(engine, fn, reads, writes, deps, 1, engine,
                        sreads=sreads)

    def dma(self, queue, fn, reads=None, writes=None, deps=None,
            private=False):
        key = queue + "dma"
        o = self._op(queue, fn, reads, writes, deps, DMA_INC, key)
        if private:
            o.semkey = "pvt%d" % o.idx
        return o

    def finalize(self, st, serial=False, noskip=False):
        import os
        trace_f = open("/tmp/prog.txt", "w") if os.environ.get("PROGTRACE") \
            else None
        nc = self.nc
        for o in self.ops:
            for d in o.deps:
                d.need_inc = True
        for o in self.ops:
            if o.inc == DMA_INC:
                o.need_inc = True
        cnt = {}
        run_cnt = []
        for o in self.ops:
            if o.need_inc:
                cnt[o.semkey] = cnt.get(o.semkey, 0) + o.inc
                o.val_after = cnt[o.semkey]
            run_cnt.append(dict(cnt))
        sems = {k: st.enter_context(nc.semaphore("s_" + k)) for k in cnt}
        per_engine = {e: [] for e in self.ENGINES}
        for o in self.ops:
            per_engine[o.engine].append(o)
        final_vals = dict(cnt)
        self_barriers = self.barriers
        block = st.enter_context(nc.Block())

        def section(name):
            def _(eng):
                hw = {}
                bar_i = 0
                for o in per_engine[name]:
                    nonlocal_barriers = self_barriers
                    while bar_i < len(nonlocal_barriers) and \
                            o.idx >= nonlocal_barriers[bar_i]:
                        bpos = nonlocal_barriers[bar_i]
                        if bpos > 0:
                            for k, v in run_cnt[bpos - 1].items():
                                if hw.get(k, 0) < v:
                                    eng.wait_ge(sems[k], v)
                                    hw[k] = v
                        bar_i += 1
                    if serial and o.idx > 0:
                        for k, v in run_cnt[o.idx - 1].items():
                            if hw.get(k, 0) < v:
                                eng.wait_ge(sems[k], v)
                                hw[k] = v
                    for d in sorted(o.deps, key=lambda x: x.idx):
                        import os as _os
                        _nsv = _os.environ.get("NOSKIP_ENGINES", "")
                        if (not noskip) and d.engine == o.engine \
                                and d.inc != DMA_INC and d not in o.nsk \
                                and o.engine not in _nsv:
                            continue
                        if d.val_after is None:
                            continue
                        if d.inc == DMA_INC:
                            val = run_cnt[o.idx - 1].get(d.semkey, 0) \
                                if o.idx > 0 else d.val_after
                            val = max(val, d.val_after)
                        else:
                            val = d.val_after
                        if (not noskip) and hw.get(d.semkey, 0) >= val:
                            continue
                        if trace_f is not None:
                            trace_f.write("%s   wait %s >= %d (dep #%d)\n" % (
                                name, d.semkey, val, d.idx))
                        eng.wait_ge(sems[d.semkey], val)
                        hw[d.semkey] = val
                    if trace_f is not None:
                        trace_f.write("%s #%d L%d sem=%s inc=%s\n" % (
                            name, o.idx, o.fn.__code__.co_firstlineno,
                            o.semkey,
                            o.val_after if o.need_inc else "-"))
                    inst = o.fn(eng)
                    if o.need_inc:
                        inst.then_inc(sems[o.semkey], o.inc)
                if name == "sync":
                    for k, v in final_vals.items():
                        eng.wait_ge(sems[k], v)
            return _

        for name in self.ENGINES:
            if per_engine[name] or name == "sync":
                getattr(block, name)(section(name))


def p1_drams(nc, xa_kind="ExternalOutput", debug=False):
    dram = lambda n, s, dt=F32, k="ExternalInput", **kw: nc.dram_tensor(
        n, s, dt, kind=k, **kw).ap()
    X = {}
    xb = X["xb"] = dram("xb", [T, D])
    inpw_x = X["inpw_x"] = dram("inpw_x", [D, E])
    inpw_z = X["inpw_z"] = dram("inpw_z", [D, EH])
    ln1_g = X["ln1_g"] = dram("ln1_g", [D])
    ln1_b = X["ln1_b"] = dram("ln1_b", [D])
    rms_w = X["rms_w"] = dram("rms_w", [D])
    conv_w = X["conv_w"] = dram("conv_w", [E, 4])
    conv_b = X["conv_b"] = dram("conv_b", [E])
    xprojw = X["xprojw"] = dram("xprojw", [E, 96])
    dtw_h = X["dtw_h"] = dram("dtw_h", [64, EH])
    dtb_h = X["dtb_h"] = dram("dtb_h", [EH])
    alog_h = X["alog_h"] = dram("alog_h", [EH, NST])
    dp_h = X["dp_h"] = dram("dp_h", [EH])
    opw = X["opw"] = dram("opw", [E, D])
    x0h = X["x0h"] = dram("x0h", [T // 2, D])
    blendm = X["blendm"] = dram("blendm", [1])
    xa_out = X["xa_out"] = dram("xa_out", [T // 2, D], k=xa_kind)
    bc_dram = X["bc_dram"] = dram("bc_dram", [32, T], k="Internal")
    NG = NSLAB // 2
    X["dbg"] = dram("dbg", [16, 128, 1024], k="ExternalOutput") if debug \
        else None
    X["agy_in"] = dram("agy_in", [NG, MCH, 128, 1024], BF16, k="Internal")
    X["agy_out"] = dram("agy_out", [NG, 2, MCH, 128, 1024], BF16,
                        k="Internal", addr_space="Local")
    return X


def emit_p1(nc, q, X, debug=False):
    (xb, inpw_x, inpw_z, ln1_g, ln1_b, rms_w, conv_w, conv_b, xprojw, dtw_h,
     dtb_h, alog_h, dp_h, opw, x0h, blendm, xa_out, bc_dram, dbg, agy_in,
     agy_out) = (X[k] for k in (
        "xb", "inpw_x", "inpw_z", "ln1_g", "ln1_b", "rms_w", "conv_w",
        "conv_b", "xprojw", "dtw_h", "dtb_h", "alog_h", "dp_h", "opw",
        "x0h", "blendm", "xa_out", "bc_dram", "dbg", "agy_in", "agy_out"))
    NG = NSLAB // 2

    st = ExitStack()
    sb = lambda name, shape, dt=F32: st.enter_context(
        nc.sbuf_tensor("A_" + name, shape, dt)).ap()
    mkpsum = lambda name, shape: st.enter_context(
        nc.psum_tensor("A_" + name, shape, F32)).ap()
    B = Buf

    with st:
        w_inp = [sb(f"w_inp{k}", [128, E + EH], BF16) for k in range(NCH)]
        w_xp = [sb(f"w_xp{k}", [128, 96]) for k in range(NCH)]
        w_dt = [sb(f"w_dt{c}", [64, 128], F32R) for c in range(MCH)]
        w_op = [sb(f"w_op{k}", [128, D], BF16) for k in range(NCH)]
        a_sb = sb("a_sb", [128, MCH * NST])
        cw_sb = [sb(f"cw{k}", [128, 4]) for k in range(NCH)]
        cb_sb = sb("cb_sb", [128, NCH])
        dtb_sb = sb("dtb_sb", [128, MCH])
        dp_sb = sb("dp_sb", [128, MCH])
        rmsw_sb = sb("rmsw_sb", [128, NCH])
        g1rep = sb("g1rep", [128, D], BF16)
        b1rep = sb("b1rep", [128, D], BF16)
        ident = sb("ident", [128, 128])
        identr = sb("identr", [128, 128], F32R)
        blm = sb("blm", [128, 2])
        eps_sb = sb("eps_sb", [128, 1])
        row_t = sb("row_t", [1, 1024])
        x0t = [sb(f"x0t{i}", [128, D]) for i in range(2)]
        aint = sb("aint", [128, D])
        ln1t = sb("ln1t", [128, D])
        rmst = sb("rmst", [128, D])
        stat = sb("stat", [128, 12])
        mvt = sb("mvt", [128, 6])
        smallt = sb("smallt", [128, 8])
        rms_fm = [sb(f"rms_fm{k}", [128, ST], BF16) for k in range(NCH)]
        xi = [sb(f"xi{k}", [128, ST + 3], BF16) for k in range(NCH)]
        cv = sb("cv", [128, ST])
        dt_r = sb("dt_r", [64, ST], F32R)
        bc_sb = sb("bc_sb", [32, ST])
        sp_t = sb("sp_t", [128, ST])
        sp_t2 = sb("sp_t2", [128, ST])
        xcm = [[sb(f"xcm{c}_{i}", [128, ST]) for c in range(MCH)]
               for i in range(2)]
        xcp = [sb(f"xcp{c}", [128, ST]) for c in range(MCH)]
        delta = [[sb(f"delta{c}_{i}", [128, ST]) for c in range(MCH)]
                 for i in range(2)]
        sz = [[sb(f"sz{c}_{i}", [128, ST], BF16) for c in range(MCH)]
              for i in range(2)]
        u_scr = sb("u_scr", [128, ST])
        da = [sb(f"da{i}", [128, ST]) for i in range(2)]
        bxt = [sb(f"bxt{i}", [128, ST]) for i in range(2)]
        hsc = [sb(f"hsc{i}", [128, ST]) for i in range(2)]
        prod = [sb(f"prod{i}", [128, ST], F32R) for i in range(2)]
        bbt = [sb(f"bbt{i}", [128, ST]) for i in range(2)]
        cbt = [sb(f"cbt{i}", [128, ST]) for i in range(2)]
        state = sb("state", [128, MCH * NST])
        t1 = sb("t1", [128, ST])
        y3t = sb("y3t", [128, ST], BF16)
        ye = sb("ye", [128, NCH * 128], BF16)
        yo = sb("yo", [128, NCH * 128], BF16)
        yb = sb("yb", [128, NCH * 128], BF16)
        x0p = sb("x0p", [128, D])
        stage = x0p
        pa = sb("pa", [128, D])
        pb = sb("pb", [128, D])
        pc = sb("pc", [128, D])
        xar = x0p
        stat2 = sb("stat2", [128, 12])
        mvt2 = sb("mvt2", [128, 6])
        small2 = sb("small2", [128, 8])
        pm = [mkpsum(f"pm{i}", [128, 512]) for i in range(2)]
        pt = mkpsum("pt", [128, 512])
        pd = mkpsum("pd", [128, 512])
        pys = mkpsum("pys", [128, 512])
        pop = mkpsum("pop", [128, D])

        b_winp = [B() for _ in range(NCH)]
        b_wxp = [B() for _ in range(NCH)]
        b_wdt = [B() for _ in range(MCH)]
        b_wop = [B() for _ in range(NCH)]
        b_a = B(); b_cw = [B() for _ in range(NCH)]
        b_cb = B(); b_dtb = B(); b_dp = B(); b_rmsw = B()
        b_g1 = B(); b_b1 = B(); b_id = B(); b_idr = B()
        b_blm = B(); b_row = B(); b_eps = B()
        b_x0t = [B(), B()]
        b_aint = B(); b_ln1t = B(); b_rmst = B()
        b_stat = B(); b_mvt = B(); b_small = B()
        b_rmsfm = [B() for _ in range(NCH)]
        b_xi = [B() for _ in range(NCH)]
        b_cv = B(); b_dtr = B(); b_bcsb = B()
        b_spt = B(); b_spt2 = B()
        b_xcm = [[B() for _ in range(MCH)] for _ in range(2)]
        b_xcp = [B() for _ in range(MCH)]
        b_delta = [[B() for _ in range(MCH)] for _ in range(2)]
        b_sz = [[B() for _ in range(MCH)] for _ in range(2)]
        b_uscr = B()
        b_da = [B(), B()]; b_bx = [B(), B()]; b_h = [B(), B()]
        b_prod = [B(), B()]; b_bbt = [B(), B()]; b_cbt = [B(), B()]
        b_state = B(); b_t1 = B(); b_y3t = B()
        b_ye = B(); b_yo = B(); b_yb = B()
        b_x0p = B(); b_pa = B(); b_pb = B(); b_pc = B()
        b_stat2 = B(); b_mvt2 = B(); b_small2 = B()
        b_pm = [B(), B()]
        b_pt = B(); b_pd = B(); b_pys = B(); b_pop = B()
        b_bcdram = B(); b_agyin = B(); b_agyout = B()

        def colload(vec, out_ap, out_buf, n):
            q.dma("sync", lambda e: e.dma_start(out=row_t[:, 0:n * 128],
                                                in_=vec[None, :]),
                  writes=[b_row], private=True)
            for c in range(n):
                q.op("tensor", lambda e, c=c: e.transpose(
                    pt[:, c:c + 1], in_=row_t[:, c * 128:(c + 1) * 128],
                    identity=ident[0:1, 0:1]),
                    reads=[b_row, b_id], writes=[b_pt])
            q.op("scalar", lambda e: e.activation(
                out=out_ap, in_=pt[:, 0:n], func=AF.Copy),
                reads=[b_pt], writes=[out_buf])

        def lnstats(xap, xbuf, sap, sbuf, mvap, mvbuf, mu_sl, r_sl, sc,
                    smbuf):
            s_d, s_d2, s_v, s_sd = sc
            for i in range(2):
                q.op("vector", lambda e, i=i: e.bn_stats(
                    out=sap[:, 6 * i:6 * i + 6],
                    in_=xap[:, 512 * i:512 * (i + 1)]),
                    reads=[xbuf], writes=[sbuf])
            q.op("vector", lambda e: e.bn_aggr(out=mvap[:, 0:2],
                                               in_=sap[:, 0:6]),
                 reads=[sbuf], writes=[mvbuf])
            q.op("vector", lambda e: e.bn_aggr(out=mvap[:, 2:4],
                                               in_=sap[:, 6:12]),
                 reads=[sbuf], writes=[mvbuf])
            q.op("vector", lambda e: e.tensor_scalar(
                out=mu_sl, in0=mvap[:, 0:1], scalar1=mvap[:, 2:3],
                scalar2=0.5, op0=OP.add, op1=OP.mult),
                reads=[mvbuf], sreads=[mvbuf], writes=[smbuf])
            q.op("vector", lambda e: e.tensor_scalar(
                out=s_d, in0=mvap[:, 0:1], scalar1=mvap[:, 2:3],
                scalar2=None, op0=OP.subtract),
                reads=[mvbuf], sreads=[mvbuf], writes=[smbuf])
            q.op("scalar", lambda e: e.activation(
                out=s_d2, in_=s_d, func=AF.Square, scale=0.5),
                reads=[smbuf], writes=[smbuf])
            q.op("vector", lambda e: e.tensor_scalar(
                out=s_v, in0=mvap[:, 1:2], scalar1=mvap[:, 3:4],
                scalar2=0.5, op0=OP.add, op1=OP.mult),
                reads=[mvbuf], sreads=[mvbuf], writes=[smbuf])
            q.op("vector", lambda e: e.tensor_scalar(
                out=s_v, in0=s_v, scalar1=s_d2, scalar2=None, op0=OP.add),
                reads=[smbuf], sreads=[smbuf], writes=[smbuf])
            q.op("scalar", lambda e: e.activation(
                out=s_sd, in_=s_v, func=AF.Sqrt, bias=eps_sb),
                reads=[smbuf], sreads=[b_eps], writes=[smbuf])
            q.op("vector", lambda e: e.reciprocal(out=r_sl, in_=s_sd),
                 reads=[smbuf], writes=[smbuf])

        q.op("vector", lambda e: e.memset(eps_sb, EPS), writes=[b_eps])
        q.op("gpsimd", lambda e: e.memset(ident, 0.0), writes=[b_id])
        q.op("gpsimd", lambda e: e.affine_select(
            out=ident, in_=ident, compare_op=OP.not_equal, fill=1.0, base=0,
            pattern=[[-1, 128]], channel_multiplier=1),
            reads=[b_id], writes=[b_id])
        q.op("vector", lambda e: e.tensor_copy(out=identr, in_=ident),
             reads=[b_id], writes=[b_idr])
        q.dma("gpsimd", lambda e: e.dma_start(
            out=g1rep, in_=ln1_g.unsqueeze(0).broadcast_to([128, D])),
            writes=[b_g1])
        q.dma("gpsimd", lambda e: e.dma_start(
            out=b1rep, in_=ln1_b.unsqueeze(0).broadcast_to([128, D])),
            writes=[b_b1])
        q.dma("sync", lambda e: e.dma_start(
            out=blm[:, 0:1], in_=blendm.unsqueeze(0).broadcast_to([128, 1])),
            writes=[b_blm])
        q.op("vector", lambda e: e.tensor_scalar(
            out=blm[:, 1:2], in0=blm[:, 0:1], scalar1=-1.0, scalar2=1.0,
            op0=OP.mult, op1=OP.add), reads=[b_blm], writes=[b_blm])
        colload(rms_w, rmsw_sb, b_rmsw, NCH)
        colload(conv_b, cb_sb, b_cb, NCH)
        colload(dtb_h, dtb_sb, b_dtb, MCH)
        colload(dp_h, dp_sb, b_dp, MCH)
        for c in range(MCH):
            q.dma("sync", lambda e, c=c: e.dma_start(
                out=stage[:, 0:NST], in_=alog_h[c * 128:(c + 1) * 128, :]),
                writes=[b_x0p])
            q.op("scalar", lambda e, c=c: e.activation(
                out=sp_t[:, 0:NST], in_=stage[:, 0:NST],
                func=AF.Exp), reads=[b_x0p], writes=[b_spt])
            q.op("vector", lambda e, c=c: e.tensor_single_scalar(
                out=a_sb[:, c * NST:(c + 1) * NST], in_=sp_t[:, 0:NST],
                scalar=-1.0, op=OP.mult),
                reads=[b_spt], writes=[b_a])
        for k in range(NCH):
            q.dma("sync", lambda e, k=k: e.dma_start(
                out=cw_sb[k], in_=conv_w[k * 128:(k + 1) * 128, :]),
                writes=[b_cw[k]])
            q.dma("sync", lambda e, k=k: e.dma_start(
                out=stage, in_=inpw_x[k * 128:(k + 1) * 128, :]),
                writes=[b_x0p])
            q.op("vector", lambda e, k=k: e.tensor_scalar(
                out=w_inp[k][:, 0:E], in0=stage,
                scalar1=rmsw_sb[:, k:k + 1], scalar2=None, op0=OP.mult),
                reads=[b_x0p], sreads=[b_rmsw], writes=[b_winp[k]])
            q.dma("sync", lambda e, k=k: e.dma_start(
                out=stage[:, 0:EH], in_=inpw_z[k * 128:(k + 1) * 128, :]),
                writes=[b_x0p])
            q.op("vector", lambda e, k=k: e.tensor_scalar(
                out=w_inp[k][:, E:E + EH], in0=stage[:, 0:EH],
                scalar1=rmsw_sb[:, k:k + 1], scalar2=None, op0=OP.mult),
                reads=[b_x0p], sreads=[b_rmsw], writes=[b_winp[k]])
            q.dma("sync", lambda e, k=k: e.dma_start(
                out=w_xp[k], in_=xprojw[k * 128:(k + 1) * 128, :]),
                writes=[b_wxp[k]])
            q.dma("sync", lambda e, k=k: e.dma_start(
                out=stage, in_=opw[k * 128:(k + 1) * 128, :]),
                writes=[b_x0p])
            q.op("scalar", lambda e, k=k: e.activation(
                out=w_op[k], in_=stage, func=AF.Copy),
                reads=[b_x0p], writes=[b_wop[k]])
        for c in range(MCH):
            q.dma("sync", lambda e, c=c: e.dma_start(
                out=stage[0:64, 0:128], in_=dtw_h[:, c * 128:(c + 1) * 128]),
                writes=[b_x0p])
            q.op("scalar", lambda e, c=c: e.activation(
                out=w_dt[c], in_=stage[0:64, 0:128], func=AF.Copy),
                reads=[b_x0p], writes=[b_wdt[c]])
        for k in range(NCH):
            q.op("vector", lambda e, k=k: e.memset(xi[k][:, 0:3], 0.0),
                 writes=[b_xi[k]])
        q.op("vector", lambda e: e.memset(state, 0.0), writes=[b_state])

        def emit_prep(s):
            xp = s % 2
            for tt in range(4):
                tok0 = s * ST + tt * 128
                q.dma("sync", lambda e, tok0=tok0, xp=xp: e.dma_start(
                    out=x0t[xp], in_=xb[tok0:tok0 + 128, :]),
                    writes=[b_x0t[xp]])
                lnstats(x0t[xp], b_x0t[xp], stat, b_stat, mvt, b_mvt,
                        smallt[:, 0:1], smallt[:, 1:2],
                        [smallt[:, 4:5], smallt[:, 5:6], smallt[:, 6:7],
                         smallt[:, 7:8]], b_small)
                q.op("vector", lambda e, xp=xp: e.tensor_scalar(
                    out=aint, in0=x0t[xp], scalar1=smallt[:, 0:1],
                    scalar2=smallt[:, 1:2], op0=OP.subtract, op1=OP.mult),
                    reads=[b_x0t[xp]], sreads=[b_small], writes=[b_aint])
                lnstats(aint, b_aint, stat, b_stat, mvt, b_mvt,
                        smallt[:, 2:3], smallt[:, 3:4],
                        [smallt[:, 4:5], smallt[:, 5:6], smallt[:, 6:7],
                         smallt[:, 7:8]], b_small)
                q.op("vector", lambda e: e.tensor_scalar(
                    out=ln1t, in0=aint, scalar1=smallt[:, 2:3],
                    scalar2=smallt[:, 3:4], op0=OP.subtract, op1=OP.mult),
                    reads=[b_aint], sreads=[b_small], writes=[b_ln1t])
                q.op("vector", lambda e: e.tensor_mul(
                    out=rmst, in0=ln1t, in1=g1rep),
                    reads=[b_ln1t, b_g1], writes=[b_rmst])
                q.op("vector", lambda e: e.tensor_add(
                    out=ln1t, in0=rmst, in1=b1rep),
                    reads=[b_rmst, b_b1], writes=[b_ln1t])
                for i in range(2):
                    q.op("vector", lambda e, i=i: e.bn_stats(
                        out=stat[:, 6 * i:6 * i + 6],
                        in_=ln1t[:, 512 * i:512 * (i + 1)]),
                        reads=[b_ln1t], writes=[b_stat])
                q.op("vector", lambda e: e.bn_aggr(out=mvt[:, 0:2],
                                                   in_=stat[:, 0:6]),
                     reads=[b_stat], writes=[b_mvt])
                q.op("vector", lambda e: e.bn_aggr(out=mvt[:, 2:4],
                                                   in_=stat[:, 6:12]),
                     reads=[b_stat], writes=[b_mvt])
                q.op("vector", lambda e: e.tensor_scalar(
                    out=mvt[:, 4:5], in0=mvt[:, 0:1], scalar1=mvt[:, 2:3],
                    scalar2=0.5, op0=OP.add, op1=OP.mult),
                    reads=[b_mvt], sreads=[b_mvt], writes=[b_mvt])
                q.op("vector", lambda e: e.tensor_scalar(
                    out=smallt[:, 4:5], in0=mvt[:, 0:1], scalar1=mvt[:, 2:3],
                    scalar2=None, op0=OP.subtract),
                    reads=[b_mvt], sreads=[b_mvt], writes=[b_small])
                q.op("scalar", lambda e: e.activation(
                    out=smallt[:, 5:6], in_=smallt[:, 4:5], func=AF.Square,
                    scale=0.5), reads=[b_small], writes=[b_small])
                q.op("vector", lambda e: e.tensor_scalar(
                    out=smallt[:, 6:7], in0=mvt[:, 1:2], scalar1=mvt[:, 3:4],
                    scalar2=0.5, op0=OP.add, op1=OP.mult),
                    reads=[b_mvt], sreads=[b_mvt], writes=[b_small])
                q.op("vector", lambda e: e.tensor_scalar(
                    out=smallt[:, 6:7], in0=smallt[:, 6:7],
                    scalar1=smallt[:, 5:6], scalar2=None, op0=OP.add),
                    reads=[b_small], sreads=[b_small], writes=[b_small])
                q.op("scalar", lambda e: e.activation(
                    out=smallt[:, 4:5], in_=mvt[:, 4:5], func=AF.Square),
                    reads=[b_mvt], writes=[b_small])
                q.op("vector", lambda e: e.tensor_scalar(
                    out=smallt[:, 6:7], in0=smallt[:, 6:7],
                    scalar1=smallt[:, 4:5], scalar2=None, op0=OP.add),
                    reads=[b_small], sreads=[b_small], writes=[b_small])
                q.op("scalar", lambda e: e.activation(
                    out=smallt[:, 7:8], in_=smallt[:, 6:7], func=AF.Sqrt,
                    bias=eps_sb),
                    reads=[b_small], sreads=[b_eps], writes=[b_small])
                q.op("vector", lambda e: e.reciprocal(
                    out=mvt[:, 5:6], in_=smallt[:, 7:8]),
                    reads=[b_small], writes=[b_mvt])
                q.op("scalar", lambda e: e.activation(
                    out=rmst, in_=ln1t, func=AF.Copy, scale=mvt[:, 5:6]),
                    reads=[b_ln1t], sreads=[b_mvt], writes=[b_rmst])
                for k in range(NCH):
                    q.op("tensor", lambda e, k=k: e.transpose(
                        pt[:, 0:128], in_=rmst[:, k * 128:(k + 1) * 128],
                        identity=ident),
                        reads=[b_rmst, b_id], writes=[b_pt])
                    q.op("scalar", lambda e, k=k, tt=tt: e.activation(
                        out=rms_fm[k][:, tt * 128:(tt + 1) * 128],
                        in_=pt[:, 0:128], func=AF.Copy),
                        reads=[b_pt], writes=[b_rmsfm[k]])
            for dc in range(12):
                pmx = dc % 2
                for k in range(NCH):
                    q.op("tensor", lambda e, dc=dc, k=k, pmx=pmx: e.matmul(
                        pm[pmx], lhsT=w_inp[k][:, dc * 128:(dc + 1) * 128],
                        rhs=rms_fm[k], start=(k == 0), stop=(k == NCH - 1)),
                        reads=[b_winp[k], b_rmsfm[k]], writes=[b_pm[pmx]])
                if dc < 8:
                    if s > 0:
                        q.op("vector", lambda e, dc=dc: e.tensor_copy(
                            out=xi[dc][:, 0:3], in_=xi[dc][:, ST:ST + 3]),
                            reads=[b_xi[dc]], writes=[b_xi[dc]])
                    q.op("scalar", lambda e, dc=dc, pmx=pmx: e.activation(
                        out=xi[dc][:, 3:ST + 3], in_=pm[pmx], func=AF.Copy),
                        reads=[b_pm[pmx]], writes=[b_xi[dc]])
                else:
                    c = dc - 8
                    q.op("scalar", lambda e, c=c, pmx=pmx, xp=xp: e.activation(
                        out=sz[xp][c], in_=pm[pmx], func=AF.Silu),
                        reads=[b_pm[pmx]], writes=[b_sz[xp][c]])
            for k in range(NCH):
                q.op("vector", lambda e, k=k: e.tensor_scalar(
                    out=cv, in0=xi[k][:, 0:ST], scalar1=cw_sb[k][:, 0:1],
                    scalar2=None, op0=OP.mult),
                    reads=[b_xi[k]], sreads=[b_cw[k]], writes=[b_cv])
                for j in (1, 2, 3):
                    q.op("vector", lambda e, k=k, j=j: e.scalar_tensor_tensor(
                        out=cv, in0=xi[k][:, j:ST + j],
                        scalar=cw_sb[k][:, j:j + 1], in1=cv,
                        op0=OP.mult, op1=OP.add),
                        reads=[b_xi[k], b_cv], sreads=[b_cw[k]], writes=[b_cv])
                if k < MCH:
                    dst, dstb = xcm[xp][k], b_xcm[xp][k]
                else:
                    dst, dstb = xcp[k - MCH], b_xcp[k - MCH]
                q.op("scalar", lambda e, k=k, dst=dst: e.activation(
                    out=dst, in_=cv, func=AF.Silu, bias=cb_sb[:, k:k + 1]),
                    reads=[b_cv], sreads=[b_cb], writes=[dstb])
            for k in range(NCH):
                src = (xcm[xp][k], b_xcm[xp][k]) if k < MCH else \
                      (xcp[k - MCH], b_xcp[k - MCH])
                q.op("tensor", lambda e, k=k, src=src: e.matmul(
                    pd[0:96, :], lhsT=w_xp[k], rhs=src[0],
                    start=(k == 0), stop=(k == NCH - 1)),
                    reads=[b_wxp[k], src[1]], writes=[b_pd])
            q.op("scalar", lambda e: e.activation(
                out=dt_r, in_=pd[0:64, :], func=AF.Copy),
                reads=[b_pd], writes=[b_dtr])
            q.op("scalar", lambda e: e.activation(
                out=bc_sb, in_=pd[64:96, :], func=AF.Copy),
                reads=[b_pd], writes=[b_bcsb])
            q.dma("sync", lambda e, s=s: e.dma_start(
                out=bc_dram[:, s * ST:(s + 1) * ST], in_=bc_sb),
                reads=[b_bcsb], writes=[b_bcdram])
            for c in range(MCH):
                pmx = c % 2
                q.op("tensor", lambda e, c=c, pmx=pmx: e.matmul(
                    pm[pmx], lhsT=w_dt[c], rhs=dt_r, start=True, stop=True),
                    reads=[b_wdt[c], b_dtr], writes=[b_pm[pmx]])
                q.op("scalar", lambda e, c=c, pmx=pmx: e.activation(
                    out=sp_t, in_=pm[pmx], func=AF.Relu,
                    bias=dtb_sb[:, c:c + 1]),
                    reads=[b_pm[pmx]], sreads=[b_dtb], writes=[b_spt])
                q.op("scalar", lambda e, c=c, pmx=pmx: e.activation(
                    out=sp_t2, in_=pm[pmx], func=AF.Abs,
                    bias=dtb_sb[:, c:c + 1]),
                    reads=[b_pm[pmx]], sreads=[b_dtb], writes=[b_spt2])
                q.op("scalar", lambda e: e.activation(
                    out=sp_t2, in_=sp_t2, func=AF.Exp, scale=-1.0),
                    sreads=[b_spt2], writes=[b_spt2])
                q.op("scalar", lambda e: e.activation(
                    out=sp_t2, in_=sp_t2, func=AF.Ln, bias=1.0),
                    sreads=[b_spt2], writes=[b_spt2])
                q.op("vector", lambda e, c=c, xp=xp: e.tensor_add(
                    out=delta[xp][c], in0=sp_t, in1=sp_t2),
                    reads=[b_spt, b_spt2], writes=[b_delta[xp][c]])

        def emit_scan(s):
            xp = s % 2
            for c in range(MCH):
                q.op("vector", lambda e, c=c, xp=xp: e.tensor_mul(
                    out=u_scr, in0=delta[xp][c], in1=xcm[xp][c]),
                    reads=[b_delta[xp][c], b_xcm[xp][c]], writes=[b_uscr])
                for n in range(NST):
                    idx = c * NST + n
                    p2 = n % 2
                    q.op("scalar", lambda e, c=c, n=n, idx=idx, p2=p2, xp=xp:
                         e.activation(out=da[p2], in_=delta[xp][c],
                                      func=AF.Exp,
                                      scale=a_sb[:, idx:idx + 1]),
                         reads=[b_delta[xp][c]], sreads=[b_a],
                         writes=[b_da[p2]])
                    q.dma("scalar", lambda e, n=n, p2=p2, s=s: e.dma_start(
                        out=bbt[p2],
                        in_=bc_dram[n, s * ST:(s + 1) * ST]
                        .unsqueeze(0).broadcast_to([128, ST])),
                        reads=[b_bcdram], writes=[b_bbt[p2]])
                    q.op("gpsimd", lambda e, p2=p2: e.tensor_mul(
                        out=bxt[p2], in0=u_scr, in1=bbt[p2]),
                        reads=[b_uscr, b_bbt[p2]], writes=[b_bx[p2]])
                    q.op("vector", lambda e, p2=p2, idx=idx: e.tensor_tensor_scan(
                        out=hsc[p2], data0=da[p2], data1=bxt[p2],
                        initial=state[:, idx:idx + 1],
                        op0=OP.mult, op1=OP.add),
                        reads=[b_da[p2], b_bx[p2]], sreads=[b_state],
                        writes=[b_h[p2]])
                    q.op("vector", lambda e, p2=p2, idx=idx: e.tensor_scalar(
                        out=state[:, idx:idx + 1], in0=hsc[p2][:, ST - 1:ST],
                        scalar1=1.0, scalar2=None, op0=OP.mult),
                        sreads=[b_h[p2]], writes=[b_state])
                    q.dma("scalar", lambda e, n=n, p2=p2, s=s: e.dma_start(
                        out=cbt[p2],
                        in_=bc_dram[16 + n, s * ST:(s + 1) * ST]
                        .unsqueeze(0).broadcast_to([128, ST])),
                        reads=[b_bcdram], writes=[b_cbt[p2]])
                    peng = "vector" if n % 2 == 0 else "gpsimd"
                    q.op(peng, lambda e, p2=p2: e.tensor_mul(
                        out=prod[p2], in0=hsc[p2], in1=cbt[p2]),
                        reads=[b_cbt[p2]], sreads=[b_h[p2]],
                        writes=[b_prod[p2]])
                    q.op("tensor", lambda e, p2=p2, n=n: e.matmul(
                        pys, lhsT=identr, rhs=prod[p2],
                        start=(n == 0), stop=(n == NST - 1)),
                        reads=[b_idr, b_prod[p2]], writes=[b_pys])
                q.op("vector", lambda e, c=c, xp=xp: e.scalar_tensor_tensor(
                    out=t1, in0=xcm[xp][c], scalar=dp_sb[:, c:c + 1], in1=pys,
                    op0=OP.mult, op1=OP.add),
                    reads=[b_xcm[xp][c], b_pys], sreads=[b_dp],
                    writes=[b_t1])
                q.op("vector", lambda e, c=c, xp=xp: e.tensor_mul(
                    out=y3t, in0=t1, in1=sz[xp][c]),
                    reads=[b_t1, b_sz[xp][c]], writes=[b_y3t])
                q.dma("sync", lambda e, c=c, s=s: e.dma_start(
                    out=agy_in[s // 2, c, :, (s % 2) * ST:(s % 2) * ST + ST],
                    in_=y3t),
                    reads=[b_y3t], writes=[b_agyin])

        coll_ops = []

        def emit_ag(g):
            o = q.op("gpsimd", lambda e, g=g: e.collective_compute(
                "AllGather", OP.bypass,
                replica_groups=[[0, 1], [2, 3], [4, 5], [6, 7]],
                ins=[agy_in[g]],
                outs=[agy_out[g]]),
                reads=[b_agyin], writes=[b_agyout])
            coll_ops.append(o)

        def emit_post(ps):
            agdep = coll_ops[ps]
            for tt in range(4):
                q.dma("sync", lambda e, ps=ps, tt=tt: e.dma_start(
                    out=ye.rearrange("p (k f) -> p k f", f=128),
                    in_=agy_out[ps, :, :, :, tt * 128:tt * 128 + 128]
                    .rearrange("s c p f -> p (s c) f")),
                    deps=[agdep], writes=[b_ye])
                q.dma("sync", lambda e, ps=ps, tt=tt: e.dma_start(
                    out=yo.rearrange("p (k f) -> p k f", f=128),
                    in_=agy_out[ps, :, :, :,
                                512 + tt * 128:640 + tt * 128]
                    .rearrange("s c p f -> p (s c) f")),
                    deps=[agdep], writes=[b_yo])
                q.op("vector", lambda e: e.tensor_scalar(
                    out=yb, in0=yo, scalar1=blm[:, 0:1], scalar2=None,
                    op0=OP.mult), reads=[b_yo], sreads=[b_blm],
                    writes=[b_yb])
                q.op("vector", lambda e: e.scalar_tensor_tensor(
                    out=yb, in0=ye, scalar=blm[:, 1:2], in1=yb,
                    op0=OP.mult, op1=OP.add),
                    reads=[b_ye, b_yb], sreads=[b_blm], writes=[b_yb])
                for hh in range(2):
                    for k in range(NCH):
                        q.op("tensor", lambda e, k=k, hh=hh: e.matmul(
                            pop[:, hh * 512:(hh + 1) * 512],
                            lhsT=yb[:, k * 128:(k + 1) * 128],
                            rhs=w_op[k][:, hh * 512:(hh + 1) * 512],
                            start=(k == 0), stop=(k == NCH - 1)),
                            reads=[b_yb, b_wop[k]], writes=[b_pop])
                r0 = ps * ST + tt * 128
                q.dma("sync", lambda e, r0=r0: e.dma_start(
                    out=x0p, in_=x0h[r0:r0 + 128, :]), writes=[b_x0p])
                lnstats(x0p, b_x0p, stat2, b_stat2, mvt2, b_mvt2,
                        small2[:, 0:1], small2[:, 1:2],
                        [small2[:, 4:5], small2[:, 5:6], small2[:, 6:7],
                         small2[:, 7:8]], b_small2)
                q.op("vector", lambda e: e.tensor_scalar(
                    out=pa, in0=x0p, scalar1=small2[:, 0:1],
                    scalar2=small2[:, 1:2], op0=OP.subtract, op1=OP.mult),
                    reads=[b_x0p], sreads=[b_small2], writes=[b_pa])
                lnstats(pa, b_pa, stat2, b_stat2, mvt2, b_mvt2,
                        small2[:, 2:3], small2[:, 3:4],
                        [small2[:, 4:5], small2[:, 5:6], small2[:, 6:7],
                         small2[:, 7:8]], b_small2)
                q.op("vector", lambda e: e.tensor_scalar(
                    out=pb, in0=pa, scalar1=small2[:, 2:3],
                    scalar2=small2[:, 3:4], op0=OP.subtract, op1=OP.mult),
                    reads=[b_pa], sreads=[b_small2], writes=[b_pb])
                q.op("gpsimd", lambda e: e.tensor_mul(
                    out=pc, in0=pb, in1=g1rep),
                    reads=[b_pb, b_g1], writes=[b_pc])
                q.op("gpsimd", lambda e: e.tensor_add(
                    out=pb, in0=pc, in1=b1rep),
                    reads=[b_pc, b_b1], writes=[b_pb])
                q.op("gpsimd", lambda e: e.tensor_add(
                    out=pc, in0=pa, in1=pb),
                    reads=[b_pa, b_pb], writes=[b_pc])
                q.op("vector", lambda e: e.tensor_add(
                    out=xar, in0=pc, in1=pop),
                    reads=[b_pc, b_pop], writes=[b_x0p])
                q.dma("sync", lambda e, r0=r0: e.dma_start(
                    out=xa_out[r0:r0 + 128, :], in_=xar),
                    reads=[b_x0p])

        def emit_debug0():
            q.dma("sync", lambda e: e.dma_start(out=dbg[0], in_=aint),
                  reads=[b_aint])
            q.dma("sync", lambda e: e.dma_start(out=dbg[1], in_=ln1t),
                  reads=[b_ln1t])
            q.dma("sync", lambda e: e.dma_start(out=dbg[2], in_=rmst),
                  reads=[b_rmst])
            q.dma("gpsimd", lambda e: e.dma_start(
                out=dbg[3][:, 0:515], in_=xi[0]), reads=[b_xi[0]])
            q.dma("sync", lambda e: e.dma_start(
                out=dbg[4][:, 0:512], in_=xcm[0][0]), reads=[b_xcm[0][0]])
            q.dma("gpsimd", lambda e: e.dma_start(
                out=dbg[5][:, 0:512], in_=xcp[0]), reads=[b_xcp[0]])
            q.dma("sync", lambda e: e.dma_start(
                out=dbg[6][:, 0:512], in_=delta[0][0]), reads=[b_delta[0][0]])
            q.dma("gpsimd", lambda e: e.dma_start(
                out=dbg[7][:, 0:512], in_=sz[0][0]), reads=[b_sz[0][0]])
            q.dma("sync", lambda e: e.dma_start(
                out=dbg[8][0:32, 0:512], in_=bc_sb), reads=[b_bcsb])
            q.dma("sync", lambda e: e.dma_start(
                out=dbg[8][:, 600:606], in_=mvt), reads=[b_mvt])
            q.dma("sync", lambda e: e.dma_start(
                out=dbg[8][:, 610:618], in_=smallt), reads=[b_small])

        def emit_debug1():
            q.dma("sync", lambda e: e.dma_start(
                out=dbg[9][:, 0:64], in_=state), reads=[b_state])
            q.dma("gpsimd", lambda e: e.dma_start(
                out=dbg[10][:, 0:512], in_=y3t), reads=[b_y3t])
            q.dma("sync", lambda e: e.dma_start(
                out=dbg[11][:, 0:512], in_=t1), reads=[b_t1])
            q.dma("sync", lambda e: e.dma_start(
                out=dbg[12][:, 0:512], in_=hsc[1]), reads=[b_h[1]])
            q.dma("sync", lambda e: e.dma_start(
                out=dbg[13][:, 0:512], in_=da[1]), reads=[b_da[1]])
            q.dma("sync", lambda e: e.dma_start(
                out=dbg[14][:, 0:512], in_=bxt[1]), reads=[b_bx[1]])
            q.dma("sync", lambda e: e.dma_start(
                out=dbg[15][:, 0:64], in_=a_sb), reads=[b_a])

        for s in range(NSLAB):
            emit_prep(s)
            if debug and s == 0:
                emit_debug0()
            if s > 0:
                emit_scan(s - 1)
                if debug and s == 1:
                    emit_debug1()
                if s % 2 == 0:
                    emit_ag(s // 2 - 1)
                    emit_post(s // 2 - 1)
        emit_scan(NSLAB - 1)
        emit_ag(NSLAB // 2 - 1)
        emit_post(NSLAB // 2 - 1)



def build_p1(xa_kind="ExternalOutput", debug=False, serial=False,
             noskip=False):
    nc = bass.Bass("TRN2", target_bir_lowering=False)
    q = Prog(nc)
    X = p1_drams(nc, xa_kind, debug)
    emit_p1(nc, q, X, debug=debug)
    with ExitStack() as fst:
        q.finalize(fst, serial=serial, noskip=noskip)
    return nc


def stage_p1_inputs(ins, core):
    b, ch = core >> 1, core & 1
    x = np.asarray(ins["L0_attn_x"])
    mine = slice(ch * EH, (ch + 1) * EH)
    other = slice((1 - ch) * EH, (2 - ch) * EH)
    perm = np.r_[np.arange(ch * EH, (ch + 1) * EH),
                 np.arange((1 - ch) * EH, (2 - ch) * EH)]
    inpw = np.asarray(ins["L0_attn_in_proj_w"])
    d = {
        "xb": np.ascontiguousarray(x[b]),
        "inpw_x": np.ascontiguousarray(inpw[:, perm]),
        "inpw_z": np.ascontiguousarray(inpw[:, 1024 + ch * EH:1024 + (ch + 1) * EH]),
        "ln1_g": np.ascontiguousarray(ins["L0_attn_ln1_g"]),
        "ln1_b": np.ascontiguousarray(ins["L0_attn_ln1_b"]),
        "rms_w": np.ascontiguousarray(ins["L0_attn_rms_w"]),
        "conv_w": np.ascontiguousarray(np.asarray(ins["L0_attn_conv_w"])[perm]),
        "conv_b": np.ascontiguousarray(np.asarray(ins["L0_attn_conv_b"])[perm]),
        "xprojw": np.ascontiguousarray(np.asarray(ins["L0_attn_x_proj_w"])[perm]),
        "dtw_h": np.ascontiguousarray(np.asarray(ins["L0_attn_dt_proj_w"])[:, mine]),
        "dtb_h": np.ascontiguousarray(np.asarray(ins["L0_attn_dt_proj_b"])[mine]),
        "alog_h": np.ascontiguousarray(np.asarray(ins["L0_attn_A_log"])[mine]),
        "dp_h": np.ascontiguousarray(np.asarray(ins["L0_attn_D_param"])[mine]),
        "opw": np.ascontiguousarray(ins["L0_attn_out_proj_w"]),
        "x0h": np.ascontiguousarray(
            x[b].reshape(NSLAB, ST, D)[ch::2].reshape(T // 2, D)),
        "blendm": np.array([float(ch)], np.float32),
    }
    return {k: np.asarray(v, dtype=np.float32) for k, v in d.items()}




def p2_drams(nc, xa_kind="ExternalInput", x1_kind="ExternalOutput",
             x0h=None, xa_in=None):
    dram = lambda n, s, dt=F32, k="ExternalInput", **kw: nc.dram_tensor(
        n, s, dt, kind=k, **kw).ap()
    X = {}
    X["xa_in"] = xa_in if xa_in is not None else \
        dram("xa_in", [T // 2, D], k=xa_kind)
    X["x0h"] = x0h if x0h is not None else dram("x0h", [T // 2, D])
    X["w1d"] = dram("w1d", [D, 4 * D])
    X["b1d"] = dram("b1d", [4 * D])
    X["w2d"] = dram("w2d", [4 * D, D])
    X["b2d"] = dram("b2d", [D])
    X["g2d"] = dram("g2d", [D])
    X["bt2d"] = dram("bt2d", [D])
    X["x1_out"] = dram("x1_out", [T // 2, D], k=x1_kind)
    return X


def emit_p2(nc, q, X):
    (xa_in, x0h, w1d, b1d, w2d, b2d, g2d, bt2d, x1_out) = (X[k] for k in (
        "xa_in", "x0h", "w1d", "b1d", "w2d", "b2d", "g2d", "bt2d", "x1_out"))
    ST2 = 256
    NS2 = (T // 2) // ST2

    st = ExitStack()
    sb = lambda name, shape, dt=F32: st.enter_context(
        nc.sbuf_tensor("B_" + name, shape, dt)).ap()
    mkpsum = lambda name, shape: st.enter_context(
        nc.psum_tensor("B_" + name, shape, F32)).ap()
    B = Buf
    with st:
        w1 = [sb(f"w1_{k}", [128, 4 * D], BF16) for k in range(NCH)]
        w2 = [sb(f"w2_{k}", [128, D], BF16) for k in range(32)]
        b1c = sb("b1c", [128, 32])
        b2row = sb("b2row", [1, D], BF16)
        onesb = sb("onesb", [1, 128], BF16)
        g2c = sb("g2c", [128, NCH])
        bt2c = sb("bt2c", [128, NCH])
        identb = sb("identb", [128, 128], BF16)
        ident = sb("ident", [128, 128])
        eps_sb = sb("eps_sb", [128, 1])
        row_t = sb("row_t", [1, 1024])
        stage = sb("stage", [128, 2048])
        b1row = sb("b1row", [1, 4 * D])
        xat = sb("xat", [128, D])
        x0t = sb("x0t", [128, D])
        ln2t = sb("ln2t", [128, D])
        ln2fm = [sb(f"ln2fm{k}", [128, ST2], BF16) for k in range(NCH)]
        g1 = [sb(f"g1_{dc}", [128, ST2], BF16) for dc in range(32)]
        x1t = sb("x1t", [128, D])
        stat = sb("stat", [128, 12])
        mvt = sb("mvt", [128, 6])
        smallt = sb("smallt", [128, 8])
        pm = [mkpsum(f"pm{i}", [128, 512]) for i in range(2)]
        pt = mkpsum("pt", [128, 512])
        pop = mkpsum("pop", [128, D])
        pb1 = mkpsum("pb1", [128, 512])

        b_w1 = [B() for _ in range(NCH)]
        b_w2 = [B() for _ in range(32)]
        b_b1c = B(); b_b2r = B(); b_ones = B(); b_g2 = B(); b_bt2 = B()
        b_idb = B(); b_id = B(); b_eps = B(); b_row = B(); b_stage = B()
        b_b1row = B()
        b_xat = B(); b_x0t = B(); b_ln2t = B()
        b_ln2fm = [B() for _ in range(NCH)]
        b_g1 = [B() for _ in range(32)]
        b_x1t = B(); b_stat = B(); b_mvt = B(); b_small = B()
        b_pm = [B(), B()]; b_pt = B(); b_pop = B(); b_pb1 = B()

        def colload(vec, n, out_ap, out_buf):
            q.dma("sync", lambda e: e.dma_start(out=row_t[:, 0:n * 128],
                                                in_=vec[None, :]),
                  writes=[b_row], private=True)
            for c in range(n):
                q.op("tensor", lambda e, c=c: e.transpose(
                    pt[:, c:c + 1], in_=row_t[:, c * 128:(c + 1) * 128],
                    identity=ident[0:1, 0:1]),
                    reads=[b_row, b_id], writes=[b_pt])
            q.op("scalar", lambda e: e.activation(
                out=out_ap, in_=pt[:, 0:n], func=AF.Copy),
                reads=[b_pt], writes=[out_buf])

        def lnstats(xap, xbuf, mu_sl, r_sl):
            for i in range(2):
                q.op("vector", lambda e, i=i: e.bn_stats(
                    out=stat[:, 6 * i:6 * i + 6],
                    in_=xap[:, 512 * i:512 * (i + 1)]),
                    reads=[xbuf], writes=[b_stat])
            q.op("vector", lambda e: e.bn_aggr(out=mvt[:, 0:2],
                                               in_=stat[:, 0:6]),
                 reads=[b_stat], writes=[b_mvt])
            q.op("vector", lambda e: e.bn_aggr(out=mvt[:, 2:4],
                                               in_=stat[:, 6:12]),
                 reads=[b_stat], writes=[b_mvt])
            q.op("vector", lambda e: e.tensor_scalar(
                out=mu_sl, in0=mvt[:, 0:1], scalar1=mvt[:, 2:3],
                scalar2=0.5, op0=OP.add, op1=OP.mult),
                reads=[b_mvt], sreads=[b_mvt], writes=[b_small])
            q.op("vector", lambda e: e.tensor_scalar(
                out=smallt[:, 4:5], in0=mvt[:, 0:1], scalar1=mvt[:, 2:3],
                scalar2=None, op0=OP.subtract),
                reads=[b_mvt], sreads=[b_mvt], writes=[b_small])
            q.op("scalar", lambda e: e.activation(
                out=smallt[:, 5:6], in_=smallt[:, 4:5], func=AF.Square,
                scale=0.5), reads=[b_small], writes=[b_small])
            q.op("vector", lambda e: e.tensor_scalar(
                out=smallt[:, 6:7], in0=mvt[:, 1:2], scalar1=mvt[:, 3:4],
                scalar2=0.5, op0=OP.add, op1=OP.mult),
                reads=[b_mvt], sreads=[b_mvt], writes=[b_small])
            q.op("vector", lambda e: e.tensor_scalar(
                out=smallt[:, 6:7], in0=smallt[:, 6:7],
                scalar1=smallt[:, 5:6], scalar2=None, op0=OP.add),
                reads=[b_small], sreads=[b_small], writes=[b_small])
            q.op("scalar", lambda e: e.activation(
                out=smallt[:, 7:8], in_=smallt[:, 6:7], func=AF.Sqrt,
                bias=eps_sb),
                reads=[b_small], sreads=[b_eps], writes=[b_small])
            q.op("vector", lambda e: e.reciprocal(out=r_sl,
                                                  in_=smallt[:, 7:8]),
                 reads=[b_small], writes=[b_small])

        q.op("vector", lambda e: e.memset(eps_sb, EPS), writes=[b_eps])
        q.op("gpsimd", lambda e: e.memset(ident, 0.0), writes=[b_id])
        q.op("gpsimd", lambda e: e.affine_select(
            out=ident, in_=ident, compare_op=OP.not_equal, fill=1.0, base=0,
            pattern=[[-1, 128]], channel_multiplier=1),
            reads=[b_id], writes=[b_id])
        q.op("vector", lambda e: e.tensor_copy(out=identb, in_=ident),
             reads=[b_id], writes=[b_idb])
        q.op("vector", lambda e: e.memset(onesb, 1.0), writes=[b_ones])
        colload(g2d, NCH, g2c, b_g2)
        colload(bt2d, NCH, bt2c, b_bt2)
        q.dma("gpsimd", lambda e: e.dma_start(out=b2row, in_=b2d[None, :]),
              writes=[b_b2r], private=True)
        q.dma("sync", lambda e: e.dma_start(out=b1row, in_=b1d[None, :]),
              writes=[b_b1row], private=True)
        for k in range(NCH):
            for h in range(2):
                q.dma("sync", lambda e, k=k, h=h: e.dma_start(
                    out=stage, in_=w1d[k * 128:(k + 1) * 128,
                                       h * 2048:(h + 1) * 2048]),
                    writes=[b_stage])
                q.op("vector", lambda e, k=k, h=h: e.tensor_scalar(
                    out=w1[k][:, h * 2048:(h + 1) * 2048], in0=stage,
                    scalar1=g2c[:, k:k + 1], scalar2=None, op0=OP.mult),
                    reads=[b_stage], sreads=[b_g2], writes=[b_w1[k]])
        for g in range(8):
            for k in range(NCH):
                q.dma("sync", lambda e, k=k, g=g: e.dma_start(
                    out=stage[:, 0:512],
                    in_=w1d[k * 128:(k + 1) * 128, g * 512:(g + 1) * 512]),
                    writes=[b_stage])
                q.op("tensor", lambda e, k=k, g=g: e.matmul(
                    pb1[0:1, :], lhsT=bt2c[:, k:k + 1], rhs=stage[:, 0:512],
                    start=(k == 0), stop=(k == NCH - 1)),
                    reads=[b_stage, b_bt2], writes=[b_pb1])
            q.op("vector", lambda e, g=g: e.tensor_tensor(
                out=row_t[0:1, 0:512], in0=pb1[0:1, 0:512],
                in1=b1row[0:1, g * 512:(g + 1) * 512], op=OP.add),
                reads=[b_pb1, b_b1row], writes=[b_row])
            for c2 in range(4):
                q.op("tensor", lambda e, g=g, c2=c2: e.transpose(
                    pt[:, g * 4 + c2:g * 4 + c2 + 1],
                    in_=row_t[0:1, c2 * 128:(c2 + 1) * 128],
                    identity=ident[0:1, 0:1]),
                    reads=[b_row, b_id], writes=[b_pt])
        q.op("scalar", lambda e: e.activation(
            out=b1c, in_=pt[:, 0:32], func=AF.Copy),
            reads=[b_pt], writes=[b_b1c])
        for kc in range(32):
            q.dma("sync", lambda e, kc=kc: e.dma_start(
                out=stage[:, 0:1024], in_=w2d[kc * 128:(kc + 1) * 128, :]),
                writes=[b_stage])
            q.op("scalar", lambda e, kc=kc: e.activation(
                out=w2[kc], in_=stage[:, 0:1024], func=AF.Copy),
                reads=[b_stage], writes=[b_w2[kc]])

        for s2 in range(NS2):
            for tt in range(2):
                r0 = s2 * ST2 + tt * 128
                q.dma("sync", lambda e, r0=r0: e.dma_start(
                    out=xat, in_=xa_in[r0:r0 + 128, :]), writes=[b_xat])
                lnstats(xat, b_xat, smallt[:, 0:1], smallt[:, 1:2])
                q.op("vector", lambda e: e.tensor_scalar(
                    out=ln2t, in0=xat, scalar1=smallt[:, 0:1],
                    scalar2=smallt[:, 1:2], op0=OP.subtract, op1=OP.mult),
                    reads=[b_xat], sreads=[b_small], writes=[b_ln2t])
                for k in range(NCH):
                    q.op("tensor", lambda e, k=k: e.transpose(
                        pt[:, 0:128], in_=ln2t[:, k * 128:(k + 1) * 128],
                        identity=ident),
                        reads=[b_ln2t, b_id], writes=[b_pt])
                    q.op("scalar", lambda e, k=k, tt=tt: e.activation(
                        out=ln2fm[k][:, tt * 128:(tt + 1) * 128],
                        in_=pt[:, 0:128], func=AF.Copy),
                        reads=[b_pt], writes=[b_ln2fm[k]])
            for dc in range(32):
                pmx = dc % 2
                for k in range(NCH):
                    q.op("tensor", lambda e, dc=dc, k=k, pmx=pmx: e.matmul(
                        pm[pmx][:, 0:ST2],
                        lhsT=w1[k][:, dc * 128:(dc + 1) * 128],
                        rhs=ln2fm[k], start=(k == 0), stop=(k == NCH - 1)),
                        reads=[b_w1[k], b_ln2fm[k]], writes=[b_pm[pmx]])
                q.op("scalar", lambda e, dc=dc, pmx=pmx: e.activation(
                    out=g1[dc], in_=pm[pmx][:, 0:ST2], func=AF.Relu,
                    bias=b1c[:, dc:dc + 1]),
                    reads=[b_pm[pmx]], sreads=[b_b1c], writes=[b_g1[dc]])
            for tt in range(2):
                r0 = s2 * ST2 + tt * 128
                for hh in range(2):
                    for kc in range(32):
                        q.op("tensor", lambda e, kc=kc, tt=tt, hh=hh: e.matmul(
                            pop[:, hh * 512:(hh + 1) * 512],
                            lhsT=g1[kc][:, tt * 128:(tt + 1) * 128],
                            rhs=w2[kc][:, hh * 512:(hh + 1) * 512],
                            start=(kc == 0), stop=False),
                            reads=[b_g1[kc], b_w2[kc]], writes=[b_pop])
                    q.op("tensor", lambda e, tt=tt, hh=hh: e.matmul(
                        pop[:, hh * 512:(hh + 1) * 512], lhsT=onesb,
                        rhs=b2row[0:1, hh * 512:(hh + 1) * 512],
                        start=False, stop=True),
                        reads=[b_ones, b_b2r], writes=[b_pop])
                q.dma("sync", lambda e, r0=r0: e.dma_start(
                    out=xat, in_=xa_in[r0:r0 + 128, :]), writes=[b_xat])
                q.dma("sync", lambda e, r0=r0: e.dma_start(
                    out=x0t, in_=x0h[r0:r0 + 128, :]), writes=[b_x0t])
                q.op("vector", lambda e: e.tensor_add(
                    out=x1t, in0=xat, in1=pop),
                    reads=[b_xat, b_pop], writes=[b_x1t])
                q.op("vector", lambda e: e.tensor_add(
                    out=x1t, in0=x1t, in1=x0t),
                    reads=[b_x1t, b_x0t], writes=[b_x1t])
                q.dma("sync", lambda e, r0=r0: e.dma_start(
                    out=x1_out[r0:r0 + 128, :], in_=x1t),
                    reads=[b_x1t])


def build_p2(noskip=True):
    nc = bass.Bass("TRN2", target_bir_lowering=False)
    q = Prog(nc)
    X = p2_drams(nc)
    emit_p2(nc, q, X)
    with ExitStack() as fst:
        q.finalize(fst, noskip=noskip)
    return nc


def stage_p2_inputs(ins, core, xa=None):
    b, ch = core >> 1, core & 1
    x = np.asarray(ins["L0_attn_x"])
    d = {
        "x0h": np.ascontiguousarray(
            x[b].reshape(NSLAB, ST, D)[ch::2].reshape(T // 2, D)),
        "w1d": np.ascontiguousarray(ins["L0_attn_ffn_w1"]),
        "b1d": np.ascontiguousarray(ins["L0_attn_ffn_b1"]),
        "w2d": np.ascontiguousarray(ins["L0_attn_ffn_w2"]),
        "b2d": np.ascontiguousarray(ins["L0_attn_ffn_b2"]),
        "g2d": np.ascontiguousarray(ins["L0_attn_ln2_g"]),
        "bt2d": np.ascontiguousarray(ins["L0_attn_ln2_b"]),
    }
    if xa is not None:
        d["xa_in"] = np.ascontiguousarray(xa)
    return {k: np.asarray(v, dtype=np.float32) for k, v in d.items()}




IC3 = [(0, 512), (512, 512), (1024, 512), (1536, 512), (2048, 512),
       (2560, 256)]


def p3_drams(nc, x1_kind="ExternalInput", out_kind="ExternalOutput",
             x1_in=None):
    dram = lambda n, s, dt=F32, k="ExternalInput", **kw: nc.dram_tensor(
        n, s, dt, kind=k, **kw).ap()
    X = {}
    X["x1_in"] = x1_in if x1_in is not None else \
        dram("x1_in", [T // 2, D], k=x1_kind)
    X["wg_sl"] = dram("wg_sl", [3 * 128, D])
    X["wu_sl"] = dram("wu_sl", [3 * 128, D])
    X["wd_sl"] = dram("wd_sl", [128, 2816])
    X["out"] = dram("out", [T // 2, D], k=out_kind)
    X["agw1_in"] = dram("agw1_in", [6, 128, D], BF16, k="Internal")
    X["agw1_out"] = dram("agw1_out", [8, 6, 128, D], BF16, k="Internal",
                         addr_space="Local")
    X["agw2_in"] = dram("agw2_in", [128, 2816], BF16, k="Internal")
    X["agw2_out"] = dram("agw2_out", [8, 128, 2816], BF16, k="Internal",
                         addr_space="Local")
    return X


def emit_p3(nc, q, X):
    (x1_in, wg_sl, wu_sl, wd_sl, out_d, agw1_in, agw1_out, agw2_in,
     agw2_out) = (X[k] for k in (
        "x1_in", "wg_sl", "wu_sl", "wd_sl", "out", "agw1_in", "agw1_out",
        "agw2_in", "agw2_out"))
    ST3 = 128
    NS3 = (T // 2) // ST3

    st = ExitStack()
    sb = lambda name, shape, dt=F32: st.enter_context(
        nc.sbuf_tensor("C_" + name, shape, dt)).ap()
    mkpsum = lambda name, shape: st.enter_context(
        nc.psum_tensor("C_" + name, shape, F32)).ap()
    B = Buf
    with st:
        wgT = [sb(f"wgT{k}", [128, 2816], BF16) for k in range(NCH)]
        wuT = [sb(f"wuT{k}", [128, 2816], BF16) for k in range(NCH)]
        wdT = [sb(f"wdT{k}", [128, D], BF16) for k in range(22)]
        ident = sb("ident", [128, 128])
        identb = sb("identb", [128, 128], BF16)
        eps_sb = sb("eps_sb", [128, 1])
        stage = sb("stage", [128, 2816])
        stage_b = sb("stage_b", [128, 2816], BF16)
        qa = sb("qa", [128, 1024])
        qxn = sb("qxn", [128, 1024])
        qb2 = sb("qb2", [128, 1024])
        qc = sb("qc", [128, 1024])
        qp3 = sb("qp3", [128, 1024])
        qsg = sb("qsg", [128, 1024], I32)
        qam = sb("qam", [128, 64])
        qr6 = sb("qr6", [128, 64])
        qs = sb("qs", [128, 64])
        x1t = sb("x1t", [128, D])
        ft = sb("ft", [128, D])
        fq = sb("fq", [128, D])
        fqfm = [sb(f"fqfm{k}", [128, ST3], BF16) for k in range(NCH)]
        ht = sb("ht", [128, 2816])
        hq = stage
        sg_t = sb("sg_t", [128, 512])
        hqfm = [sb(f"hqfm{k}", [128, ST3], BF16) for k in range(22)]
        x2t = ft
        stat = sb("stat", [128, 12])
        mvt = sb("mvt", [128, 6])
        smallt = sb("smallt", [128, 8])
        pg = mkpsum("pg", [128, 512])
        pu = mkpsum("pu", [128, 512])
        pt = mkpsum("pt", [128, 512])
        pd2 = mkpsum("pd2", [128, D])

        b_wgT = [B() for _ in range(NCH)]
        b_wuT = [B() for _ in range(NCH)]
        b_wdT = [B() for _ in range(22)]
        b_id = B(); b_idb = B(); b_eps = B(); b_stage = B(); b_stageb = B()
        b_qa = B(); b_qxn = B(); b_qb2 = B(); b_qc = B(); b_qp3 = B()
        b_qsg = B(); b_qam = B(); b_qr6 = B(); b_qs = B()
        b_x1t = B(); b_ft = B(); b_fq = B()
        b_fqfm = [B() for _ in range(NCH)]
        b_ht = B(); b_sg = B()
        b_hq = b_stage
        b_hqfm = [B() for _ in range(22)]
        b_stat = B(); b_mvt = B(); b_small = B()
        b_x2t = b_ft
        b_pg = B(); b_pu = B(); b_pt = B(); b_pd2 = B()
        b_agw1i = B(); b_agw1o = B(); b_agw2i = B(); b_agw2o = B()

        def quant(x_ap, x_buf, out_ap, out_buf, nblk, gq="gpsimd"):
            scr = dict(a=(qa, b_qa), xn=(qxn, b_qxn), b2=(qb2, b_qb2),
                       c=(qc, b_qc), p3=(qp3, b_qp3), sg=(qsg, b_qsg),
                       am=(qam, b_qam), r6=(qr6, b_qr6), s=(qs, b_qs))
            n16 = nblk * 16
            q.op("vector", lambda e: e.tensor_scalar(
                out=qsg[:, 0:n16], in0=x_ap.bitcast(I32),
                scalar1=-0x80000000, scalar2=None, op0=OP.bitwise_and),
                reads=[x_buf], writes=[b_qsg])
            q.op("vector", lambda e: e.tensor_scalar(
                out=qa.bitcast(I32)[:, 0:n16], in0=x_ap.bitcast(I32),
                scalar1=0x7FFFFFFF, scalar2=None, op0=OP.bitwise_and),
                reads=[x_buf], writes=[b_qa])
            q.op("vector", lambda e: e.tensor_reduce(
                out=qam[:, 0:nblk],
                in_=qa[:, 0:n16].rearrange("p (b k) -> p b k", k=16),
                axis=AX.X, op=OP.max), reads=[b_qa], writes=[b_qam])
            q.op("vector", lambda e: e.reciprocal(
                out=qr6[:, 0:nblk], in_=qam[:, 0:nblk]),
                sreads=[b_qam], writes=[b_qr6])
            q.op("vector", lambda e: e.tensor_single_scalar(
                out=qr6[:, 0:nblk], in_=qr6[:, 0:nblk], scalar=6.0,
                op=OP.mult), sreads=[b_qr6], writes=[b_qr6])
            q.op("vector", lambda e: e.tensor_single_scalar(
                out=qs[:, 0:nblk], in_=qam[:, 0:nblk], scalar=1.0 / 6.0,
                op=OP.mult), sreads=[b_qam], writes=[b_qs])
            q.op(gq, lambda e: e.tensor_tensor(
                out=qxn[:, 0:n16].rearrange("p (b k) -> p b k", k=16),
                in0=qa[:, 0:n16].rearrange("p (b k) -> p b k", k=16),
                in1=qr6[:, 0:nblk].unsqueeze(2).broadcast_to([128, nblk, 16]),
                op=OP.mult), reads=[b_qa, b_qr6], writes=[b_qxn])
            q.op("vector", lambda e: e.tensor_scalar(
                out=qb2[:, 0:n16], in0=qxn[:, 0:n16], scalar1=2.0,
                scalar2=MAGIC, op0=OP.mult, op1=OP.add),
                reads=[b_qxn], writes=[b_qb2])
            q.op("vector", lambda e: e.tensor_scalar(
                out=qb2[:, 0:n16], in0=qb2[:, 0:n16], scalar1=MAGIC,
                scalar2=0.5, op0=OP.subtract, op1=OP.mult),
                reads=[b_qb2], writes=[b_qb2])
            q.op("vector", lambda e: e.tensor_single_scalar(
                out=qb2[:, 0:n16], in_=qb2[:, 0:n16], scalar=2.0, op=OP.min),
                reads=[b_qb2], writes=[b_qb2])
            q.op("vector", lambda e: e.tensor_scalar(
                out=qc[:, 0:n16], in0=qxn[:, 0:n16], scalar1=MAGIC,
                scalar2=MAGIC, op0=OP.add, op1=OP.subtract),
                reads=[b_qxn], writes=[b_qc])
            q.op("vector", lambda e: e.tensor_scalar(
                out=qc[:, 0:n16], in0=qc[:, 0:n16], scalar1=4.0, scalar2=2.0,
                op0=OP.min, op1=OP.subtract), reads=[b_qc], writes=[b_qc])
            q.op("vector", lambda e: e.scalar_tensor_tensor(
                out=qc[:, 0:n16], in0=qc[:, 0:n16], scalar=0.0,
                in1=qb2[:, 0:n16], op0=OP.max, op1=OP.add),
                reads=[b_qc, b_qb2], writes=[b_qc])
            q.op("vector", lambda e: e.tensor_scalar(
                out=qp3[:, 0:n16], in0=qxn[:, 0:n16], scalar1=5.0,
                scalar2=2.0, op0=OP.is_gt, op1=OP.mult),
                reads=[b_qxn], writes=[b_qp3])
            q.op(gq, lambda e: e.tensor_tensor(
                out=qc[:, 0:n16], in0=qc[:, 0:n16], in1=qp3[:, 0:n16],
                op=OP.add), reads=[b_qc, b_qp3], writes=[b_qc])
            q.op(gq, lambda e: e.tensor_tensor(
                out=qxn[:, 0:n16].rearrange("p (b k) -> p b k", k=16),
                in0=qc[:, 0:n16].rearrange("p (b k) -> p b k", k=16),
                in1=qs[:, 0:nblk].unsqueeze(2).broadcast_to([128, nblk, 16]),
                op=OP.mult), reads=[b_qc, b_qs], writes=[b_qxn])
            q.op("vector", lambda e: e.tensor_tensor(
                out=out_ap.bitcast(I32), in0=qxn.bitcast(I32)[:, 0:n16],
                in1=qsg[:, 0:n16], op=OP.bitwise_or),
                reads=[b_qxn, b_qsg], writes=[out_buf])

        def lnstats(xap, xbuf, mu_sl, r_sl):
            for i in range(2):
                q.op("vector", lambda e, i=i: e.bn_stats(
                    out=stat[:, 6 * i:6 * i + 6],
                    in_=xap[:, 512 * i:512 * (i + 1)]),
                    reads=[xbuf], writes=[b_stat])
            q.op("vector", lambda e: e.bn_aggr(out=mvt[:, 0:2],
                                               in_=stat[:, 0:6]),
                 reads=[b_stat], writes=[b_mvt])
            q.op("vector", lambda e: e.bn_aggr(out=mvt[:, 2:4],
                                               in_=stat[:, 6:12]),
                 reads=[b_stat], writes=[b_mvt])
            q.op("vector", lambda e: e.tensor_scalar(
                out=mu_sl, in0=mvt[:, 0:1], scalar1=mvt[:, 2:3],
                scalar2=0.5, op0=OP.add, op1=OP.mult),
                reads=[b_mvt], sreads=[b_mvt], writes=[b_small])
            q.op("vector", lambda e: e.tensor_scalar(
                out=smallt[:, 4:5], in0=mvt[:, 0:1], scalar1=mvt[:, 2:3],
                scalar2=None, op0=OP.subtract),
                reads=[b_mvt], sreads=[b_mvt], writes=[b_small])
            q.op("scalar", lambda e: e.activation(
                out=smallt[:, 5:6], in_=smallt[:, 4:5], func=AF.Square,
                scale=0.5), reads=[b_small], writes=[b_small])
            q.op("vector", lambda e: e.tensor_scalar(
                out=smallt[:, 6:7], in0=mvt[:, 1:2], scalar1=mvt[:, 3:4],
                scalar2=0.5, op0=OP.add, op1=OP.mult),
                reads=[b_mvt], sreads=[b_mvt], writes=[b_small])
            q.op("vector", lambda e: e.tensor_scalar(
                out=smallt[:, 6:7], in0=smallt[:, 6:7],
                scalar1=smallt[:, 5:6], scalar2=None, op0=OP.add),
                reads=[b_small], sreads=[b_small], writes=[b_small])
            q.op("scalar", lambda e: e.activation(
                out=smallt[:, 7:8], in_=smallt[:, 6:7], func=AF.Sqrt,
                bias=eps_sb),
                reads=[b_small], sreads=[b_eps], writes=[b_small])
            q.op("vector", lambda e: e.reciprocal(out=r_sl,
                                                  in_=smallt[:, 7:8]),
                 reads=[b_small], writes=[b_small])

        q.op("vector", lambda e: e.memset(eps_sb, EPS), writes=[b_eps])
        q.op("gpsimd", lambda e: e.memset(ident, 0.0), writes=[b_id])
        q.op("gpsimd", lambda e: e.affine_select(
            out=ident, in_=ident, compare_op=OP.not_equal, fill=1.0, base=0,
            pattern=[[-1, 128]], channel_multiplier=1),
            reads=[b_id], writes=[b_id])
        q.op("vector", lambda e: e.tensor_copy(out=identb, in_=ident),
             reads=[b_id], writes=[b_idb])
        for sl in range(3):
            for wsrc, slot in ((wg_sl, sl), (wu_sl, 3 + sl)):
                q.dma("sync", lambda e, wsrc=wsrc, sl=sl: e.dma_start(
                    out=stage[:, 0:D], in_=wsrc[sl * 128:(sl + 1) * 128, :]),
                    writes=[b_stage])
                quant(stage[:, 0:D], b_stage, qp3[:, 0:D], b_qp3, 64)
                q.op("scalar", lambda e: e.activation(
                    out=stage_b[:, 0:D], in_=qp3[:, 0:D], func=AF.Copy),
                    reads=[b_qp3], writes=[b_stageb])
                q.dma("sync", lambda e, slot=slot: e.dma_start(
                    out=agw1_in[slot], in_=stage_b[:, 0:D]),
                    reads=[b_stageb], writes=[b_agw1i])
        q.dma("sync", lambda e: e.dma_start(out=stage, in_=wd_sl),
              writes=[b_stage])
        for off, w in ((0, 1024), (1024, 1024), (2048, 768)):
            quant(stage[:, off:off + w], b_stage, qp3[:, 0:w], b_qp3, w // 16)
            q.op("scalar", lambda e, off=off, w=w: e.activation(
                out=stage_b[:, off:off + w], in_=qp3[:, 0:w], func=AF.Copy),
                reads=[b_qp3], writes=[b_stageb])
        q.dma("sync", lambda e: e.dma_start(out=agw2_in, in_=stage_b),
              reads=[b_stageb], writes=[b_agw2i])
        ag1 = q.op("gpsimd", lambda e: e.collective_compute(
            "AllGather", OP.bypass,
            replica_groups=[list(range(8))],
            ins=[agw1_in], outs=[agw1_out]),
            reads=[b_agw1i], writes=[b_agw1o])
        ag2 = q.op("gpsimd", lambda e: e.collective_compute(
            "AllGather", OP.bypass,
            replica_groups=[list(range(8))],
            ins=[agw2_in], outs=[agw2_out]),
            reads=[b_agw2i], writes=[b_agw2o])
        for j in range(22):
            rk, sl = j % 8, j // 8
            for which, wt, bt, slot in (("g", wgT, b_wgT, sl),
                                        ("u", wuT, b_wuT, 3 + sl)):
                q.dma("sync", lambda e, rk=rk, slot=slot: e.dma_start(
                    out=stage_b[:, 0:D], in_=agw1_out[rk, slot]),
                    deps=[ag1], reads=[b_agw1o], writes=[b_stageb])
                for k in range(NCH):
                    q.op("tensor", lambda e, k=k: e.transpose(
                        pt[:, 0:128].bitcast(BF16)[:, 0:128],
                        in_=stage_b[:, k * 128:(k + 1) * 128],
                        identity=identb),
                        reads=[b_stageb, b_idb], writes=[b_pt])
                    q.op("scalar", lambda e, k=k, j=j, wt=wt, bt=bt:
                         e.activation(
                             out=wt[k][:, j * 128:(j + 1) * 128],
                             in_=pt[:, 0:128].bitcast(BF16)[:, 0:128],
                             func=AF.Copy),
                         reads=[b_pt], writes=[bt[k]])
        for j in range(8):
            q.dma("sync", lambda e, j=j: e.dma_start(
                out=stage_b, in_=agw2_out[j]),
                deps=[ag2], reads=[b_agw2o], writes=[b_stageb])
            for kc in range(22):
                q.op("tensor", lambda e, kc=kc: e.transpose(
                    pt[:, 0:128].bitcast(BF16)[:, 0:128],
                    in_=stage_b[:, kc * 128:(kc + 1) * 128],
                    identity=identb),
                    reads=[b_stageb, b_idb], writes=[b_pt])
                q.op("scalar", lambda e, kc=kc, j=j: e.activation(
                    out=wdT[kc][:, j * 128:(j + 1) * 128],
                    in_=pt[:, 0:128].bitcast(BF16)[:, 0:128],
                    func=AF.Copy),
                    reads=[b_pt], writes=[b_wdT[kc]])

        for s3 in range(NS3):
            r0 = s3 * ST3
            q.dma("sync", lambda e, r0=r0: e.dma_start(
                out=x1t, in_=x1_in[r0:r0 + 128, :]), writes=[b_x1t])
            lnstats(x1t, b_x1t, smallt[:, 0:1], smallt[:, 1:2])
            q.op("vector", lambda e: e.tensor_scalar(
                out=ft, in0=x1t, scalar1=smallt[:, 0:1],
                scalar2=smallt[:, 1:2], op0=OP.subtract, op1=OP.mult),
                reads=[b_x1t], sreads=[b_small], writes=[b_ft])
            quant(ft, b_ft, fq, b_fq, 64)
            for k in range(NCH):
                q.op("tensor", lambda e, k=k: e.transpose(
                    pt[:, 0:128], in_=fq[:, k * 128:(k + 1) * 128],
                    identity=ident),
                    reads=[b_fq, b_id], writes=[b_pt])
                q.op("scalar", lambda e, k=k: e.activation(
                    out=fqfm[k], in_=pt[:, 0:128], func=AF.Copy),
                    reads=[b_pt], writes=[b_fqfm[k]])
            for ic, (off, w) in enumerate(IC3):
                for k in range(NCH):
                    q.op("tensor", lambda e, k=k, off=off, w=w: e.matmul(
                        pg[:, 0:w], lhsT=fqfm[k],
                        rhs=wgT[k][:, off:off + w],
                        start=(k == 0), stop=(k == NCH - 1)),
                        reads=[b_fqfm[k], b_wgT[k]], writes=[b_pg])
                for k in range(NCH):
                    q.op("tensor", lambda e, k=k, off=off, w=w: e.matmul(
                        pu[:, 0:w], lhsT=fqfm[k],
                        rhs=wuT[k][:, off:off + w],
                        start=(k == 0), stop=(k == NCH - 1)),
                        reads=[b_fqfm[k], b_wuT[k]], writes=[b_pu])
                q.op("scalar", lambda e, off=off, w=w: e.activation(
                    out=sg_t[:, 0:w], in_=pg[:, 0:w], func=AF.Silu),
                    reads=[b_pg], writes=[b_sg])
                q.op("vector", lambda e, off=off, w=w: e.tensor_tensor(
                    out=ht[:, off:off + w], in0=sg_t[:, 0:w],
                    in1=pu[:, 0:w], op=OP.mult),
                    reads=[b_sg, b_pu], writes=[b_ht])
            for off, w in ((0, 1024), (1024, 1024), (2048, 768)):
                quant(ht[:, off:off + w], b_ht, hq[:, off:off + w], b_hq,
                      w // 16)
            for kc in range(22):
                q.op("tensor", lambda e, kc=kc: e.transpose(
                    pt[:, 0:128], in_=hq[:, kc * 128:(kc + 1) * 128],
                    identity=ident),
                    reads=[b_hq, b_id], writes=[b_pt])
                q.op("scalar", lambda e, kc=kc: e.activation(
                    out=hqfm[kc], in_=pt[:, 0:128], func=AF.Copy),
                    reads=[b_pt], writes=[b_hqfm[kc]])
            for hh in range(2):
                for kc in range(22):
                    q.op("tensor", lambda e, kc=kc, hh=hh: e.matmul(
                        pd2[:, hh * 512:(hh + 1) * 512], lhsT=hqfm[kc],
                        rhs=wdT[kc][:, hh * 512:(hh + 1) * 512],
                        start=(kc == 0), stop=(kc == 21)),
                        reads=[b_hqfm[kc], b_wdT[kc]], writes=[b_pd2])
            q.op("vector", lambda e: e.tensor_add(
                out=x2t, in0=x1t, in1=pd2),
                reads=[b_x1t, b_pd2], writes=[b_x2t])
            q.dma("sync", lambda e, r0=r0: e.dma_start(
                out=out_d[r0:r0 + 128, :], in_=x2t),
                reads=[b_x2t])


def build_p3(noskip=True):
    nc = bass.Bass("TRN2", target_bir_lowering=False)
    q = Prog(nc)
    X = p3_drams(nc)
    emit_p3(nc, q, X)
    with ExitStack() as fst:
        q.finalize(fst, noskip=noskip)
    return nc


def stage_p3_inputs(ins, core, x1=None):
    wg = np.asarray(ins["L0_ffn_w_gate"])
    wu = np.asarray(ins["L0_ffn_w_up"])
    wd = np.asarray(ins["L0_ffn_w_down"])
    c = core
    gsl = [min(c + 8 * i, 21) for i in range(3)]
    d = {
        "wg_sl": np.ascontiguousarray(np.concatenate(
            [wg[j * 128:(j + 1) * 128] for j in gsl], axis=0)),
        "wu_sl": np.ascontiguousarray(np.concatenate(
            [wu[j * 128:(j + 1) * 128] for j in gsl], axis=0)),
        "wd_sl": np.ascontiguousarray(wd[c * 128:(c + 1) * 128]),
    }
    if x1 is not None:
        d["x1_in"] = np.ascontiguousarray(x1)
    return {k: np.asarray(v, dtype=np.float32) for k, v in d.items()}


def build_fused(noskip=True):
    nc = bass.Bass("TRN2", target_bir_lowering=False)
    q = Prog(nc)
    X1 = p1_drams(nc, xa_kind="Internal")
    emit_p1(nc, q, X1)
    q.barrier()
    X2 = p2_drams(nc, x1_kind="Internal", x0h=X1["x0h"],
                  xa_in=X1["xa_out"])
    emit_p2(nc, q, X2)
    q.barrier()
    X3 = p3_drams(nc, x1_in=X2["x1_out"])
    emit_p3(nc, q, X3)
    with ExitStack() as fst:
        q.finalize(fst, noskip=noskip)
    return nc


def stage_fused_inputs(ins, core):
    d = stage_p1_inputs(ins, core)
    d.update(stage_p2_inputs(ins, core))
    d.update(stage_p3_inputs(ins, core))
    return d


_BUILD_CACHE = {}


def _get(name, fn):
    if name not in _BUILD_CACHE:
        _BUILD_CACHE[name] = fn()
    return _BUILD_CACHE[name]


def kernel(**inputs):
    cores = list(range(8))
    nc = _get("fused", lambda: build_fused(noskip=False))
    maps = [stage_fused_inputs(inputs, c) for c in cores]
    r = run_bass_kernel_spmd(nc, maps, core_ids=cores)
    out = np.empty((4, T, D), np.float32)
    for c in cores:
        b, ch = c >> 1, c & 1
        out[b].reshape(NSLAB, ST, D)[ch::2] = \
            np.asarray(r.results[c]["out"]).reshape(NSLAB // 2, ST, D)
    return out


# version 05
# speedup vs baseline: 1.3314x; 1.0144x over previous
import sys
sys.path.insert(0, "/opt/trn_rl_repo")
from contextlib import ExitStack

import numpy as np
import concourse.bass as bass
import concourse.mybir as mybir
from concourse.bass_utils import run_bass_kernel_spmd

F32 = mybir.dt.float32
F32R = mybir.dt.float32r
BF16 = mybir.dt.bfloat16
I32 = mybir.dt.int32
AF = mybir.ActivationFunctionType
OP = mybir.AluOpType
AX = mybir.AxisListType

MAGIC = 12582912.0
EPS = 1e-5
D = 1024
E = 1024
EH = 512
NCH = 8
MCH = 4
NST = 16
T = 8192
ST = 512
NSLAB = T // ST
DMA_INC = 16


class Buf:
    __slots__ = ("writer", "readers", "name")

    def __init__(self, name=""):
        self.writer = None
        self.readers = []
        self.name = name


class _Op:
    __slots__ = ("engine", "fn", "inc", "deps", "semkey", "need_inc",
                 "val_after", "idx", "nsk")


class Prog:
    ENGINES = ("sync", "tensor", "vector", "scalar", "gpsimd")

    def __init__(self, nc):
        self.nc = nc
        self.ops = []
        self.barriers = []

    def barrier(self):
        self.barriers.append(len(self.ops))

    def _op(self, engine, fn, reads, writes, deps, inc, semkey, sreads=None):
        o = _Op()
        o.engine, o.fn, o.inc, o.semkey = engine, fn, inc, semkey
        o.need_inc = False
        o.val_after = None
        o.idx = len(self.ops)
        dep = set(deps or ())
        nsk = set()
        for b in reads or ():
            if b.writer is not None:
                dep.add(b.writer)
        for b in sreads or ():
            if b.writer is not None:
                dep.add(b.writer)
                nsk.add(b.writer)
        for b in writes or ():
            if b.writer is not None:
                dep.add(b.writer)
            dep.update(b.readers)
        o.deps = dep
        o.nsk = nsk
        self.ops.append(o)
        for b in reads or ():
            b.readers.append(o)
        for b in sreads or ():
            b.readers.append(o)
        for b in writes or ():
            b.writer = o
            b.readers = []
        return o

    def op(self, engine, fn, reads=None, writes=None, deps=None, sreads=None):
        return self._op(engine, fn, reads, writes, deps, 1, engine,
                        sreads=sreads)

    def dma(self, queue, fn, reads=None, writes=None, deps=None,
            private=False, stream=None):
        key = (queue + "dma") if stream is None else ("dm_" + stream)
        o = self._op(queue, fn, reads, writes, deps, DMA_INC, key)
        if private:
            o.semkey = "pvt%d" % o.idx
        return o

    def finalize(self, st, serial=False, noskip=False):
        import os
        trace_f = open("/tmp/prog.txt", "w") if os.environ.get("PROGTRACE") \
            else None
        nc = self.nc
        for o in self.ops:
            for d in o.deps:
                d.need_inc = True
        for o in self.ops:
            if o.inc == DMA_INC:
                o.need_inc = True
        cnt = {}
        run_cnt = []
        for o in self.ops:
            if o.need_inc:
                cnt[o.semkey] = cnt.get(o.semkey, 0) + o.inc
                o.val_after = cnt[o.semkey]
            run_cnt.append(dict(cnt))
        sems = {k: st.enter_context(nc.semaphore("s_" + k)) for k in cnt}
        per_engine = {e: [] for e in self.ENGINES}
        for o in self.ops:
            per_engine[o.engine].append(o)
        final_vals = dict(cnt)
        self_barriers = self.barriers
        block = st.enter_context(nc.Block())

        def section(name):
            def _(eng):
                hw = {}
                bar_i = 0
                for o in per_engine[name]:
                    nonlocal_barriers = self_barriers
                    while bar_i < len(nonlocal_barriers) and \
                            o.idx >= nonlocal_barriers[bar_i]:
                        bpos = nonlocal_barriers[bar_i]
                        if bpos > 0:
                            for k, v in run_cnt[bpos - 1].items():
                                if hw.get(k, 0) < v:
                                    eng.wait_ge(sems[k], v)
                                    hw[k] = v
                        bar_i += 1
                    if serial and o.idx > 0:
                        for k, v in run_cnt[o.idx - 1].items():
                            if hw.get(k, 0) < v:
                                eng.wait_ge(sems[k], v)
                                hw[k] = v
                    for d in sorted(o.deps, key=lambda x: x.idx):
                        import os as _os
                        _nsv = _os.environ.get("NOSKIP_ENGINES", "")
                        if (not noskip) and d.engine == o.engine \
                                and d.inc != DMA_INC and d not in o.nsk \
                                and o.engine not in _nsv:
                            continue
                        if d.val_after is None:
                            continue
                        if d.inc == DMA_INC:
                            val = run_cnt[o.idx - 1].get(d.semkey, 0) \
                                if o.idx > 0 else d.val_after
                            val = max(val, d.val_after)
                        else:
                            val = d.val_after
                        if (not noskip) and hw.get(d.semkey, 0) >= val:
                            continue
                        if trace_f is not None:
                            trace_f.write("%s   wait %s >= %d (dep #%d)\n" % (
                                name, d.semkey, val, d.idx))
                        eng.wait_ge(sems[d.semkey], val)
                        hw[d.semkey] = val
                    if trace_f is not None:
                        trace_f.write("%s #%d L%d sem=%s inc=%s\n" % (
                            name, o.idx, o.fn.__code__.co_firstlineno,
                            o.semkey,
                            o.val_after if o.need_inc else "-"))
                    inst = o.fn(eng)
                    if o.need_inc:
                        inst.then_inc(sems[o.semkey], o.inc)
                if name == "sync":
                    for k, v in final_vals.items():
                        eng.wait_ge(sems[k], v)
            return _

        for name in self.ENGINES:
            if per_engine[name] or name == "sync":
                getattr(block, name)(section(name))


def p1_drams(nc, xa_kind="ExternalOutput", debug=False):
    dram = lambda n, s, dt=F32, k="ExternalInput", **kw: nc.dram_tensor(
        n, s, dt, kind=k, **kw).ap()
    X = {}
    xb = X["xb"] = dram("xb", [T, D])
    inpw_x = X["inpw_x"] = dram("inpw_x", [D, E])
    inpw_z = X["inpw_z"] = dram("inpw_z", [D, EH])
    ln1_g = X["ln1_g"] = dram("ln1_g", [D])
    ln1_b = X["ln1_b"] = dram("ln1_b", [D])
    rms_w = X["rms_w"] = dram("rms_w", [D])
    conv_w = X["conv_w"] = dram("conv_w", [E, 4])
    conv_b = X["conv_b"] = dram("conv_b", [E])
    xprojw = X["xprojw"] = dram("xprojw", [E, 96])
    dtw_h = X["dtw_h"] = dram("dtw_h", [64, EH])
    dtb_h = X["dtb_h"] = dram("dtb_h", [EH])
    alog_h = X["alog_h"] = dram("alog_h", [EH, NST])
    dp_h = X["dp_h"] = dram("dp_h", [EH])
    opw = X["opw"] = dram("opw", [E, D])
    x0h = X["x0h"] = dram("x0h", [T // 2, D])
    blendm = X["blendm"] = dram("blendm", [1])
    xa_out = X["xa_out"] = dram("xa_out", [T // 2, D], k=xa_kind)
    bc_dram = X["bc_dram"] = dram("bc_dram", [32, T], k="Internal")
    NG = NSLAB // 2
    X["dbg"] = dram("dbg", [16, 128, 1024], k="ExternalOutput") if debug \
        else None
    X["agy_in"] = dram("agy_in", [NG, MCH, 128, 1024], BF16, k="Internal")
    X["agy_out"] = dram("agy_out", [NG, 2, MCH, 128, 1024], BF16,
                        k="Internal", addr_space="Local")
    return X


def emit_p1(nc, q, X, debug=False):
    (xb, inpw_x, inpw_z, ln1_g, ln1_b, rms_w, conv_w, conv_b, xprojw, dtw_h,
     dtb_h, alog_h, dp_h, opw, x0h, blendm, xa_out, bc_dram, dbg, agy_in,
     agy_out) = (X[k] for k in (
        "xb", "inpw_x", "inpw_z", "ln1_g", "ln1_b", "rms_w", "conv_w",
        "conv_b", "xprojw", "dtw_h", "dtb_h", "alog_h", "dp_h", "opw",
        "x0h", "blendm", "xa_out", "bc_dram", "dbg", "agy_in", "agy_out"))
    NG = NSLAB // 2

    st = ExitStack()
    sb = lambda name, shape, dt=F32: st.enter_context(
        nc.sbuf_tensor("A_" + name, shape, dt)).ap()
    mkpsum = lambda name, shape: st.enter_context(
        nc.psum_tensor("A_" + name, shape, F32)).ap()
    B = Buf

    with st:
        w_inp = [sb(f"w_inp{k}", [128, E + EH], BF16) for k in range(NCH)]
        w_xp = [sb(f"w_xp{k}", [128, 96]) for k in range(NCH)]
        w_dt = [sb(f"w_dt{c}", [64, 128], F32R) for c in range(MCH)]
        w_op = [sb(f"w_op{k}", [128, D], BF16) for k in range(NCH)]
        a_sb = sb("a_sb", [128, MCH * NST])
        cw_sb = [sb(f"cw{k}", [128, 4]) for k in range(NCH)]
        cb_sb = sb("cb_sb", [128, NCH])
        dtb_sb = sb("dtb_sb", [128, MCH])
        dp_sb = sb("dp_sb", [128, MCH])
        rmsw_sb = sb("rmsw_sb", [128, NCH])
        g1rep = sb("g1rep", [128, D], BF16)
        b1rep = sb("b1rep", [128, D], BF16)
        ident = sb("ident", [128, 128])
        identr = sb("identr", [128, 128], F32R)
        blm = sb("blm", [128, 2])
        eps_sb = sb("eps_sb", [128, 1])
        row_t = sb("row_t", [1, 1024])
        x0t = [sb(f"x0t{i}", [128, D]) for i in range(2)]
        aint = sb("aint", [128, D])
        ln1t = sb("ln1t", [128, D])
        rmst = sb("rmst", [128, D])
        stat = sb("stat", [128, 12])
        mvt = sb("mvt", [128, 6])
        smallt = sb("smallt", [128, 8])
        rms_fm = [sb(f"rms_fm{k}", [128, ST], BF16) for k in range(NCH)]
        xi = [sb(f"xi{k}", [128, ST + 3], BF16) for k in range(NCH)]
        cv = sb("cv", [128, ST])
        dt_r = sb("dt_r", [64, ST], F32R)
        bc_sb = sb("bc_sb", [32, ST])
        sp_t = sb("sp_t", [128, ST])
        sp_t2 = sb("sp_t2", [128, ST])
        xcm = [[sb(f"xcm{c}_{i}", [128, ST]) for c in range(MCH)]
               for i in range(2)]
        xcp = [sb(f"xcp{c}", [128, ST]) for c in range(MCH)]
        delta = [[sb(f"delta{c}_{i}", [128, ST]) for c in range(MCH)]
                 for i in range(2)]
        sz = [[sb(f"sz{c}_{i}", [128, ST], BF16) for c in range(MCH)]
              for i in range(2)]
        u_scr = sb("u_scr", [128, ST])
        da = [sb(f"da{i}", [128, ST]) for i in range(2)]
        bxt = [sb(f"bxt{i}", [128, ST]) for i in range(2)]
        hsc = [sb(f"hsc{i}", [128, ST]) for i in range(2)]
        prod = [sb(f"prod{i}", [128, ST], F32R) for i in range(2)]
        bbt = [sb(f"bbt{i}", [128, ST]) for i in range(2)]
        cbt = [sb(f"cbt{i}", [128, ST]) for i in range(2)]
        state = sb("state", [128, MCH * NST])
        t1 = sb("t1", [128, ST])
        y3t = sb("y3t", [128, ST], BF16)
        ye = sb("ye", [128, NCH * 128], BF16)
        yo = sb("yo", [128, NCH * 128], BF16)
        yb = sb("yb", [128, NCH * 128], BF16)
        x0p = sb("x0p", [128, D])
        stage = x0p
        pa = sb("pa", [128, D])
        pb = sb("pb", [128, D])
        pc = sb("pc", [128, D])
        xar = x0p
        stat2 = sb("stat2", [128, 12])
        mvt2 = sb("mvt2", [128, 6])
        small2 = sb("small2", [128, 8])
        pm = [mkpsum(f"pm{i}", [128, 512]) for i in range(2)]
        pt = mkpsum("pt", [128, 512])
        pd = mkpsum("pd", [128, 512])
        pys = mkpsum("pys", [128, 512])
        pop = mkpsum("pop", [128, D])

        b_winp = [B() for _ in range(NCH)]
        b_wxp = [B() for _ in range(NCH)]
        b_wdt = [B() for _ in range(MCH)]
        b_wop = [B() for _ in range(NCH)]
        b_a = B(); b_cw = [B() for _ in range(NCH)]
        b_cb = B(); b_dtb = B(); b_dp = B(); b_rmsw = B()
        b_g1 = B(); b_b1 = B(); b_id = B(); b_idr = B()
        b_blm = B(); b_row = B(); b_eps = B()
        b_x0t = [B(), B()]
        b_aint = B(); b_ln1t = B(); b_rmst = B()
        b_stat = B(); b_mvt = B(); b_small = B()
        b_rmsfm = [B() for _ in range(NCH)]
        b_xi = [B() for _ in range(NCH)]
        b_cv = B(); b_dtr = B(); b_bcsb = B()
        b_spt = B(); b_spt2 = B()
        b_xcm = [[B() for _ in range(MCH)] for _ in range(2)]
        b_xcp = [B() for _ in range(MCH)]
        b_delta = [[B() for _ in range(MCH)] for _ in range(2)]
        b_sz = [[B() for _ in range(MCH)] for _ in range(2)]
        b_uscr = B()
        b_da = [B(), B()]; b_bx = [B(), B()]; b_h = [B(), B()]
        b_prod = [B(), B()]; b_bbt = [B(), B()]; b_cbt = [B(), B()]
        b_state = B(); b_t1 = B(); b_y3t = B()
        b_ye = B(); b_yo = B(); b_yb = B()
        b_x0p = B(); b_pa = B(); b_pb = B(); b_pc = B()
        b_stat2 = B(); b_mvt2 = B(); b_small2 = B()
        b_pm = [B(), B()]
        b_pt = B(); b_pd = B(); b_pys = B(); b_pop = B()
        b_bcdram = B(); b_agyin = B(); b_agyout = B()

        def colload(vec, out_ap, out_buf, n):
            q.dma("sync", lambda e: e.dma_start(out=row_t[:, 0:n * 128],
                                                in_=vec[None, :]),
                  writes=[b_row], private=True)
            for c in range(n):
                q.op("tensor", lambda e, c=c: e.transpose(
                    pt[:, c:c + 1], in_=row_t[:, c * 128:(c + 1) * 128],
                    identity=ident[0:1, 0:1]),
                    reads=[b_row, b_id], writes=[b_pt])
            q.op("scalar", lambda e: e.activation(
                out=out_ap, in_=pt[:, 0:n], func=AF.Copy),
                reads=[b_pt], writes=[out_buf])

        def lnstats(xap, xbuf, sap, sbuf, mvap, mvbuf, mu_sl, r_sl, sc,
                    smbuf):
            s_d, s_d2, s_v, s_sd = sc
            for i in range(2):
                q.op("vector", lambda e, i=i: e.bn_stats(
                    out=sap[:, 6 * i:6 * i + 6],
                    in_=xap[:, 512 * i:512 * (i + 1)]),
                    reads=[xbuf], writes=[sbuf])
            q.op("vector", lambda e: e.bn_aggr(out=mvap[:, 0:2],
                                               in_=sap[:, 0:6]),
                 reads=[sbuf], writes=[mvbuf])
            q.op("vector", lambda e: e.bn_aggr(out=mvap[:, 2:4],
                                               in_=sap[:, 6:12]),
                 reads=[sbuf], writes=[mvbuf])
            q.op("vector", lambda e: e.tensor_scalar(
                out=mu_sl, in0=mvap[:, 0:1], scalar1=mvap[:, 2:3],
                scalar2=0.5, op0=OP.add, op1=OP.mult),
                reads=[mvbuf], sreads=[mvbuf], writes=[smbuf])
            q.op("vector", lambda e: e.tensor_scalar(
                out=s_d, in0=mvap[:, 0:1], scalar1=mvap[:, 2:3],
                scalar2=None, op0=OP.subtract),
                reads=[mvbuf], sreads=[mvbuf], writes=[smbuf])
            q.op("scalar", lambda e: e.activation(
                out=s_d2, in_=s_d, func=AF.Square, scale=0.5),
                reads=[smbuf], writes=[smbuf])
            q.op("vector", lambda e: e.tensor_scalar(
                out=s_v, in0=mvap[:, 1:2], scalar1=mvap[:, 3:4],
                scalar2=0.5, op0=OP.add, op1=OP.mult),
                reads=[mvbuf], sreads=[mvbuf], writes=[smbuf])
            q.op("vector", lambda e: e.tensor_scalar(
                out=s_v, in0=s_v, scalar1=s_d2, scalar2=None, op0=OP.add),
                reads=[smbuf], sreads=[smbuf], writes=[smbuf])
            q.op("scalar", lambda e: e.activation(
                out=s_sd, in_=s_v, func=AF.Sqrt, bias=eps_sb),
                reads=[smbuf], sreads=[b_eps], writes=[smbuf])
            q.op("vector", lambda e: e.reciprocal(out=r_sl, in_=s_sd),
                 reads=[smbuf], writes=[smbuf])

        q.op("vector", lambda e: e.memset(eps_sb, EPS), writes=[b_eps])
        q.op("gpsimd", lambda e: e.memset(ident, 0.0), writes=[b_id])
        q.op("gpsimd", lambda e: e.affine_select(
            out=ident, in_=ident, compare_op=OP.not_equal, fill=1.0, base=0,
            pattern=[[-1, 128]], channel_multiplier=1),
            reads=[b_id], writes=[b_id])
        q.op("vector", lambda e: e.tensor_copy(out=identr, in_=ident),
             reads=[b_id], writes=[b_idr])
        q.dma("gpsimd", lambda e: e.dma_start(
            out=g1rep, in_=ln1_g.unsqueeze(0).broadcast_to([128, D])),
            writes=[b_g1])
        q.dma("gpsimd", lambda e: e.dma_start(
            out=b1rep, in_=ln1_b.unsqueeze(0).broadcast_to([128, D])),
            writes=[b_b1])
        q.dma("sync", lambda e: e.dma_start(
            out=blm[:, 0:1], in_=blendm.unsqueeze(0).broadcast_to([128, 1])),
            writes=[b_blm])
        q.op("vector", lambda e: e.tensor_scalar(
            out=blm[:, 1:2], in0=blm[:, 0:1], scalar1=-1.0, scalar2=1.0,
            op0=OP.mult, op1=OP.add), reads=[b_blm], writes=[b_blm])
        colload(rms_w, rmsw_sb, b_rmsw, NCH)
        colload(conv_b, cb_sb, b_cb, NCH)
        colload(dtb_h, dtb_sb, b_dtb, MCH)
        colload(dp_h, dp_sb, b_dp, MCH)
        for c in range(MCH):
            q.dma("sync", lambda e, c=c: e.dma_start(
                out=stage[:, 0:NST], in_=alog_h[c * 128:(c + 1) * 128, :]),
                writes=[b_x0p])
            q.op("scalar", lambda e, c=c: e.activation(
                out=sp_t[:, 0:NST], in_=stage[:, 0:NST],
                func=AF.Exp), reads=[b_x0p], writes=[b_spt])
            q.op("vector", lambda e, c=c: e.tensor_single_scalar(
                out=a_sb[:, c * NST:(c + 1) * NST], in_=sp_t[:, 0:NST],
                scalar=-1.0, op=OP.mult),
                reads=[b_spt], writes=[b_a])
        for k in range(NCH):
            q.dma("sync", lambda e, k=k: e.dma_start(
                out=cw_sb[k], in_=conv_w[k * 128:(k + 1) * 128, :]),
                writes=[b_cw[k]])
            q.dma("sync", lambda e, k=k: e.dma_start(
                out=stage, in_=inpw_x[k * 128:(k + 1) * 128, :]),
                writes=[b_x0p])
            q.op("vector", lambda e, k=k: e.tensor_scalar(
                out=w_inp[k][:, 0:E], in0=stage,
                scalar1=rmsw_sb[:, k:k + 1], scalar2=None, op0=OP.mult),
                reads=[b_x0p], sreads=[b_rmsw], writes=[b_winp[k]])
            q.dma("sync", lambda e, k=k: e.dma_start(
                out=stage[:, 0:EH], in_=inpw_z[k * 128:(k + 1) * 128, :]),
                writes=[b_x0p])
            q.op("vector", lambda e, k=k: e.tensor_scalar(
                out=w_inp[k][:, E:E + EH], in0=stage[:, 0:EH],
                scalar1=rmsw_sb[:, k:k + 1], scalar2=None, op0=OP.mult),
                reads=[b_x0p], sreads=[b_rmsw], writes=[b_winp[k]])
            q.dma("sync", lambda e, k=k: e.dma_start(
                out=w_xp[k], in_=xprojw[k * 128:(k + 1) * 128, :]),
                writes=[b_wxp[k]])
            q.dma("sync", lambda e, k=k: e.dma_start(
                out=stage, in_=opw[k * 128:(k + 1) * 128, :]),
                writes=[b_x0p])
            q.op("scalar", lambda e, k=k: e.activation(
                out=w_op[k], in_=stage, func=AF.Copy),
                reads=[b_x0p], writes=[b_wop[k]])
        for c in range(MCH):
            q.dma("sync", lambda e, c=c: e.dma_start(
                out=stage[0:64, 0:128], in_=dtw_h[:, c * 128:(c + 1) * 128]),
                writes=[b_x0p])
            q.op("scalar", lambda e, c=c: e.activation(
                out=w_dt[c], in_=stage[0:64, 0:128], func=AF.Copy),
                reads=[b_x0p], writes=[b_wdt[c]])
        for k in range(NCH):
            q.op("vector", lambda e, k=k: e.memset(xi[k][:, 0:3], 0.0),
                 writes=[b_xi[k]])
        q.op("vector", lambda e: e.memset(state, 0.0), writes=[b_state])

        def emit_prep(s):
            xp = s % 2
            for tt in range(4):
                tok0 = s * ST + tt * 128
                q.dma("sync", lambda e, tok0=tok0, xp=xp: e.dma_start(
                    out=x0t[xp], in_=xb[tok0:tok0 + 128, :]),
                    writes=[b_x0t[xp]], stream="x0")
                lnstats(x0t[xp], b_x0t[xp], stat, b_stat, mvt, b_mvt,
                        smallt[:, 0:1], smallt[:, 1:2],
                        [smallt[:, 4:5], smallt[:, 5:6], smallt[:, 6:7],
                         smallt[:, 7:8]], b_small)
                q.op("vector", lambda e, xp=xp: e.tensor_scalar(
                    out=aint, in0=x0t[xp], scalar1=smallt[:, 0:1],
                    scalar2=smallt[:, 1:2], op0=OP.subtract, op1=OP.mult),
                    reads=[b_x0t[xp]], sreads=[b_small], writes=[b_aint])
                lnstats(aint, b_aint, stat, b_stat, mvt, b_mvt,
                        smallt[:, 2:3], smallt[:, 3:4],
                        [smallt[:, 4:5], smallt[:, 5:6], smallt[:, 6:7],
                         smallt[:, 7:8]], b_small)
                q.op("vector", lambda e: e.tensor_scalar(
                    out=ln1t, in0=aint, scalar1=smallt[:, 2:3],
                    scalar2=smallt[:, 3:4], op0=OP.subtract, op1=OP.mult),
                    reads=[b_aint], sreads=[b_small], writes=[b_ln1t])
                q.op("vector", lambda e: e.tensor_mul(
                    out=rmst, in0=ln1t, in1=g1rep),
                    reads=[b_ln1t, b_g1], writes=[b_rmst])
                q.op("vector", lambda e: e.tensor_add(
                    out=ln1t, in0=rmst, in1=b1rep),
                    reads=[b_rmst, b_b1], writes=[b_ln1t])
                for i in range(2):
                    q.op("vector", lambda e, i=i: e.bn_stats(
                        out=stat[:, 6 * i:6 * i + 6],
                        in_=ln1t[:, 512 * i:512 * (i + 1)]),
                        reads=[b_ln1t], writes=[b_stat])
                q.op("vector", lambda e: e.bn_aggr(out=mvt[:, 0:2],
                                                   in_=stat[:, 0:6]),
                     reads=[b_stat], writes=[b_mvt])
                q.op("vector", lambda e: e.bn_aggr(out=mvt[:, 2:4],
                                                   in_=stat[:, 6:12]),
                     reads=[b_stat], writes=[b_mvt])
                q.op("vector", lambda e: e.tensor_scalar(
                    out=mvt[:, 4:5], in0=mvt[:, 0:1], scalar1=mvt[:, 2:3],
                    scalar2=0.5, op0=OP.add, op1=OP.mult),
                    reads=[b_mvt], sreads=[b_mvt], writes=[b_mvt])
                q.op("vector", lambda e: e.tensor_scalar(
                    out=smallt[:, 4:5], in0=mvt[:, 0:1], scalar1=mvt[:, 2:3],
                    scalar2=None, op0=OP.subtract),
                    reads=[b_mvt], sreads=[b_mvt], writes=[b_small])
                q.op("scalar", lambda e: e.activation(
                    out=smallt[:, 5:6], in_=smallt[:, 4:5], func=AF.Square,
                    scale=0.5), reads=[b_small], writes=[b_small])
                q.op("vector", lambda e: e.tensor_scalar(
                    out=smallt[:, 6:7], in0=mvt[:, 1:2], scalar1=mvt[:, 3:4],
                    scalar2=0.5, op0=OP.add, op1=OP.mult),
                    reads=[b_mvt], sreads=[b_mvt], writes=[b_small])
                q.op("vector", lambda e: e.tensor_scalar(
                    out=smallt[:, 6:7], in0=smallt[:, 6:7],
                    scalar1=smallt[:, 5:6], scalar2=None, op0=OP.add),
                    reads=[b_small], sreads=[b_small], writes=[b_small])
                q.op("scalar", lambda e: e.activation(
                    out=smallt[:, 4:5], in_=mvt[:, 4:5], func=AF.Square),
                    reads=[b_mvt], writes=[b_small])
                q.op("vector", lambda e: e.tensor_scalar(
                    out=smallt[:, 6:7], in0=smallt[:, 6:7],
                    scalar1=smallt[:, 4:5], scalar2=None, op0=OP.add),
                    reads=[b_small], sreads=[b_small], writes=[b_small])
                q.op("scalar", lambda e: e.activation(
                    out=smallt[:, 7:8], in_=smallt[:, 6:7], func=AF.Sqrt,
                    bias=eps_sb),
                    reads=[b_small], sreads=[b_eps], writes=[b_small])
                q.op("vector", lambda e: e.reciprocal(
                    out=mvt[:, 5:6], in_=smallt[:, 7:8]),
                    reads=[b_small], writes=[b_mvt])
                q.op("scalar", lambda e: e.activation(
                    out=rmst, in_=ln1t, func=AF.Copy, scale=mvt[:, 5:6]),
                    reads=[b_ln1t], sreads=[b_mvt], writes=[b_rmst])
                for k in range(NCH):
                    q.op("tensor", lambda e, k=k: e.transpose(
                        pt[:, 0:128], in_=rmst[:, k * 128:(k + 1) * 128],
                        identity=ident),
                        reads=[b_rmst, b_id], writes=[b_pt])
                    q.op("scalar", lambda e, k=k, tt=tt: e.activation(
                        out=rms_fm[k][:, tt * 128:(tt + 1) * 128],
                        in_=pt[:, 0:128], func=AF.Copy),
                        reads=[b_pt], writes=[b_rmsfm[k]])
            for dc in range(12):
                pmx = dc % 2
                for k in range(NCH):
                    q.op("tensor", lambda e, dc=dc, k=k, pmx=pmx: e.matmul(
                        pm[pmx], lhsT=w_inp[k][:, dc * 128:(dc + 1) * 128],
                        rhs=rms_fm[k], start=(k == 0), stop=(k == NCH - 1)),
                        reads=[b_winp[k], b_rmsfm[k]], writes=[b_pm[pmx]])
                if dc < 8:
                    if s > 0:
                        q.op("vector", lambda e, dc=dc: e.tensor_copy(
                            out=xi[dc][:, 0:3], in_=xi[dc][:, ST:ST + 3]),
                            reads=[b_xi[dc]], writes=[b_xi[dc]])
                    q.op("scalar", lambda e, dc=dc, pmx=pmx: e.activation(
                        out=xi[dc][:, 3:ST + 3], in_=pm[pmx], func=AF.Copy),
                        reads=[b_pm[pmx]], writes=[b_xi[dc]])
                else:
                    c = dc - 8
                    q.op("scalar", lambda e, c=c, pmx=pmx, xp=xp: e.activation(
                        out=sz[xp][c], in_=pm[pmx], func=AF.Silu),
                        reads=[b_pm[pmx]], writes=[b_sz[xp][c]])
            for k in range(NCH):
                q.op("vector", lambda e, k=k: e.tensor_scalar(
                    out=cv, in0=xi[k][:, 0:ST], scalar1=cw_sb[k][:, 0:1],
                    scalar2=None, op0=OP.mult),
                    reads=[b_xi[k]], sreads=[b_cw[k]], writes=[b_cv])
                for j in (1, 2, 3):
                    q.op("vector", lambda e, k=k, j=j: e.scalar_tensor_tensor(
                        out=cv, in0=xi[k][:, j:ST + j],
                        scalar=cw_sb[k][:, j:j + 1], in1=cv,
                        op0=OP.mult, op1=OP.add),
                        reads=[b_xi[k], b_cv], sreads=[b_cw[k]], writes=[b_cv])
                if k < MCH:
                    dst, dstb = xcm[xp][k], b_xcm[xp][k]
                else:
                    dst, dstb = xcp[k - MCH], b_xcp[k - MCH]
                q.op("scalar", lambda e, k=k, dst=dst: e.activation(
                    out=dst, in_=cv, func=AF.Silu, bias=cb_sb[:, k:k + 1]),
                    reads=[b_cv], sreads=[b_cb], writes=[dstb])
            for k in range(NCH):
                src = (xcm[xp][k], b_xcm[xp][k]) if k < MCH else \
                      (xcp[k - MCH], b_xcp[k - MCH])
                q.op("tensor", lambda e, k=k, src=src: e.matmul(
                    pd[0:96, :], lhsT=w_xp[k], rhs=src[0],
                    start=(k == 0), stop=(k == NCH - 1)),
                    reads=[b_wxp[k], src[1]], writes=[b_pd])
            q.op("scalar", lambda e: e.activation(
                out=dt_r, in_=pd[0:64, :], func=AF.Copy),
                reads=[b_pd], writes=[b_dtr])
            q.op("scalar", lambda e: e.activation(
                out=bc_sb, in_=pd[64:96, :], func=AF.Copy),
                reads=[b_pd], writes=[b_bcsb])
            q.dma("sync", lambda e, s=s: e.dma_start(
                out=bc_dram[:, s * ST:(s + 1) * ST], in_=bc_sb),
                reads=[b_bcsb], writes=[b_bcdram], stream="bc")
            for c in range(MCH):
                pmx = c % 2
                q.op("tensor", lambda e, c=c, pmx=pmx: e.matmul(
                    pm[pmx], lhsT=w_dt[c], rhs=dt_r, start=True, stop=True),
                    reads=[b_wdt[c], b_dtr], writes=[b_pm[pmx]])
                q.op("scalar", lambda e, c=c, pmx=pmx: e.activation(
                    out=sp_t, in_=pm[pmx], func=AF.Relu,
                    bias=dtb_sb[:, c:c + 1]),
                    reads=[b_pm[pmx]], sreads=[b_dtb], writes=[b_spt])
                q.op("scalar", lambda e, c=c, pmx=pmx: e.activation(
                    out=sp_t2, in_=pm[pmx], func=AF.Abs,
                    bias=dtb_sb[:, c:c + 1]),
                    reads=[b_pm[pmx]], sreads=[b_dtb], writes=[b_spt2])
                q.op("scalar", lambda e: e.activation(
                    out=sp_t2, in_=sp_t2, func=AF.Exp, scale=-1.0),
                    sreads=[b_spt2], writes=[b_spt2])
                q.op("scalar", lambda e: e.activation(
                    out=sp_t2, in_=sp_t2, func=AF.Ln, bias=1.0),
                    sreads=[b_spt2], writes=[b_spt2])
                q.op("vector", lambda e, c=c, xp=xp: e.tensor_add(
                    out=delta[xp][c], in0=sp_t, in1=sp_t2),
                    reads=[b_spt, b_spt2], writes=[b_delta[xp][c]])

        def emit_scan(s):
            xp = s % 2
            for c in range(MCH):
                q.op("vector", lambda e, c=c, xp=xp: e.tensor_mul(
                    out=u_scr, in0=delta[xp][c], in1=xcm[xp][c]),
                    reads=[b_delta[xp][c], b_xcm[xp][c]], writes=[b_uscr])
                for n in range(NST):
                    idx = c * NST + n
                    p2 = n % 2
                    q.op("scalar", lambda e, c=c, n=n, idx=idx, p2=p2, xp=xp:
                         e.activation(out=da[p2], in_=delta[xp][c],
                                      func=AF.Exp,
                                      scale=a_sb[:, idx:idx + 1]),
                         reads=[b_delta[xp][c]], sreads=[b_a],
                         writes=[b_da[p2]])
                    q.dma("scalar", lambda e, n=n, p2=p2, s=s: e.dma_start(
                        out=bbt[p2],
                        in_=bc_dram[n, s * ST:(s + 1) * ST]
                        .unsqueeze(0).broadcast_to([128, ST])),
                        reads=[b_bcdram], writes=[b_bbt[p2]], stream="bb")
                    q.op("gpsimd", lambda e, p2=p2: e.tensor_mul(
                        out=bxt[p2], in0=u_scr, in1=bbt[p2]),
                        reads=[b_uscr, b_bbt[p2]], writes=[b_bx[p2]])
                    q.op("vector", lambda e, p2=p2, idx=idx: e.tensor_tensor_scan(
                        out=hsc[p2], data0=da[p2], data1=bxt[p2],
                        initial=state[:, idx:idx + 1],
                        op0=OP.mult, op1=OP.add),
                        reads=[b_da[p2], b_bx[p2]], sreads=[b_state],
                        writes=[b_h[p2]])
                    q.op("vector", lambda e, p2=p2, idx=idx: e.tensor_scalar(
                        out=state[:, idx:idx + 1], in0=hsc[p2][:, ST - 1:ST],
                        scalar1=1.0, scalar2=None, op0=OP.mult),
                        sreads=[b_h[p2]], writes=[b_state])
                    q.dma("scalar", lambda e, n=n, p2=p2, s=s: e.dma_start(
                        out=cbt[p2],
                        in_=bc_dram[16 + n, s * ST:(s + 1) * ST]
                        .unsqueeze(0).broadcast_to([128, ST])),
                        reads=[b_bcdram], writes=[b_cbt[p2]], stream="bb")
                    peng = "vector" if n % 2 == 0 else "gpsimd"
                    q.op(peng, lambda e, p2=p2: e.tensor_mul(
                        out=prod[p2], in0=hsc[p2], in1=cbt[p2]),
                        reads=[b_cbt[p2]], sreads=[b_h[p2]],
                        writes=[b_prod[p2]])
                    q.op("tensor", lambda e, p2=p2, n=n: e.matmul(
                        pys, lhsT=identr, rhs=prod[p2],
                        start=(n == 0), stop=(n == NST - 1)),
                        reads=[b_idr, b_prod[p2]], writes=[b_pys])
                q.op("vector", lambda e, c=c, xp=xp: e.scalar_tensor_tensor(
                    out=t1, in0=xcm[xp][c], scalar=dp_sb[:, c:c + 1], in1=pys,
                    op0=OP.mult, op1=OP.add),
                    reads=[b_xcm[xp][c], b_pys], sreads=[b_dp],
                    writes=[b_t1])
                q.op("vector", lambda e, c=c, xp=xp: e.tensor_mul(
                    out=y3t, in0=t1, in1=sz[xp][c]),
                    reads=[b_t1, b_sz[xp][c]], writes=[b_y3t])
                q.dma("sync", lambda e, c=c, s=s: e.dma_start(
                    out=agy_in[s // 2, c, :, (s % 2) * ST:(s % 2) * ST + ST],
                    in_=y3t),
                    reads=[b_y3t], writes=[b_agyin], stream="y3")

        coll_ops = []

        def emit_ag(g):
            o = q.op("gpsimd", lambda e, g=g: e.collective_compute(
                "AllGather", OP.bypass,
                replica_groups=[[0, 1], [2, 3], [4, 5], [6, 7]],
                ins=[agy_in[g]],
                outs=[agy_out[g]]),
                reads=[b_agyin], writes=[b_agyout])
            coll_ops.append(o)

        def emit_post(ps):
            agdep = coll_ops[ps]
            for tt in range(4):
                q.dma("sync", lambda e, ps=ps, tt=tt: e.dma_start(
                    out=ye.rearrange("p (k f) -> p k f", f=128),
                    in_=agy_out[ps, :, :, :, tt * 128:tt * 128 + 128]
                    .rearrange("s c p f -> p (s c) f")),
                    deps=[agdep], writes=[b_ye], stream="yeo")
                q.dma("sync", lambda e, ps=ps, tt=tt: e.dma_start(
                    out=yo.rearrange("p (k f) -> p k f", f=128),
                    in_=agy_out[ps, :, :, :,
                                512 + tt * 128:640 + tt * 128]
                    .rearrange("s c p f -> p (s c) f")),
                    deps=[agdep], writes=[b_yo], stream="yeo")
                q.op("vector", lambda e: e.tensor_scalar(
                    out=yb, in0=yo, scalar1=blm[:, 0:1], scalar2=None,
                    op0=OP.mult), reads=[b_yo], sreads=[b_blm],
                    writes=[b_yb])
                q.op("vector", lambda e: e.scalar_tensor_tensor(
                    out=yb, in0=ye, scalar=blm[:, 1:2], in1=yb,
                    op0=OP.mult, op1=OP.add),
                    reads=[b_ye, b_yb], sreads=[b_blm], writes=[b_yb])
                for hh in range(2):
                    for k in range(NCH):
                        q.op("tensor", lambda e, k=k, hh=hh: e.matmul(
                            pop[:, hh * 512:(hh + 1) * 512],
                            lhsT=yb[:, k * 128:(k + 1) * 128],
                            rhs=w_op[k][:, hh * 512:(hh + 1) * 512],
                            start=(k == 0), stop=(k == NCH - 1)),
                            reads=[b_yb, b_wop[k]], writes=[b_pop])
                r0 = ps * ST + tt * 128
                q.dma("sync", lambda e, r0=r0: e.dma_start(
                    out=x0p, in_=x0h[r0:r0 + 128, :]), writes=[b_x0p],
                    stream="x0p")
                lnstats(x0p, b_x0p, stat2, b_stat2, mvt2, b_mvt2,
                        small2[:, 0:1], small2[:, 1:2],
                        [small2[:, 4:5], small2[:, 5:6], small2[:, 6:7],
                         small2[:, 7:8]], b_small2)
                q.op("vector", lambda e: e.tensor_scalar(
                    out=pa, in0=x0p, scalar1=small2[:, 0:1],
                    scalar2=small2[:, 1:2], op0=OP.subtract, op1=OP.mult),
                    reads=[b_x0p], sreads=[b_small2], writes=[b_pa])
                lnstats(pa, b_pa, stat2, b_stat2, mvt2, b_mvt2,
                        small2[:, 2:3], small2[:, 3:4],
                        [small2[:, 4:5], small2[:, 5:6], small2[:, 6:7],
                         small2[:, 7:8]], b_small2)
                q.op("vector", lambda e: e.tensor_scalar(
                    out=pb, in0=pa, scalar1=small2[:, 2:3],
                    scalar2=small2[:, 3:4], op0=OP.subtract, op1=OP.mult),
                    reads=[b_pa], sreads=[b_small2], writes=[b_pb])
                q.op("gpsimd", lambda e: e.tensor_mul(
                    out=pc, in0=pb, in1=g1rep),
                    reads=[b_pb, b_g1], writes=[b_pc])
                q.op("gpsimd", lambda e: e.tensor_add(
                    out=pb, in0=pc, in1=b1rep),
                    reads=[b_pc, b_b1], writes=[b_pb])
                q.op("gpsimd", lambda e: e.tensor_add(
                    out=pc, in0=pa, in1=pb),
                    reads=[b_pa, b_pb], writes=[b_pc])
                q.op("vector", lambda e: e.tensor_add(
                    out=xar, in0=pc, in1=pop),
                    reads=[b_pc, b_pop], writes=[b_x0p])
                q.dma("sync", lambda e, r0=r0: e.dma_start(
                    out=xa_out[r0:r0 + 128, :], in_=xar),
                    reads=[b_x0p], stream="xaout")

        def emit_debug0():
            q.dma("sync", lambda e: e.dma_start(out=dbg[0], in_=aint),
                  reads=[b_aint])
            q.dma("sync", lambda e: e.dma_start(out=dbg[1], in_=ln1t),
                  reads=[b_ln1t])
            q.dma("sync", lambda e: e.dma_start(out=dbg[2], in_=rmst),
                  reads=[b_rmst])
            q.dma("gpsimd", lambda e: e.dma_start(
                out=dbg[3][:, 0:515], in_=xi[0]), reads=[b_xi[0]])
            q.dma("sync", lambda e: e.dma_start(
                out=dbg[4][:, 0:512], in_=xcm[0][0]), reads=[b_xcm[0][0]])
            q.dma("gpsimd", lambda e: e.dma_start(
                out=dbg[5][:, 0:512], in_=xcp[0]), reads=[b_xcp[0]])
            q.dma("sync", lambda e: e.dma_start(
                out=dbg[6][:, 0:512], in_=delta[0][0]), reads=[b_delta[0][0]])
            q.dma("gpsimd", lambda e: e.dma_start(
                out=dbg[7][:, 0:512], in_=sz[0][0]), reads=[b_sz[0][0]])
            q.dma("sync", lambda e: e.dma_start(
                out=dbg[8][0:32, 0:512], in_=bc_sb), reads=[b_bcsb])
            q.dma("sync", lambda e: e.dma_start(
                out=dbg[8][:, 600:606], in_=mvt), reads=[b_mvt])
            q.dma("sync", lambda e: e.dma_start(
                out=dbg[8][:, 610:618], in_=smallt), reads=[b_small])

        def emit_debug1():
            q.dma("sync", lambda e: e.dma_start(
                out=dbg[9][:, 0:64], in_=state), reads=[b_state])
            q.dma("gpsimd", lambda e: e.dma_start(
                out=dbg[10][:, 0:512], in_=y3t), reads=[b_y3t])
            q.dma("sync", lambda e: e.dma_start(
                out=dbg[11][:, 0:512], in_=t1), reads=[b_t1])
            q.dma("sync", lambda e: e.dma_start(
                out=dbg[12][:, 0:512], in_=hsc[1]), reads=[b_h[1]])
            q.dma("sync", lambda e: e.dma_start(
                out=dbg[13][:, 0:512], in_=da[1]), reads=[b_da[1]])
            q.dma("sync", lambda e: e.dma_start(
                out=dbg[14][:, 0:512], in_=bxt[1]), reads=[b_bx[1]])
            q.dma("sync", lambda e: e.dma_start(
                out=dbg[15][:, 0:64], in_=a_sb), reads=[b_a])

        for s in range(NSLAB):
            emit_prep(s)
            if debug and s == 0:
                emit_debug0()
            if s > 0:
                emit_scan(s - 1)
                if debug and s == 1:
                    emit_debug1()
                if s % 2 == 0:
                    emit_ag(s // 2 - 1)
                    emit_post(s // 2 - 1)
        emit_scan(NSLAB - 1)
        emit_ag(NSLAB // 2 - 1)
        emit_post(NSLAB // 2 - 1)



def build_p1(xa_kind="ExternalOutput", debug=False, serial=False,
             noskip=False):
    nc = bass.Bass("TRN2", target_bir_lowering=False)
    q = Prog(nc)
    X = p1_drams(nc, xa_kind, debug)
    emit_p1(nc, q, X, debug=debug)
    with ExitStack() as fst:
        q.finalize(fst, serial=serial, noskip=noskip)
    return nc


def stage_p1_inputs(ins, core):
    b, ch = core >> 1, core & 1
    x = np.asarray(ins["L0_attn_x"])
    mine = slice(ch * EH, (ch + 1) * EH)
    other = slice((1 - ch) * EH, (2 - ch) * EH)
    perm = np.r_[np.arange(ch * EH, (ch + 1) * EH),
                 np.arange((1 - ch) * EH, (2 - ch) * EH)]
    inpw = np.asarray(ins["L0_attn_in_proj_w"])
    d = {
        "xb": np.ascontiguousarray(x[b]),
        "inpw_x": np.ascontiguousarray(inpw[:, perm]),
        "inpw_z": np.ascontiguousarray(inpw[:, 1024 + ch * EH:1024 + (ch + 1) * EH]),
        "ln1_g": np.ascontiguousarray(ins["L0_attn_ln1_g"]),
        "ln1_b": np.ascontiguousarray(ins["L0_attn_ln1_b"]),
        "rms_w": np.ascontiguousarray(ins["L0_attn_rms_w"]),
        "conv_w": np.ascontiguousarray(np.asarray(ins["L0_attn_conv_w"])[perm]),
        "conv_b": np.ascontiguousarray(np.asarray(ins["L0_attn_conv_b"])[perm]),
        "xprojw": np.ascontiguousarray(np.asarray(ins["L0_attn_x_proj_w"])[perm]),
        "dtw_h": np.ascontiguousarray(np.asarray(ins["L0_attn_dt_proj_w"])[:, mine]),
        "dtb_h": np.ascontiguousarray(np.asarray(ins["L0_attn_dt_proj_b"])[mine]),
        "alog_h": np.ascontiguousarray(np.asarray(ins["L0_attn_A_log"])[mine]),
        "dp_h": np.ascontiguousarray(np.asarray(ins["L0_attn_D_param"])[mine]),
        "opw": np.ascontiguousarray(ins["L0_attn_out_proj_w"]),
        "x0h": np.ascontiguousarray(
            x[b].reshape(NSLAB, ST, D)[ch::2].reshape(T // 2, D)),
        "blendm": np.array([float(ch)], np.float32),
    }
    return {k: np.asarray(v, dtype=np.float32) for k, v in d.items()}




def p2_drams(nc, xa_kind="ExternalInput", x1_kind="ExternalOutput",
             x0h=None, xa_in=None):
    dram = lambda n, s, dt=F32, k="ExternalInput", **kw: nc.dram_tensor(
        n, s, dt, kind=k, **kw).ap()
    X = {}
    X["xa_in"] = xa_in if xa_in is not None else \
        dram("xa_in", [T // 2, D], k=xa_kind)
    X["x0h"] = x0h if x0h is not None else dram("x0h", [T // 2, D])
    X["w1d"] = dram("w1d", [D, 4 * D])
    X["b1d"] = dram("b1d", [4 * D])
    X["w2d"] = dram("w2d", [4 * D, D])
    X["b2d"] = dram("b2d", [D])
    X["g2d"] = dram("g2d", [D])
    X["bt2d"] = dram("bt2d", [D])
    X["x1_out"] = dram("x1_out", [T // 2, D], k=x1_kind)
    return X


def emit_p2(nc, q, X):
    (xa_in, x0h, w1d, b1d, w2d, b2d, g2d, bt2d, x1_out) = (X[k] for k in (
        "xa_in", "x0h", "w1d", "b1d", "w2d", "b2d", "g2d", "bt2d", "x1_out"))
    ST2 = 256
    NS2 = (T // 2) // ST2

    st = ExitStack()
    sb = lambda name, shape, dt=F32: st.enter_context(
        nc.sbuf_tensor("B_" + name, shape, dt)).ap()
    mkpsum = lambda name, shape: st.enter_context(
        nc.psum_tensor("B_" + name, shape, F32)).ap()
    B = Buf
    with st:
        w1 = [sb(f"w1_{k}", [128, 4 * D], BF16) for k in range(NCH)]
        w2 = [sb(f"w2_{k}", [128, D], BF16) for k in range(32)]
        b1c = sb("b1c", [128, 32])
        b2row = sb("b2row", [1, D], BF16)
        onesb = sb("onesb", [1, 128], BF16)
        g2c = sb("g2c", [128, NCH])
        bt2c = sb("bt2c", [128, NCH])
        identb = sb("identb", [128, 128], BF16)
        ident = sb("ident", [128, 128])
        eps_sb = sb("eps_sb", [128, 1])
        row_t = sb("row_t", [1, 1024])
        stage = sb("stage", [128, 2048])
        b1row = sb("b1row", [1, 4 * D])
        xat = sb("xat", [128, D])
        x0t = sb("x0t", [128, D])
        ln2t = sb("ln2t", [128, D])
        ln2fm = [sb(f"ln2fm{k}", [128, ST2], BF16) for k in range(NCH)]
        g1 = [sb(f"g1_{dc}", [128, ST2], BF16) for dc in range(32)]
        x1t = sb("x1t", [128, D])
        stat = sb("stat", [128, 12])
        mvt = sb("mvt", [128, 6])
        smallt = sb("smallt", [128, 8])
        pm = [mkpsum(f"pm{i}", [128, 512]) for i in range(2)]
        pt = mkpsum("pt", [128, 512])
        pop = mkpsum("pop", [128, D])
        pb1 = mkpsum("pb1", [128, 512])

        b_w1 = [B() for _ in range(NCH)]
        b_w2 = [B() for _ in range(32)]
        b_b1c = B(); b_b2r = B(); b_ones = B(); b_g2 = B(); b_bt2 = B()
        b_idb = B(); b_id = B(); b_eps = B(); b_row = B(); b_stage = B()
        b_b1row = B()
        b_xat = B(); b_x0t = B(); b_ln2t = B()
        b_ln2fm = [B() for _ in range(NCH)]
        b_g1 = [B() for _ in range(32)]
        b_x1t = B(); b_stat = B(); b_mvt = B(); b_small = B()
        b_pm = [B(), B()]; b_pt = B(); b_pop = B(); b_pb1 = B()

        def colload(vec, n, out_ap, out_buf):
            q.dma("sync", lambda e: e.dma_start(out=row_t[:, 0:n * 128],
                                                in_=vec[None, :]),
                  writes=[b_row], private=True)
            for c in range(n):
                q.op("tensor", lambda e, c=c: e.transpose(
                    pt[:, c:c + 1], in_=row_t[:, c * 128:(c + 1) * 128],
                    identity=ident[0:1, 0:1]),
                    reads=[b_row, b_id], writes=[b_pt])
            q.op("scalar", lambda e: e.activation(
                out=out_ap, in_=pt[:, 0:n], func=AF.Copy),
                reads=[b_pt], writes=[out_buf])

        def lnstats(xap, xbuf, mu_sl, r_sl):
            for i in range(2):
                q.op("vector", lambda e, i=i: e.bn_stats(
                    out=stat[:, 6 * i:6 * i + 6],
                    in_=xap[:, 512 * i:512 * (i + 1)]),
                    reads=[xbuf], writes=[b_stat])
            q.op("vector", lambda e: e.bn_aggr(out=mvt[:, 0:2],
                                               in_=stat[:, 0:6]),
                 reads=[b_stat], writes=[b_mvt])
            q.op("vector", lambda e: e.bn_aggr(out=mvt[:, 2:4],
                                               in_=stat[:, 6:12]),
                 reads=[b_stat], writes=[b_mvt])
            q.op("vector", lambda e: e.tensor_scalar(
                out=mu_sl, in0=mvt[:, 0:1], scalar1=mvt[:, 2:3],
                scalar2=0.5, op0=OP.add, op1=OP.mult),
                reads=[b_mvt], sreads=[b_mvt], writes=[b_small])
            q.op("vector", lambda e: e.tensor_scalar(
                out=smallt[:, 4:5], in0=mvt[:, 0:1], scalar1=mvt[:, 2:3],
                scalar2=None, op0=OP.subtract),
                reads=[b_mvt], sreads=[b_mvt], writes=[b_small])
            q.op("scalar", lambda e: e.activation(
                out=smallt[:, 5:6], in_=smallt[:, 4:5], func=AF.Square,
                scale=0.5), reads=[b_small], writes=[b_small])
            q.op("vector", lambda e: e.tensor_scalar(
                out=smallt[:, 6:7], in0=mvt[:, 1:2], scalar1=mvt[:, 3:4],
                scalar2=0.5, op0=OP.add, op1=OP.mult),
                reads=[b_mvt], sreads=[b_mvt], writes=[b_small])
            q.op("vector", lambda e: e.tensor_scalar(
                out=smallt[:, 6:7], in0=smallt[:, 6:7],
                scalar1=smallt[:, 5:6], scalar2=None, op0=OP.add),
                reads=[b_small], sreads=[b_small], writes=[b_small])
            q.op("scalar", lambda e: e.activation(
                out=smallt[:, 7:8], in_=smallt[:, 6:7], func=AF.Sqrt,
                bias=eps_sb),
                reads=[b_small], sreads=[b_eps], writes=[b_small])
            q.op("vector", lambda e: e.reciprocal(out=r_sl,
                                                  in_=smallt[:, 7:8]),
                 reads=[b_small], writes=[b_small])

        q.op("vector", lambda e: e.memset(eps_sb, EPS), writes=[b_eps])
        q.op("gpsimd", lambda e: e.memset(ident, 0.0), writes=[b_id])
        q.op("gpsimd", lambda e: e.affine_select(
            out=ident, in_=ident, compare_op=OP.not_equal, fill=1.0, base=0,
            pattern=[[-1, 128]], channel_multiplier=1),
            reads=[b_id], writes=[b_id])
        q.op("vector", lambda e: e.tensor_copy(out=identb, in_=ident),
             reads=[b_id], writes=[b_idb])
        q.op("vector", lambda e: e.memset(onesb, 1.0), writes=[b_ones])
        colload(g2d, NCH, g2c, b_g2)
        colload(bt2d, NCH, bt2c, b_bt2)
        q.dma("gpsimd", lambda e: e.dma_start(out=b2row, in_=b2d[None, :]),
              writes=[b_b2r], private=True)
        q.dma("sync", lambda e: e.dma_start(out=b1row, in_=b1d[None, :]),
              writes=[b_b1row], private=True)
        for k in range(NCH):
            for h in range(2):
                q.dma("sync", lambda e, k=k, h=h: e.dma_start(
                    out=stage, in_=w1d[k * 128:(k + 1) * 128,
                                       h * 2048:(h + 1) * 2048]),
                    writes=[b_stage])
                q.op("vector", lambda e, k=k, h=h: e.tensor_scalar(
                    out=w1[k][:, h * 2048:(h + 1) * 2048], in0=stage,
                    scalar1=g2c[:, k:k + 1], scalar2=None, op0=OP.mult),
                    reads=[b_stage], sreads=[b_g2], writes=[b_w1[k]])
        for g in range(8):
            for k in range(NCH):
                q.dma("sync", lambda e, k=k, g=g: e.dma_start(
                    out=stage[:, 0:512],
                    in_=w1d[k * 128:(k + 1) * 128, g * 512:(g + 1) * 512]),
                    writes=[b_stage])
                q.op("tensor", lambda e, k=k, g=g: e.matmul(
                    pb1[0:1, :], lhsT=bt2c[:, k:k + 1], rhs=stage[:, 0:512],
                    start=(k == 0), stop=(k == NCH - 1)),
                    reads=[b_stage, b_bt2], writes=[b_pb1])
            q.op("vector", lambda e, g=g: e.tensor_tensor(
                out=row_t[0:1, 0:512], in0=pb1[0:1, 0:512],
                in1=b1row[0:1, g * 512:(g + 1) * 512], op=OP.add),
                reads=[b_pb1, b_b1row], writes=[b_row])
            for c2 in range(4):
                q.op("tensor", lambda e, g=g, c2=c2: e.transpose(
                    pt[:, g * 4 + c2:g * 4 + c2 + 1],
                    in_=row_t[0:1, c2 * 128:(c2 + 1) * 128],
                    identity=ident[0:1, 0:1]),
                    reads=[b_row, b_id], writes=[b_pt])
        q.op("scalar", lambda e: e.activation(
            out=b1c, in_=pt[:, 0:32], func=AF.Copy),
            reads=[b_pt], writes=[b_b1c])
        for kc in range(32):
            q.dma("sync", lambda e, kc=kc: e.dma_start(
                out=stage[:, 0:1024], in_=w2d[kc * 128:(kc + 1) * 128, :]),
                writes=[b_stage])
            q.op("scalar", lambda e, kc=kc: e.activation(
                out=w2[kc], in_=stage[:, 0:1024], func=AF.Copy),
                reads=[b_stage], writes=[b_w2[kc]])

        for s2 in range(NS2):
            for tt in range(2):
                r0 = s2 * ST2 + tt * 128
                q.dma("sync", lambda e, r0=r0: e.dma_start(
                    out=xat, in_=xa_in[r0:r0 + 128, :]), writes=[b_xat],
                    stream="xain")
                lnstats(xat, b_xat, smallt[:, 0:1], smallt[:, 1:2])
                q.op("vector", lambda e: e.tensor_scalar(
                    out=ln2t, in0=xat, scalar1=smallt[:, 0:1],
                    scalar2=smallt[:, 1:2], op0=OP.subtract, op1=OP.mult),
                    reads=[b_xat], sreads=[b_small], writes=[b_ln2t])
                for k in range(NCH):
                    q.op("tensor", lambda e, k=k: e.transpose(
                        pt[:, 0:128], in_=ln2t[:, k * 128:(k + 1) * 128],
                        identity=ident),
                        reads=[b_ln2t, b_id], writes=[b_pt])
                    q.op("scalar", lambda e, k=k, tt=tt: e.activation(
                        out=ln2fm[k][:, tt * 128:(tt + 1) * 128],
                        in_=pt[:, 0:128], func=AF.Copy),
                        reads=[b_pt], writes=[b_ln2fm[k]])
            for dc in range(32):
                pmx = dc % 2
                for k in range(NCH):
                    q.op("tensor", lambda e, dc=dc, k=k, pmx=pmx: e.matmul(
                        pm[pmx][:, 0:ST2],
                        lhsT=w1[k][:, dc * 128:(dc + 1) * 128],
                        rhs=ln2fm[k], start=(k == 0), stop=(k == NCH - 1)),
                        reads=[b_w1[k], b_ln2fm[k]], writes=[b_pm[pmx]])
                q.op("scalar", lambda e, dc=dc, pmx=pmx: e.activation(
                    out=g1[dc], in_=pm[pmx][:, 0:ST2], func=AF.Relu,
                    bias=b1c[:, dc:dc + 1]),
                    reads=[b_pm[pmx]], sreads=[b_b1c], writes=[b_g1[dc]])
            for tt in range(2):
                r0 = s2 * ST2 + tt * 128
                for hh in range(2):
                    for kc in range(32):
                        q.op("tensor", lambda e, kc=kc, tt=tt, hh=hh: e.matmul(
                            pop[:, hh * 512:(hh + 1) * 512],
                            lhsT=g1[kc][:, tt * 128:(tt + 1) * 128],
                            rhs=w2[kc][:, hh * 512:(hh + 1) * 512],
                            start=(kc == 0), stop=False),
                            reads=[b_g1[kc], b_w2[kc]], writes=[b_pop])
                    q.op("tensor", lambda e, tt=tt, hh=hh: e.matmul(
                        pop[:, hh * 512:(hh + 1) * 512], lhsT=onesb,
                        rhs=b2row[0:1, hh * 512:(hh + 1) * 512],
                        start=False, stop=True),
                        reads=[b_ones, b_b2r], writes=[b_pop])
                q.dma("sync", lambda e, r0=r0: e.dma_start(
                    out=xat, in_=xa_in[r0:r0 + 128, :]), writes=[b_xat],
                    stream="xain")
                q.dma("sync", lambda e, r0=r0: e.dma_start(
                    out=x0t, in_=x0h[r0:r0 + 128, :]), writes=[b_x0t],
                    stream="x0p2")
                q.op("vector", lambda e: e.tensor_add(
                    out=x1t, in0=xat, in1=pop),
                    reads=[b_xat, b_pop], writes=[b_x1t])
                q.op("vector", lambda e: e.tensor_add(
                    out=x1t, in0=x1t, in1=x0t),
                    reads=[b_x1t, b_x0t], writes=[b_x1t])
                q.dma("sync", lambda e, r0=r0: e.dma_start(
                    out=x1_out[r0:r0 + 128, :], in_=x1t),
                    reads=[b_x1t], stream="x1out")


def build_p2(noskip=True):
    nc = bass.Bass("TRN2", target_bir_lowering=False)
    q = Prog(nc)
    X = p2_drams(nc)
    emit_p2(nc, q, X)
    with ExitStack() as fst:
        q.finalize(fst, noskip=noskip)
    return nc


def stage_p2_inputs(ins, core, xa=None):
    b, ch = core >> 1, core & 1
    x = np.asarray(ins["L0_attn_x"])
    d = {
        "x0h": np.ascontiguousarray(
            x[b].reshape(NSLAB, ST, D)[ch::2].reshape(T // 2, D)),
        "w1d": np.ascontiguousarray(ins["L0_attn_ffn_w1"]),
        "b1d": np.ascontiguousarray(ins["L0_attn_ffn_b1"]),
        "w2d": np.ascontiguousarray(ins["L0_attn_ffn_w2"]),
        "b2d": np.ascontiguousarray(ins["L0_attn_ffn_b2"]),
        "g2d": np.ascontiguousarray(ins["L0_attn_ln2_g"]),
        "bt2d": np.ascontiguousarray(ins["L0_attn_ln2_b"]),
    }
    if xa is not None:
        d["xa_in"] = np.ascontiguousarray(xa)
    return {k: np.asarray(v, dtype=np.float32) for k, v in d.items()}




IC3 = [(0, 512), (512, 512), (1024, 512), (1536, 512), (2048, 512),
       (2560, 256)]


def p3_drams(nc, x1_kind="ExternalInput", out_kind="ExternalOutput",
             x1_in=None):
    dram = lambda n, s, dt=F32, k="ExternalInput", **kw: nc.dram_tensor(
        n, s, dt, kind=k, **kw).ap()
    X = {}
    X["x1_in"] = x1_in if x1_in is not None else \
        dram("x1_in", [T // 2, D], k=x1_kind)
    X["wg_sl"] = dram("wg_sl", [3 * 128, D])
    X["wu_sl"] = dram("wu_sl", [3 * 128, D])
    X["wd_sl"] = dram("wd_sl", [128, 2816])
    X["out"] = dram("out", [T // 2, D], k=out_kind)
    X["agw1_in"] = dram("agw1_in", [6, 128, D], BF16, k="Internal")
    X["agw1_out"] = dram("agw1_out", [8, 6, 128, D], BF16, k="Internal",
                         addr_space="Local")
    X["agw2_in"] = dram("agw2_in", [128, 2816], BF16, k="Internal")
    X["agw2_out"] = dram("agw2_out", [8, 128, 2816], BF16, k="Internal",
                         addr_space="Local")
    return X


def emit_p3(nc, q, X):
    (x1_in, wg_sl, wu_sl, wd_sl, out_d, agw1_in, agw1_out, agw2_in,
     agw2_out) = (X[k] for k in (
        "x1_in", "wg_sl", "wu_sl", "wd_sl", "out", "agw1_in", "agw1_out",
        "agw2_in", "agw2_out"))
    ST3 = 128
    NS3 = (T // 2) // ST3

    st = ExitStack()
    sb = lambda name, shape, dt=F32: st.enter_context(
        nc.sbuf_tensor("C_" + name, shape, dt)).ap()
    mkpsum = lambda name, shape: st.enter_context(
        nc.psum_tensor("C_" + name, shape, F32)).ap()
    B = Buf
    with st:
        wgT = [sb(f"wgT{k}", [128, 2816], BF16) for k in range(NCH)]
        wuT = [sb(f"wuT{k}", [128, 2816], BF16) for k in range(NCH)]
        wdT = [sb(f"wdT{k}", [128, D], BF16) for k in range(22)]
        ident = sb("ident", [128, 128])
        identb = sb("identb", [128, 128], BF16)
        eps_sb = sb("eps_sb", [128, 1])
        stage = sb("stage", [128, 2816])
        stage_b = sb("stage_b", [128, 2816], BF16)
        qa = sb("qa", [128, 1024])
        qxn = sb("qxn", [128, 1024])
        qb2 = sb("qb2", [128, 1024])
        qc = sb("qc", [128, 1024])
        qp3 = sb("qp3", [128, 1024])
        qsg = sb("qsg", [128, 1024], I32)
        qam = sb("qam", [128, 64])
        qr6 = sb("qr6", [128, 64])
        qs = sb("qs", [128, 64])
        x1t = sb("x1t", [128, D])
        ft = sb("ft", [128, D])
        fq = sb("fq", [128, D])
        fqfm = [sb(f"fqfm{k}", [128, ST3], BF16) for k in range(NCH)]
        ht = sb("ht", [128, 2816])
        hq = stage
        sg_t = sb("sg_t", [128, 512])
        hqfm = [sb(f"hqfm{k}", [128, ST3], BF16) for k in range(22)]
        x2t = ft
        stat = sb("stat", [128, 12])
        mvt = sb("mvt", [128, 6])
        smallt = sb("smallt", [128, 8])
        pg = mkpsum("pg", [128, 512])
        pu = mkpsum("pu", [128, 512])
        pt = mkpsum("pt", [128, 512])
        pd2 = mkpsum("pd2", [128, D])

        b_wgT = [B() for _ in range(NCH)]
        b_wuT = [B() for _ in range(NCH)]
        b_wdT = [B() for _ in range(22)]
        b_id = B(); b_idb = B(); b_eps = B(); b_stage = B(); b_stageb = B()
        b_qa = B(); b_qxn = B(); b_qb2 = B(); b_qc = B(); b_qp3 = B()
        b_qsg = B(); b_qam = B(); b_qr6 = B(); b_qs = B()
        b_x1t = B(); b_ft = B(); b_fq = B()
        b_fqfm = [B() for _ in range(NCH)]
        b_ht = B(); b_sg = B()
        b_hq = b_stage
        b_hqfm = [B() for _ in range(22)]
        b_stat = B(); b_mvt = B(); b_small = B()
        b_x2t = b_ft
        b_pg = B(); b_pu = B(); b_pt = B(); b_pd2 = B()
        b_agw1i = B(); b_agw1o = B(); b_agw2i = B(); b_agw2o = B()

        def quant(x_ap, x_buf, out_ap, out_buf, nblk, gq="gpsimd"):
            scr = dict(a=(qa, b_qa), xn=(qxn, b_qxn), b2=(qb2, b_qb2),
                       c=(qc, b_qc), p3=(qp3, b_qp3), sg=(qsg, b_qsg),
                       am=(qam, b_qam), r6=(qr6, b_qr6), s=(qs, b_qs))
            n16 = nblk * 16
            q.op("vector", lambda e: e.tensor_scalar(
                out=qsg[:, 0:n16], in0=x_ap.bitcast(I32),
                scalar1=-0x80000000, scalar2=None, op0=OP.bitwise_and),
                reads=[x_buf], writes=[b_qsg])
            q.op("vector", lambda e: e.tensor_scalar(
                out=qa.bitcast(I32)[:, 0:n16], in0=x_ap.bitcast(I32),
                scalar1=0x7FFFFFFF, scalar2=None, op0=OP.bitwise_and),
                reads=[x_buf], writes=[b_qa])
            q.op("vector", lambda e: e.tensor_reduce(
                out=qam[:, 0:nblk],
                in_=qa[:, 0:n16].rearrange("p (b k) -> p b k", k=16),
                axis=AX.X, op=OP.max), reads=[b_qa], writes=[b_qam])
            q.op("vector", lambda e: e.reciprocal(
                out=qr6[:, 0:nblk], in_=qam[:, 0:nblk]),
                sreads=[b_qam], writes=[b_qr6])
            q.op("vector", lambda e: e.tensor_single_scalar(
                out=qr6[:, 0:nblk], in_=qr6[:, 0:nblk], scalar=6.0,
                op=OP.mult), sreads=[b_qr6], writes=[b_qr6])
            q.op("vector", lambda e: e.tensor_single_scalar(
                out=qs[:, 0:nblk], in_=qam[:, 0:nblk], scalar=1.0 / 6.0,
                op=OP.mult), sreads=[b_qam], writes=[b_qs])
            q.op(gq, lambda e: e.tensor_tensor(
                out=qxn[:, 0:n16].rearrange("p (b k) -> p b k", k=16),
                in0=qa[:, 0:n16].rearrange("p (b k) -> p b k", k=16),
                in1=qr6[:, 0:nblk].unsqueeze(2).broadcast_to([128, nblk, 16]),
                op=OP.mult), reads=[b_qa, b_qr6], writes=[b_qxn])
            q.op("vector", lambda e: e.tensor_scalar(
                out=qb2[:, 0:n16], in0=qxn[:, 0:n16], scalar1=2.0,
                scalar2=MAGIC, op0=OP.mult, op1=OP.add),
                reads=[b_qxn], writes=[b_qb2])
            q.op("vector", lambda e: e.tensor_scalar(
                out=qb2[:, 0:n16], in0=qb2[:, 0:n16], scalar1=MAGIC,
                scalar2=0.5, op0=OP.subtract, op1=OP.mult),
                reads=[b_qb2], writes=[b_qb2])
            q.op("vector", lambda e: e.tensor_single_scalar(
                out=qb2[:, 0:n16], in_=qb2[:, 0:n16], scalar=2.0, op=OP.min),
                reads=[b_qb2], writes=[b_qb2])
            q.op("vector", lambda e: e.tensor_scalar(
                out=qc[:, 0:n16], in0=qxn[:, 0:n16], scalar1=MAGIC,
                scalar2=MAGIC, op0=OP.add, op1=OP.subtract),
                reads=[b_qxn], writes=[b_qc])
            q.op("vector", lambda e: e.tensor_scalar(
                out=qc[:, 0:n16], in0=qc[:, 0:n16], scalar1=4.0, scalar2=2.0,
                op0=OP.min, op1=OP.subtract), reads=[b_qc], writes=[b_qc])
            q.op("vector", lambda e: e.scalar_tensor_tensor(
                out=qc[:, 0:n16], in0=qc[:, 0:n16], scalar=0.0,
                in1=qb2[:, 0:n16], op0=OP.max, op1=OP.add),
                reads=[b_qc, b_qb2], writes=[b_qc])
            q.op("vector", lambda e: e.tensor_scalar(
                out=qp3[:, 0:n16], in0=qxn[:, 0:n16], scalar1=5.0,
                scalar2=2.0, op0=OP.is_gt, op1=OP.mult),
                reads=[b_qxn], writes=[b_qp3])
            q.op(gq, lambda e: e.tensor_tensor(
                out=qc[:, 0:n16], in0=qc[:, 0:n16], in1=qp3[:, 0:n16],
                op=OP.add), reads=[b_qc, b_qp3], writes=[b_qc])
            q.op(gq, lambda e: e.tensor_tensor(
                out=qxn[:, 0:n16].rearrange("p (b k) -> p b k", k=16),
                in0=qc[:, 0:n16].rearrange("p (b k) -> p b k", k=16),
                in1=qs[:, 0:nblk].unsqueeze(2).broadcast_to([128, nblk, 16]),
                op=OP.mult), reads=[b_qc, b_qs], writes=[b_qxn])
            q.op("vector", lambda e: e.tensor_tensor(
                out=out_ap.bitcast(I32), in0=qxn.bitcast(I32)[:, 0:n16],
                in1=qsg[:, 0:n16], op=OP.bitwise_or),
                reads=[b_qxn, b_qsg], writes=[out_buf])

        def lnstats(xap, xbuf, mu_sl, r_sl):
            for i in range(2):
                q.op("vector", lambda e, i=i: e.bn_stats(
                    out=stat[:, 6 * i:6 * i + 6],
                    in_=xap[:, 512 * i:512 * (i + 1)]),
                    reads=[xbuf], writes=[b_stat])
            q.op("vector", lambda e: e.bn_aggr(out=mvt[:, 0:2],
                                               in_=stat[:, 0:6]),
                 reads=[b_stat], writes=[b_mvt])
            q.op("vector", lambda e: e.bn_aggr(out=mvt[:, 2:4],
                                               in_=stat[:, 6:12]),
                 reads=[b_stat], writes=[b_mvt])
            q.op("vector", lambda e: e.tensor_scalar(
                out=mu_sl, in0=mvt[:, 0:1], scalar1=mvt[:, 2:3],
                scalar2=0.5, op0=OP.add, op1=OP.mult),
                reads=[b_mvt], sreads=[b_mvt], writes=[b_small])
            q.op("vector", lambda e: e.tensor_scalar(
                out=smallt[:, 4:5], in0=mvt[:, 0:1], scalar1=mvt[:, 2:3],
                scalar2=None, op0=OP.subtract),
                reads=[b_mvt], sreads=[b_mvt], writes=[b_small])
            q.op("scalar", lambda e: e.activation(
                out=smallt[:, 5:6], in_=smallt[:, 4:5], func=AF.Square,
                scale=0.5), reads=[b_small], writes=[b_small])
            q.op("vector", lambda e: e.tensor_scalar(
                out=smallt[:, 6:7], in0=mvt[:, 1:2], scalar1=mvt[:, 3:4],
                scalar2=0.5, op0=OP.add, op1=OP.mult),
                reads=[b_mvt], sreads=[b_mvt], writes=[b_small])
            q.op("vector", lambda e: e.tensor_scalar(
                out=smallt[:, 6:7], in0=smallt[:, 6:7],
                scalar1=smallt[:, 5:6], scalar2=None, op0=OP.add),
                reads=[b_small], sreads=[b_small], writes=[b_small])
            q.op("scalar", lambda e: e.activation(
                out=smallt[:, 7:8], in_=smallt[:, 6:7], func=AF.Sqrt,
                bias=eps_sb),
                reads=[b_small], sreads=[b_eps], writes=[b_small])
            q.op("vector", lambda e: e.reciprocal(out=r_sl,
                                                  in_=smallt[:, 7:8]),
                 reads=[b_small], writes=[b_small])

        q.op("vector", lambda e: e.memset(eps_sb, EPS), writes=[b_eps])
        q.op("gpsimd", lambda e: e.memset(ident, 0.0), writes=[b_id])
        q.op("gpsimd", lambda e: e.affine_select(
            out=ident, in_=ident, compare_op=OP.not_equal, fill=1.0, base=0,
            pattern=[[-1, 128]], channel_multiplier=1),
            reads=[b_id], writes=[b_id])
        q.op("vector", lambda e: e.tensor_copy(out=identb, in_=ident),
             reads=[b_id], writes=[b_idb])
        for sl in range(3):
            for wsrc, slot in ((wg_sl, sl), (wu_sl, 3 + sl)):
                q.dma("sync", lambda e, wsrc=wsrc, sl=sl: e.dma_start(
                    out=stage[:, 0:D], in_=wsrc[sl * 128:(sl + 1) * 128, :]),
                    writes=[b_stage])
                quant(stage[:, 0:D], b_stage, qp3[:, 0:D], b_qp3, 64)
                q.op("scalar", lambda e: e.activation(
                    out=stage_b[:, 0:D], in_=qp3[:, 0:D], func=AF.Copy),
                    reads=[b_qp3], writes=[b_stageb])
                q.dma("sync", lambda e, slot=slot: e.dma_start(
                    out=agw1_in[slot], in_=stage_b[:, 0:D]),
                    reads=[b_stageb], writes=[b_agw1i])
        q.dma("sync", lambda e: e.dma_start(out=stage, in_=wd_sl),
              writes=[b_stage])
        for off, w in ((0, 1024), (1024, 1024), (2048, 768)):
            quant(stage[:, off:off + w], b_stage, qp3[:, 0:w], b_qp3, w // 16)
            q.op("scalar", lambda e, off=off, w=w: e.activation(
                out=stage_b[:, off:off + w], in_=qp3[:, 0:w], func=AF.Copy),
                reads=[b_qp3], writes=[b_stageb])
        q.dma("sync", lambda e: e.dma_start(out=agw2_in, in_=stage_b),
              reads=[b_stageb], writes=[b_agw2i])
        ag1 = q.op("gpsimd", lambda e: e.collective_compute(
            "AllGather", OP.bypass,
            replica_groups=[list(range(8))],
            ins=[agw1_in], outs=[agw1_out]),
            reads=[b_agw1i], writes=[b_agw1o])
        ag2 = q.op("gpsimd", lambda e: e.collective_compute(
            "AllGather", OP.bypass,
            replica_groups=[list(range(8))],
            ins=[agw2_in], outs=[agw2_out]),
            reads=[b_agw2i], writes=[b_agw2o])
        for j in range(22):
            rk, sl = j % 8, j // 8
            for which, wt, bt, slot in (("g", wgT, b_wgT, sl),
                                        ("u", wuT, b_wuT, 3 + sl)):
                q.dma("sync", lambda e, rk=rk, slot=slot: e.dma_start(
                    out=stage_b[:, 0:D], in_=agw1_out[rk, slot]),
                    deps=[ag1], reads=[b_agw1o], writes=[b_stageb])
                for k in range(NCH):
                    q.op("tensor", lambda e, k=k: e.transpose(
                        pt[:, 0:128].bitcast(BF16)[:, 0:128],
                        in_=stage_b[:, k * 128:(k + 1) * 128],
                        identity=identb),
                        reads=[b_stageb, b_idb], writes=[b_pt])
                    q.op("scalar", lambda e, k=k, j=j, wt=wt, bt=bt:
                         e.activation(
                             out=wt[k][:, j * 128:(j + 1) * 128],
                             in_=pt[:, 0:128].bitcast(BF16)[:, 0:128],
                             func=AF.Copy),
                         reads=[b_pt], writes=[bt[k]])
        for j in range(8):
            q.dma("sync", lambda e, j=j: e.dma_start(
                out=stage_b, in_=agw2_out[j]),
                deps=[ag2], reads=[b_agw2o], writes=[b_stageb])
            for kc in range(22):
                q.op("tensor", lambda e, kc=kc: e.transpose(
                    pt[:, 0:128].bitcast(BF16)[:, 0:128],
                    in_=stage_b[:, kc * 128:(kc + 1) * 128],
                    identity=identb),
                    reads=[b_stageb, b_idb], writes=[b_pt])
                q.op("scalar", lambda e, kc=kc, j=j: e.activation(
                    out=wdT[kc][:, j * 128:(j + 1) * 128],
                    in_=pt[:, 0:128].bitcast(BF16)[:, 0:128],
                    func=AF.Copy),
                    reads=[b_pt], writes=[b_wdT[kc]])

        for s3 in range(NS3):
            r0 = s3 * ST3
            q.dma("sync", lambda e, r0=r0: e.dma_start(
                out=x1t, in_=x1_in[r0:r0 + 128, :]), writes=[b_x1t],
                stream="x1in")
            lnstats(x1t, b_x1t, smallt[:, 0:1], smallt[:, 1:2])
            q.op("vector", lambda e: e.tensor_scalar(
                out=ft, in0=x1t, scalar1=smallt[:, 0:1],
                scalar2=smallt[:, 1:2], op0=OP.subtract, op1=OP.mult),
                reads=[b_x1t], sreads=[b_small], writes=[b_ft])
            quant(ft, b_ft, fq, b_fq, 64)
            for k in range(NCH):
                q.op("tensor", lambda e, k=k: e.transpose(
                    pt[:, 0:128], in_=fq[:, k * 128:(k + 1) * 128],
                    identity=ident),
                    reads=[b_fq, b_id], writes=[b_pt])
                q.op("scalar", lambda e, k=k: e.activation(
                    out=fqfm[k], in_=pt[:, 0:128], func=AF.Copy),
                    reads=[b_pt], writes=[b_fqfm[k]])
            for ic, (off, w) in enumerate(IC3):
                for k in range(NCH):
                    q.op("tensor", lambda e, k=k, off=off, w=w: e.matmul(
                        pg[:, 0:w], lhsT=fqfm[k],
                        rhs=wgT[k][:, off:off + w],
                        start=(k == 0), stop=(k == NCH - 1)),
                        reads=[b_fqfm[k], b_wgT[k]], writes=[b_pg])
                for k in range(NCH):
                    q.op("tensor", lambda e, k=k, off=off, w=w: e.matmul(
                        pu[:, 0:w], lhsT=fqfm[k],
                        rhs=wuT[k][:, off:off + w],
                        start=(k == 0), stop=(k == NCH - 1)),
                        reads=[b_fqfm[k], b_wuT[k]], writes=[b_pu])
                q.op("scalar", lambda e, off=off, w=w: e.activation(
                    out=sg_t[:, 0:w], in_=pg[:, 0:w], func=AF.Silu),
                    reads=[b_pg], writes=[b_sg])
                q.op("vector", lambda e, off=off, w=w: e.tensor_tensor(
                    out=ht[:, off:off + w], in0=sg_t[:, 0:w],
                    in1=pu[:, 0:w], op=OP.mult),
                    reads=[b_sg, b_pu], writes=[b_ht])
            for off, w in ((0, 1024), (1024, 1024), (2048, 768)):
                quant(ht[:, off:off + w], b_ht, hq[:, off:off + w], b_hq,
                      w // 16)
            for kc in range(22):
                q.op("tensor", lambda e, kc=kc: e.transpose(
                    pt[:, 0:128], in_=hq[:, kc * 128:(kc + 1) * 128],
                    identity=ident),
                    reads=[b_hq, b_id], writes=[b_pt])
                q.op("scalar", lambda e, kc=kc: e.activation(
                    out=hqfm[kc], in_=pt[:, 0:128], func=AF.Copy),
                    reads=[b_pt], writes=[b_hqfm[kc]])
            for hh in range(2):
                for kc in range(22):
                    q.op("tensor", lambda e, kc=kc, hh=hh: e.matmul(
                        pd2[:, hh * 512:(hh + 1) * 512], lhsT=hqfm[kc],
                        rhs=wdT[kc][:, hh * 512:(hh + 1) * 512],
                        start=(kc == 0), stop=(kc == 21)),
                        reads=[b_hqfm[kc], b_wdT[kc]], writes=[b_pd2])
            q.op("vector", lambda e: e.tensor_add(
                out=x2t, in0=x1t, in1=pd2),
                reads=[b_x1t, b_pd2], writes=[b_x2t])
            q.dma("sync", lambda e, r0=r0: e.dma_start(
                out=out_d[r0:r0 + 128, :], in_=x2t),
                reads=[b_x2t], stream="outd")


def build_p3(noskip=True):
    nc = bass.Bass("TRN2", target_bir_lowering=False)
    q = Prog(nc)
    X = p3_drams(nc)
    emit_p3(nc, q, X)
    with ExitStack() as fst:
        q.finalize(fst, noskip=noskip)
    return nc


def stage_p3_inputs(ins, core, x1=None):
    wg = np.asarray(ins["L0_ffn_w_gate"])
    wu = np.asarray(ins["L0_ffn_w_up"])
    wd = np.asarray(ins["L0_ffn_w_down"])
    c = core
    gsl = [min(c + 8 * i, 21) for i in range(3)]
    d = {
        "wg_sl": np.ascontiguousarray(np.concatenate(
            [wg[j * 128:(j + 1) * 128] for j in gsl], axis=0)),
        "wu_sl": np.ascontiguousarray(np.concatenate(
            [wu[j * 128:(j + 1) * 128] for j in gsl], axis=0)),
        "wd_sl": np.ascontiguousarray(wd[c * 128:(c + 1) * 128]),
    }
    if x1 is not None:
        d["x1_in"] = np.ascontiguousarray(x1)
    return {k: np.asarray(v, dtype=np.float32) for k, v in d.items()}


def build_fused(noskip=True):
    nc = bass.Bass("TRN2", target_bir_lowering=False)
    q = Prog(nc)
    X1 = p1_drams(nc, xa_kind="Internal")
    emit_p1(nc, q, X1)
    q.barrier()
    X2 = p2_drams(nc, x1_kind="Internal", x0h=X1["x0h"],
                  xa_in=X1["xa_out"])
    emit_p2(nc, q, X2)
    q.barrier()
    X3 = p3_drams(nc, x1_in=X2["x1_out"])
    emit_p3(nc, q, X3)
    with ExitStack() as fst:
        q.finalize(fst, noskip=noskip)
    return nc


def stage_fused_inputs(ins, core):
    d = stage_p1_inputs(ins, core)
    d.update(stage_p2_inputs(ins, core))
    d.update(stage_p3_inputs(ins, core))
    return d


_BUILD_CACHE = {}


def _get(name, fn):
    if name not in _BUILD_CACHE:
        _BUILD_CACHE[name] = fn()
    return _BUILD_CACHE[name]


def kernel(**inputs):
    cores = list(range(8))
    nc = _get("fused", lambda: build_fused(noskip=False))
    maps = [stage_fused_inputs(inputs, c) for c in cores]
    r = run_bass_kernel_spmd(nc, maps, core_ids=cores)
    out = np.empty((4, T, D), np.float32)
    for c in cores:
        b, ch = c >> 1, c & 1
        out[b].reshape(NSLAB, ST, D)[ch::2] = \
            np.asarray(r.results[c]["out"]).reshape(NSLAB // 2, ST, D)
    return out


# version 06 (final)
# speedup vs baseline: 1.7329x; 1.3016x over previous
import sys
sys.path.insert(0, "/opt/trn_rl_repo")
from contextlib import ExitStack

import numpy as np
import concourse.bass as bass
import concourse.mybir as mybir
from concourse.bass_utils import run_bass_kernel_spmd

F32 = mybir.dt.float32
F32R = mybir.dt.float32r
BF16 = mybir.dt.bfloat16
I32 = mybir.dt.int32
AF = mybir.ActivationFunctionType
OP = mybir.AluOpType
AX = mybir.AxisListType

MAGIC = 12582912.0
EPS = 1e-5
D = 1024
E = 1024
EH = 512
NCH = 8
MCH = 4
NST = 16
T = 8192
ST = 512
NSLAB = T // ST
DMA_INC = 16


class Buf:
    __slots__ = ("writer", "readers", "name")

    def __init__(self, name=""):
        self.writer = None
        self.readers = []
        self.name = name


class _Op:
    __slots__ = ("engine", "fn", "inc", "deps", "semkey", "need_inc",
                 "val_after", "idx", "nsk", "lbl")


class Prog:
    ENGINES = ("sync", "tensor", "vector", "scalar", "gpsimd")

    def __init__(self, nc):
        self.nc = nc
        self.ops = []
        self.barriers = []
        self.lbl = "init" 

    def barrier(self):
        self.barriers.append(len(self.ops))

    def _op(self, engine, fn, reads, writes, deps, inc, semkey, sreads=None):
        o = _Op()
        o.engine, o.fn, o.inc, o.semkey = engine, fn, inc, semkey
        o.need_inc = False
        o.val_after = None
        o.idx = len(self.ops)
        dep = set(deps or ())
        nsk = set()
        for b in reads or ():
            if b.writer is not None:
                dep.add(b.writer)
        for b in sreads or ():
            if b.writer is not None:
                dep.add(b.writer)
                nsk.add(b.writer)
        for b in writes or ():
            if b.writer is not None:
                dep.add(b.writer)
            dep.update(b.readers)
        o.deps = dep
        o.nsk = nsk
        o.lbl = self.lbl
        self.ops.append(o)
        for b in reads or ():
            b.readers.append(o)
        for b in sreads or ():
            b.readers.append(o)
        for b in writes or ():
            b.writer = o
            b.readers = []
        return o

    def op(self, engine, fn, reads=None, writes=None, deps=None, sreads=None):
        return self._op(engine, fn, reads, writes, deps, 1, engine,
                        sreads=sreads)

    def dma(self, queue, fn, reads=None, writes=None, deps=None,
            private=False, stream=None):
        key = (queue + "dma") if stream is None else ("dm_" + stream)
        o = self._op(queue, fn, reads, writes, deps, DMA_INC, key)
        if private:
            o.semkey = "pvt%d" % o.idx
        return o

    def finalize(self, st, serial=False, noskip=False):
        import os
        trace_f = open("/tmp/prog.txt", "w") if os.environ.get("PROGTRACE") \
            else None
        nc = self.nc
        for o in self.ops:
            for d in o.deps:
                d.need_inc = True
        for o in self.ops:
            if o.inc == DMA_INC:
                o.need_inc = True
        cnt = {}
        run_cnt = []
        for o in self.ops:
            if o.need_inc:
                cnt[o.semkey] = cnt.get(o.semkey, 0) + o.inc
                o.val_after = cnt[o.semkey]
            run_cnt.append(dict(cnt))
        sems = {k: st.enter_context(nc.semaphore("s_" + k)) for k in cnt}
        per_engine = {e: [] for e in self.ENGINES}
        for o in self.ops:
            per_engine[o.engine].append(o)
        final_vals = dict(cnt)
        self_barriers = self.barriers
        block = st.enter_context(nc.Block())

        def section(name):
            def _(eng):
                hw = {}
                bar_i = 0
                for o in per_engine[name]:
                    nonlocal_barriers = self_barriers
                    while bar_i < len(nonlocal_barriers) and \
                            o.idx >= nonlocal_barriers[bar_i]:
                        bpos = nonlocal_barriers[bar_i]
                        if bpos > 0:
                            for k, v in run_cnt[bpos - 1].items():
                                if hw.get(k, 0) < v:
                                    if trace_f is not None:
                                        trace_f.write(
                                            "%s   BARRIER wait %s >= %d\n"
                                            % (name, k, v))
                                    eng.wait_ge(sems[k], v)
                                    hw[k] = v
                        bar_i += 1
                    if serial and o.idx > 0:
                        for k, v in run_cnt[o.idx - 1].items():
                            if hw.get(k, 0) < v:
                                eng.wait_ge(sems[k], v)
                                hw[k] = v
                    for d in sorted(o.deps, key=lambda x: x.idx):
                        import os as _os
                        _nsv = _os.environ.get("NOSKIP_ENGINES", "")
                        if (not noskip) and d.engine == o.engine \
                                and d.inc != DMA_INC and d not in o.nsk \
                                and o.engine not in _nsv:
                            continue
                        if d.val_after is None:
                            continue
                        if d.inc == DMA_INC:
                            val = run_cnt[o.idx - 1].get(d.semkey, 0) \
                                if o.idx > 0 else d.val_after
                            val = max(val, d.val_after)
                        else:
                            val = d.val_after
                        if (not noskip) and hw.get(d.semkey, 0) >= val:
                            continue
                        if trace_f is not None:
                            trace_f.write("%s   wait %s >= %d (dep #%d)\n" % (
                                name, d.semkey, val, d.idx))
                        eng.wait_ge(sems[d.semkey], val)
                        hw[d.semkey] = val
                    if trace_f is not None:
                        trace_f.write("%s #%d L%d sem=%s inc=%s\n" % (
                            name, o.idx, o.fn.__code__.co_firstlineno,
                            o.semkey,
                            o.val_after if o.need_inc else "-"))
                    inst = o.fn(eng)
                    if o.need_inc:
                        inst.then_inc(sems[o.semkey], o.inc)
                if name == "sync":
                    for k, v in final_vals.items():
                        eng.wait_ge(sems[k], v)
            return _

        for name in self.ENGINES:
            if per_engine[name] or name == "sync":
                getattr(block, name)(section(name))


def p1_drams(nc, xa_kind="ExternalOutput", debug=False):
    dram = lambda n, s, dt=F32, k="ExternalInput", **kw: nc.dram_tensor(
        n, s, dt, kind=k, **kw).ap()
    X = {}
    xb = X["xb"] = dram("xb", [T, D])
    inpw_x = X["inpw_x"] = dram("inpw_x", [D, E])
    inpw_z = X["inpw_z"] = dram("inpw_z", [D, EH])
    ln1_g = X["ln1_g"] = dram("ln1_g", [D])
    ln1_b = X["ln1_b"] = dram("ln1_b", [D])
    rms_w = X["rms_w"] = dram("rms_w", [D])
    conv_w = X["conv_w"] = dram("conv_w", [E, 4])
    conv_b = X["conv_b"] = dram("conv_b", [E])
    xprojw = X["xprojw"] = dram("xprojw", [E, 96])
    dtw_h = X["dtw_h"] = dram("dtw_h", [64, EH])
    dtb_h = X["dtb_h"] = dram("dtb_h", [EH])
    alog_h = X["alog_h"] = dram("alog_h", [EH, NST])
    dp_h = X["dp_h"] = dram("dp_h", [EH])
    opw = X["opw"] = dram("opw", [E, D])
    x0h = X["x0h"] = dram("x0h", [T // 2, D])
    blendm = X["blendm"] = dram("blendm", [1])
    xa_out = X["xa_out"] = dram("xa_out", [T // 2, D], k=xa_kind)
    bc_dram = X["bc_dram"] = dram("bc_dram", [32, T], k="Internal")
    NG = NSLAB // 2
    X["dbg"] = dram("dbg", [16, 128, 1024], k="ExternalOutput") if debug \
        else None
    X["agy_in"] = dram("agy_in", [NG, MCH, 128, 1024], BF16, k="Internal")
    X["agy_out"] = dram("agy_out", [NG, 2, MCH, 128, 1024], BF16,
                        k="Internal", addr_space="Local")
    return X


def emit_p1(nc, q, X, debug=False):
    (xb, inpw_x, inpw_z, ln1_g, ln1_b, rms_w, conv_w, conv_b, xprojw, dtw_h,
     dtb_h, alog_h, dp_h, opw, x0h, blendm, xa_out, bc_dram, dbg, agy_in,
     agy_out) = (X[k] for k in (
        "xb", "inpw_x", "inpw_z", "ln1_g", "ln1_b", "rms_w", "conv_w",
        "conv_b", "xprojw", "dtw_h", "dtb_h", "alog_h", "dp_h", "opw",
        "x0h", "blendm", "xa_out", "bc_dram", "dbg", "agy_in", "agy_out"))
    NG = NSLAB // 2

    st = ExitStack()
    sb = lambda name, shape, dt=F32: st.enter_context(
        nc.sbuf_tensor("A_" + name, shape, dt)).ap()
    mkpsum = lambda name, shape: st.enter_context(
        nc.psum_tensor("A_" + name, shape, F32)).ap()
    B = Buf

    with st:
        w_inp = [sb(f"w_inp{k}", [128, E + EH], BF16) for k in range(NCH)]
        w_xp = [sb(f"w_xp{k}", [128, 96]) for k in range(NCH)]
        w_dt = [sb(f"w_dt{c}", [64, 128], F32R) for c in range(MCH)]
        w_op = [sb(f"w_op{k}", [128, D], BF16) for k in range(NCH)]
        a_sb = sb("a_sb", [128, MCH * NST])
        cw_sb = [sb(f"cw{k}", [128, 4]) for k in range(NCH)]
        cb_sb = sb("cb_sb", [128, NCH])
        dtb_sb = sb("dtb_sb", [128, MCH])
        dp_sb = sb("dp_sb", [128, MCH])
        rmsw_sb = sb("rmsw_sb", [128, NCH])
        g1rep = sb("g1rep", [128, D], BF16)
        b1rep = sb("b1rep", [128, D], BF16)
        ident = sb("ident", [128, 128])
        identr = sb("identr", [128, 128], F32R)
        blm = sb("blm", [128, 2])
        eps_sb = sb("eps_sb", [128, 1])
        row_t = sb("row_t", [1, 1024])
        x0t = [sb(f"x0t{i}", [128, D]) for i in range(2)]
        aint = sb("aint", [128, D])
        ln1t = sb("ln1t", [128, D])
        rmst = sb("rmst", [128, D])
        stat = sb("stat", [128, 12])
        mvt = sb("mvt", [128, 6])
        smallt = sb("smallt", [128, 8])
        rms_fm = [sb(f"rms_fm{k}", [128, ST], BF16) for k in range(NCH)]
        xi = [sb(f"xi{k}", [128, ST + 3], BF16) for k in range(NCH)]
        cv = sb("cv", [128, ST])
        dt_r = sb("dt_r", [64, ST], F32R)
        bc_sb = [sb(f"bc_sb{i}", [32, ST]) for i in range(2)]
        u_t = [sb(f"u_t{c}", [128, ST]) for c in range(2)]
        sp_t = sb("sp_t", [128, ST])
        sp_t2 = sb("sp_t2", [128, ST])
        xcm = [[sb(f"xcm{c}_{i}", [128, ST]) for c in range(MCH)]
               for i in range(2)]
        xcp = [sb(f"xcp{c}", [128, ST]) for c in range(MCH)]
        delta = [[sb(f"delta{c}_{i}", [128, ST]) for c in range(MCH)]
                 for i in range(2)]
        sz = [[sb(f"sz{c}_{i}", [128, ST], BF16) for c in range(MCH)]
              for i in range(2)]
        u_scr = sb("u_scr", [128, ST])
        da = [sb(f"da{i}", [128, ST]) for i in range(2)]
        bxt = [sb(f"bxt{i}", [128, ST]) for i in range(2)]
        hsc = [sb(f"hsc{i}", [128, ST]) for i in range(2)]
        prod = [sb(f"prod{i}", [128, ST], F32R) for i in range(2)]
        bbt = [sb(f"bbt{i}", [128, ST]) for i in range(2)]
        cbt = [sb(f"cbt{i}", [128, ST]) for i in range(2)]
        state = sb("state", [128, MCH * NST])
        t1 = sb("t1", [128, ST])
        y3t = sb("y3t", [128, ST], BF16)
        ye = sb("ye", [128, NCH * 128], BF16)
        yo = sb("yo", [128, NCH * 128], BF16)
        yb = sb("yb", [128, NCH * 128], BF16)
        x0p = sb("x0p", [128, D])
        stage = x0p
        pa = sb("pa", [128, D])
        pb = sb("pb", [128, D])
        pc = sb("pc", [128, D])
        xar = x0p
        stat2 = sb("stat2", [128, 12])
        mvt2 = sb("mvt2", [128, 6])
        small2 = sb("small2", [128, 8])
        pm = [mkpsum(f"pm{i}", [128, 512]) for i in range(2)]
        pt = mkpsum("pt", [128, 512])
        pd = mkpsum("pd", [128, 512])
        pys = [mkpsum(f"pys{i}", [128, 512]) for i in range(2)]
        pop = mkpsum("pop", [128, D])

        b_winp = [B() for _ in range(NCH)]
        b_wxp = [B() for _ in range(NCH)]
        b_wdt = [B() for _ in range(MCH)]
        b_wop = [B() for _ in range(NCH)]
        b_a = B(); b_cw = [B() for _ in range(NCH)]
        b_cb = B(); b_dtb = B(); b_dp = B(); b_rmsw = B()
        b_g1 = B(); b_b1 = B(); b_id = B(); b_idr = B()
        b_blm = B(); b_row = B(); b_eps = B()
        b_x0t = [B(), B()]
        b_aint = B(); b_ln1t = B(); b_rmst = B()
        b_stat = B(); b_mvt = B(); b_small = B()
        b_rmsfm = [B() for _ in range(NCH)]
        b_xi = [B() for _ in range(NCH)]
        b_cv = B(); b_dtr = B(); b_bcsb = [B(), B()]
        b_ut = [B(), B()]
        b_spt = B(); b_spt2 = B()
        b_xcm = [[B() for _ in range(MCH)] for _ in range(2)]
        b_xcp = [B() for _ in range(MCH)]
        b_delta = [[B() for _ in range(MCH)] for _ in range(2)]
        b_sz = [[B() for _ in range(MCH)] for _ in range(2)]
        b_uscr = B()
        b_da = [B(), B()]; b_bx = [B(), B()]; b_h = [B(), B()]
        b_prod = [B(), B()]; b_bbt = [B(), B()]; b_cbt = [B(), B()]
        b_state = B(); b_t1 = B(); b_y3t = B()
        b_ye = B(); b_yo = B(); b_yb = B()
        b_x0p = B(); b_pa = B(); b_pb = B(); b_pc = B()
        b_stat2 = B(); b_mvt2 = B(); b_small2 = B()
        b_pm = [B(), B()]
        b_pt = B(); b_pd = B(); b_pys = [B(), B()]; b_pop = B()
        b_bcdram = B(); b_agyin = B(); b_agyout = B()

        def colload(vec, out_ap, out_buf, n):
            q.dma("sync", lambda e: e.dma_start(out=row_t[:, 0:n * 128],
                                                in_=vec[None, :]),
                  writes=[b_row], private=True)
            for c in range(n):
                q.op("tensor", lambda e, c=c: e.transpose(
                    pt[:, c:c + 1], in_=row_t[:, c * 128:(c + 1) * 128],
                    identity=ident[0:1, 0:1]),
                    reads=[b_row, b_id], writes=[b_pt])
            q.op("scalar", lambda e: e.activation(
                out=out_ap, in_=pt[:, 0:n], func=AF.Copy),
                reads=[b_pt], writes=[out_buf])

        def lnstats(xap, xbuf, sap, sbuf, mvap, mvbuf, mu_sl, r_sl, sc,
                    smbuf):
            s_d, s_d2, s_v, s_sd = sc
            for i in range(2):
                q.op("vector", lambda e, i=i: e.bn_stats(
                    out=sap[:, 6 * i:6 * i + 6],
                    in_=xap[:, 512 * i:512 * (i + 1)]),
                    reads=[xbuf], writes=[sbuf])
            q.op("vector", lambda e: e.bn_aggr(out=mvap[:, 0:2],
                                               in_=sap[:, 0:6]),
                 reads=[sbuf], writes=[mvbuf])
            q.op("vector", lambda e: e.bn_aggr(out=mvap[:, 2:4],
                                               in_=sap[:, 6:12]),
                 reads=[sbuf], writes=[mvbuf])
            q.op("vector", lambda e: e.tensor_scalar(
                out=mu_sl, in0=mvap[:, 0:1], scalar1=mvap[:, 2:3],
                scalar2=0.5, op0=OP.add, op1=OP.mult),
                reads=[mvbuf], sreads=[mvbuf], writes=[smbuf])
            q.op("vector", lambda e: e.tensor_scalar(
                out=s_d, in0=mvap[:, 0:1], scalar1=mvap[:, 2:3],
                scalar2=None, op0=OP.subtract),
                reads=[mvbuf], sreads=[mvbuf], writes=[smbuf])
            q.op("scalar", lambda e: e.activation(
                out=s_d2, in_=s_d, func=AF.Square, scale=0.5),
                reads=[smbuf], writes=[smbuf])
            q.op("vector", lambda e: e.tensor_scalar(
                out=s_v, in0=mvap[:, 1:2], scalar1=mvap[:, 3:4],
                scalar2=0.5, op0=OP.add, op1=OP.mult),
                reads=[mvbuf], sreads=[mvbuf], writes=[smbuf])
            q.op("vector", lambda e: e.tensor_scalar(
                out=s_v, in0=s_v, scalar1=s_d2, scalar2=None, op0=OP.add),
                reads=[smbuf], sreads=[smbuf], writes=[smbuf])
            q.op("scalar", lambda e: e.activation(
                out=s_sd, in_=s_v, func=AF.Sqrt, bias=eps_sb),
                reads=[smbuf], sreads=[b_eps], writes=[smbuf])
            q.op("vector", lambda e: e.reciprocal(out=r_sl, in_=s_sd),
                 reads=[smbuf], writes=[smbuf])

        q.op("vector", lambda e: e.memset(eps_sb, EPS), writes=[b_eps])
        q.op("gpsimd", lambda e: e.memset(ident, 0.0), writes=[b_id])
        q.op("gpsimd", lambda e: e.affine_select(
            out=ident, in_=ident, compare_op=OP.not_equal, fill=1.0, base=0,
            pattern=[[-1, 128]], channel_multiplier=1),
            reads=[b_id], writes=[b_id])
        q.op("vector", lambda e: e.tensor_copy(out=identr, in_=ident),
             reads=[b_id], writes=[b_idr])
        q.dma("gpsimd", lambda e: e.dma_start(
            out=g1rep, in_=ln1_g.unsqueeze(0).broadcast_to([128, D])),
            writes=[b_g1])
        q.dma("gpsimd", lambda e: e.dma_start(
            out=b1rep, in_=ln1_b.unsqueeze(0).broadcast_to([128, D])),
            writes=[b_b1])
        q.dma("sync", lambda e: e.dma_start(
            out=blm[:, 0:1], in_=blendm.unsqueeze(0).broadcast_to([128, 1])),
            writes=[b_blm])
        q.op("vector", lambda e: e.tensor_scalar(
            out=blm[:, 1:2], in0=blm[:, 0:1], scalar1=-1.0, scalar2=1.0,
            op0=OP.mult, op1=OP.add), reads=[b_blm], writes=[b_blm])
        colload(rms_w, rmsw_sb, b_rmsw, NCH)
        colload(conv_b, cb_sb, b_cb, NCH)
        colload(dtb_h, dtb_sb, b_dtb, MCH)
        colload(dp_h, dp_sb, b_dp, MCH)
        for c in range(MCH):
            q.dma("sync", lambda e, c=c: e.dma_start(
                out=stage[:, 0:NST], in_=alog_h[c * 128:(c + 1) * 128, :]),
                writes=[b_x0p])
            q.op("scalar", lambda e, c=c: e.activation(
                out=sp_t[:, 0:NST], in_=stage[:, 0:NST],
                func=AF.Exp), reads=[b_x0p], writes=[b_spt])
            q.op("vector", lambda e, c=c: e.tensor_single_scalar(
                out=a_sb[:, c * NST:(c + 1) * NST], in_=sp_t[:, 0:NST],
                scalar=-1.0, op=OP.mult),
                reads=[b_spt], writes=[b_a])
        for k in range(NCH):
            q.dma("sync", lambda e, k=k: e.dma_start(
                out=cw_sb[k], in_=conv_w[k * 128:(k + 1) * 128, :]),
                writes=[b_cw[k]])
            q.dma("sync", lambda e, k=k: e.dma_start(
                out=stage, in_=inpw_x[k * 128:(k + 1) * 128, :]),
                writes=[b_x0p])
            q.op("vector", lambda e, k=k: e.tensor_scalar(
                out=w_inp[k][:, 0:E], in0=stage,
                scalar1=rmsw_sb[:, k:k + 1], scalar2=None, op0=OP.mult),
                reads=[b_x0p], sreads=[b_rmsw], writes=[b_winp[k]])
            q.dma("sync", lambda e, k=k: e.dma_start(
                out=stage[:, 0:EH], in_=inpw_z[k * 128:(k + 1) * 128, :]),
                writes=[b_x0p])
            q.op("vector", lambda e, k=k: e.tensor_scalar(
                out=w_inp[k][:, E:E + EH], in0=stage[:, 0:EH],
                scalar1=rmsw_sb[:, k:k + 1], scalar2=None, op0=OP.mult),
                reads=[b_x0p], sreads=[b_rmsw], writes=[b_winp[k]])
            q.dma("sync", lambda e, k=k: e.dma_start(
                out=w_xp[k], in_=xprojw[k * 128:(k + 1) * 128, :]),
                writes=[b_wxp[k]])
            q.dma("sync", lambda e, k=k: e.dma_start(
                out=stage, in_=opw[k * 128:(k + 1) * 128, :]),
                writes=[b_x0p])
            q.op("scalar", lambda e, k=k: e.activation(
                out=w_op[k], in_=stage, func=AF.Copy),
                reads=[b_x0p], writes=[b_wop[k]])
        for c in range(MCH):
            q.dma("sync", lambda e, c=c: e.dma_start(
                out=stage[0:64, 0:128], in_=dtw_h[:, c * 128:(c + 1) * 128]),
                writes=[b_x0p])
            q.op("scalar", lambda e, c=c: e.activation(
                out=w_dt[c], in_=stage[0:64, 0:128], func=AF.Copy),
                reads=[b_x0p], writes=[b_wdt[c]])
        for k in range(NCH):
            q.op("vector", lambda e, k=k: e.memset(xi[k][:, 0:3], 0.0),
                 writes=[b_xi[k]])
        q.op("vector", lambda e: e.memset(state, 0.0), writes=[b_state])

        def emit_prep(s):
            q.lbl = "p1_prep"
            xp = s % 2
            for tt in range(4):
                tok0 = s * ST + tt * 128
                q.dma("sync", lambda e, tok0=tok0, xp=xp: e.dma_start(
                    out=x0t[xp], in_=xb[tok0:tok0 + 128, :]),
                    writes=[b_x0t[xp]], stream="x0")
                lnstats(x0t[xp], b_x0t[xp], stat, b_stat, mvt, b_mvt,
                        smallt[:, 0:1], smallt[:, 1:2],
                        [smallt[:, 4:5], smallt[:, 5:6], smallt[:, 6:7],
                         smallt[:, 7:8]], b_small)
                q.op("vector", lambda e, xp=xp: e.tensor_scalar(
                    out=aint, in0=x0t[xp], scalar1=smallt[:, 0:1],
                    scalar2=smallt[:, 1:2], op0=OP.subtract, op1=OP.mult),
                    reads=[b_x0t[xp]], sreads=[b_small], writes=[b_aint])
                lnstats(aint, b_aint, stat, b_stat, mvt, b_mvt,
                        smallt[:, 2:3], smallt[:, 3:4],
                        [smallt[:, 4:5], smallt[:, 5:6], smallt[:, 6:7],
                         smallt[:, 7:8]], b_small)
                q.op("vector", lambda e: e.tensor_scalar(
                    out=ln1t, in0=aint, scalar1=smallt[:, 2:3],
                    scalar2=smallt[:, 3:4], op0=OP.subtract, op1=OP.mult),
                    reads=[b_aint], sreads=[b_small], writes=[b_ln1t])
                q.op("vector", lambda e: e.tensor_mul(
                    out=rmst, in0=ln1t, in1=g1rep),
                    reads=[b_ln1t, b_g1], writes=[b_rmst])
                q.op("vector", lambda e: e.tensor_add(
                    out=ln1t, in0=rmst, in1=b1rep),
                    reads=[b_rmst, b_b1], writes=[b_ln1t])
                for i in range(2):
                    q.op("vector", lambda e, i=i: e.bn_stats(
                        out=stat[:, 6 * i:6 * i + 6],
                        in_=ln1t[:, 512 * i:512 * (i + 1)]),
                        reads=[b_ln1t], writes=[b_stat])
                q.op("vector", lambda e: e.bn_aggr(out=mvt[:, 0:2],
                                                   in_=stat[:, 0:6]),
                     reads=[b_stat], writes=[b_mvt])
                q.op("vector", lambda e: e.bn_aggr(out=mvt[:, 2:4],
                                                   in_=stat[:, 6:12]),
                     reads=[b_stat], writes=[b_mvt])
                q.op("vector", lambda e: e.tensor_scalar(
                    out=mvt[:, 4:5], in0=mvt[:, 0:1], scalar1=mvt[:, 2:3],
                    scalar2=0.5, op0=OP.add, op1=OP.mult),
                    reads=[b_mvt], sreads=[b_mvt], writes=[b_mvt])
                q.op("vector", lambda e: e.tensor_scalar(
                    out=smallt[:, 4:5], in0=mvt[:, 0:1], scalar1=mvt[:, 2:3],
                    scalar2=None, op0=OP.subtract),
                    reads=[b_mvt], sreads=[b_mvt], writes=[b_small])
                q.op("scalar", lambda e: e.activation(
                    out=smallt[:, 5:6], in_=smallt[:, 4:5], func=AF.Square,
                    scale=0.5), reads=[b_small], writes=[b_small])
                q.op("vector", lambda e: e.tensor_scalar(
                    out=smallt[:, 6:7], in0=mvt[:, 1:2], scalar1=mvt[:, 3:4],
                    scalar2=0.5, op0=OP.add, op1=OP.mult),
                    reads=[b_mvt], sreads=[b_mvt], writes=[b_small])
                q.op("vector", lambda e: e.tensor_scalar(
                    out=smallt[:, 6:7], in0=smallt[:, 6:7],
                    scalar1=smallt[:, 5:6], scalar2=None, op0=OP.add),
                    reads=[b_small], sreads=[b_small], writes=[b_small])
                q.op("scalar", lambda e: e.activation(
                    out=smallt[:, 4:5], in_=mvt[:, 4:5], func=AF.Square),
                    reads=[b_mvt], writes=[b_small])
                q.op("vector", lambda e: e.tensor_scalar(
                    out=smallt[:, 6:7], in0=smallt[:, 6:7],
                    scalar1=smallt[:, 4:5], scalar2=None, op0=OP.add),
                    reads=[b_small], sreads=[b_small], writes=[b_small])
                q.op("scalar", lambda e: e.activation(
                    out=smallt[:, 7:8], in_=smallt[:, 6:7], func=AF.Sqrt,
                    bias=eps_sb),
                    reads=[b_small], sreads=[b_eps], writes=[b_small])
                q.op("vector", lambda e: e.reciprocal(
                    out=mvt[:, 5:6], in_=smallt[:, 7:8]),
                    reads=[b_small], writes=[b_mvt])
                q.op("scalar", lambda e: e.activation(
                    out=rmst, in_=ln1t, func=AF.Copy, scale=mvt[:, 5:6]),
                    reads=[b_ln1t], sreads=[b_mvt], writes=[b_rmst])
                for k in range(NCH):
                    q.op("tensor", lambda e, k=k: e.transpose(
                        pt[:, 0:128], in_=rmst[:, k * 128:(k + 1) * 128],
                        identity=ident),
                        reads=[b_rmst, b_id], writes=[b_pt])
                    q.op("scalar", lambda e, k=k, tt=tt: e.activation(
                        out=rms_fm[k][:, tt * 128:(tt + 1) * 128],
                        in_=pt[:, 0:128], func=AF.Copy),
                        reads=[b_pt], writes=[b_rmsfm[k]])
            for dc in range(12):
                pmx = dc % 2
                for k in range(NCH):
                    q.op("tensor", lambda e, dc=dc, k=k, pmx=pmx: e.matmul(
                        pm[pmx], lhsT=w_inp[k][:, dc * 128:(dc + 1) * 128],
                        rhs=rms_fm[k], start=(k == 0), stop=(k == NCH - 1)),
                        reads=[b_winp[k], b_rmsfm[k]], writes=[b_pm[pmx]])
                if dc < 8:
                    if s > 0:
                        q.op("vector", lambda e, dc=dc: e.tensor_copy(
                            out=xi[dc][:, 0:3], in_=xi[dc][:, ST:ST + 3]),
                            reads=[b_xi[dc]], writes=[b_xi[dc]])
                    q.op("scalar", lambda e, dc=dc, pmx=pmx: e.activation(
                        out=xi[dc][:, 3:ST + 3], in_=pm[pmx], func=AF.Copy),
                        reads=[b_pm[pmx]], writes=[b_xi[dc]])
                else:
                    c = dc - 8
                    q.op("scalar", lambda e, c=c, pmx=pmx, xp=xp: e.activation(
                        out=sz[xp][c], in_=pm[pmx], func=AF.Silu),
                        reads=[b_pm[pmx]], writes=[b_sz[xp][c]])
            for k in range(NCH):
                q.op("vector", lambda e, k=k: e.tensor_scalar(
                    out=cv, in0=xi[k][:, 0:ST], scalar1=cw_sb[k][:, 0:1],
                    scalar2=None, op0=OP.mult),
                    reads=[b_xi[k]], sreads=[b_cw[k]], writes=[b_cv])
                for j in (1, 2, 3):
                    q.op("vector", lambda e, k=k, j=j: e.scalar_tensor_tensor(
                        out=cv, in0=xi[k][:, j:ST + j],
                        scalar=cw_sb[k][:, j:j + 1], in1=cv,
                        op0=OP.mult, op1=OP.add),
                        reads=[b_xi[k], b_cv], sreads=[b_cw[k]], writes=[b_cv])
                if k < MCH:
                    dst, dstb = xcm[xp][k], b_xcm[xp][k]
                else:
                    dst, dstb = xcp[k - MCH], b_xcp[k - MCH]
                q.op("scalar", lambda e, k=k, dst=dst: e.activation(
                    out=dst, in_=cv, func=AF.Silu, bias=cb_sb[:, k:k + 1]),
                    reads=[b_cv], sreads=[b_cb], writes=[dstb])
            for k in range(NCH):
                src = (xcm[xp][k], b_xcm[xp][k]) if k < MCH else \
                      (xcp[k - MCH], b_xcp[k - MCH])
                q.op("tensor", lambda e, k=k, src=src: e.matmul(
                    pd[0:96, :], lhsT=w_xp[k], rhs=src[0],
                    start=(k == 0), stop=(k == NCH - 1)),
                    reads=[b_wxp[k], src[1]], writes=[b_pd])
            q.op("scalar", lambda e: e.activation(
                out=dt_r, in_=pd[0:64, :], func=AF.Copy),
                reads=[b_pd], writes=[b_dtr])
            q.op("scalar", lambda e, xp=xp: e.activation(
                out=bc_sb[xp], in_=pd[64:96, :], func=AF.Copy),
                reads=[b_pd], writes=[b_bcsb[xp]])
            q.dma("sync", lambda e, s=s, xp=xp: e.dma_start(
                out=bc_dram[:, s * ST:(s + 1) * ST], in_=bc_sb[xp]),
                reads=[b_bcsb[xp]], writes=[b_bcdram], stream="bc")
            for c in range(MCH):
                pmx = c % 2
                q.op("tensor", lambda e, c=c, pmx=pmx: e.matmul(
                    pm[pmx], lhsT=w_dt[c], rhs=dt_r, start=True, stop=True),
                    reads=[b_wdt[c], b_dtr], writes=[b_pm[pmx]])
                q.op("scalar", lambda e, c=c, pmx=pmx: e.activation(
                    out=sp_t, in_=pm[pmx], func=AF.Relu,
                    bias=dtb_sb[:, c:c + 1]),
                    reads=[b_pm[pmx]], sreads=[b_dtb], writes=[b_spt])
                q.op("scalar", lambda e, c=c, pmx=pmx: e.activation(
                    out=sp_t2, in_=pm[pmx], func=AF.Abs,
                    bias=dtb_sb[:, c:c + 1]),
                    reads=[b_pm[pmx]], sreads=[b_dtb], writes=[b_spt2])
                q.op("scalar", lambda e: e.activation(
                    out=sp_t2, in_=sp_t2, func=AF.Exp, scale=-1.0),
                    sreads=[b_spt2], writes=[b_spt2])
                q.op("scalar", lambda e: e.activation(
                    out=sp_t2, in_=sp_t2, func=AF.Ln, bias=1.0),
                    sreads=[b_spt2], writes=[b_spt2])
                q.op("vector", lambda e, c=c, xp=xp: e.tensor_add(
                    out=delta[xp][c], in0=sp_t, in1=sp_t2),
                    reads=[b_spt, b_spt2], writes=[b_delta[xp][c]])

        def emit_scan(s):
            q.lbl = "p1_scan"
            xp = s % 2
            for half in range(2):
                for c in (2 * half, 2 * half + 1):
                    q.op("vector", lambda e, c=c, xp=xp: e.tensor_mul(
                        out=u_t[c & 1], in0=delta[xp][c], in1=xcm[xp][c]),
                        reads=[b_delta[xp][c], b_xcm[xp][c]],
                        writes=[b_ut[c & 1]])
                for n in range(NST):
                    p2 = n % 2
                    q.dma("sync", lambda e, n=n, s=s, p2=p2: e.dma_start(
                        out=bbt[p2],
                        in_=bc_dram[n, s * ST:(s + 1) * ST]
                        .unsqueeze(0).broadcast_to([128, ST])),
                        reads=[b_bcdram], writes=[b_bbt[p2]], stream="bb")
                    q.dma("scalar", lambda e, n=n, s=s, p2=p2: e.dma_start(
                        out=cbt[p2],
                        in_=bc_dram[16 + n, s * ST:(s + 1) * ST]
                        .unsqueeze(0).broadcast_to([128, ST])),
                        reads=[b_bcdram], writes=[b_cbt[p2]], stream="cb")
                    for c in (2 * half, 2 * half + 1):
                        idx = c * NST + n
                        pq = (n * 2 + c) % 2
                        q.op("scalar", lambda e, c=c, idx=idx, pq=pq, xp=xp:
                             e.activation(out=da[pq], in_=delta[xp][c],
                                          func=AF.Exp,
                                          scale=a_sb[:, idx:idx + 1]),
                             reads=[b_delta[xp][c]], sreads=[b_a],
                             writes=[b_da[pq]])
                        q.op("vector", lambda e, c=c, pq=pq, p2=p2:
                             e.tensor_mul(
                                 out=bxt[pq], in0=u_t[c & 1], in1=bbt[p2]),
                             reads=[b_ut[c & 1], b_bbt[p2]],
                             writes=[b_bx[pq]])
                        q.op("vector", lambda e, pq=pq, idx=idx:
                             e.tensor_tensor_scan(
                                 out=hsc[pq], data0=da[pq], data1=bxt[pq],
                                 initial=state[:, idx:idx + 1],
                                 op0=OP.mult, op1=OP.add),
                             reads=[b_da[pq], b_bx[pq]], sreads=[b_state],
                             writes=[b_h[pq]])
                        q.op("vector", lambda e, pq=pq, idx=idx:
                             e.tensor_scalar(
                                 out=state[:, idx:idx + 1],
                                 in0=hsc[pq][:, ST - 1:ST],
                                 scalar1=1.0, scalar2=None, op0=OP.mult),
                             sreads=[b_h[pq]], writes=[b_state])
                        q.op("gpsimd", lambda e, pq=pq, p2=p2: e.tensor_mul(
                            out=prod[pq], in0=hsc[pq], in1=cbt[p2]),
                            reads=[b_cbt[p2]], sreads=[b_h[pq]],
                            writes=[b_prod[pq]])
                        q.op("tensor", lambda e, pq=pq, c=c, n=n: e.matmul(
                            pys[c & 1], lhsT=identr, rhs=prod[pq],
                            start=(n == 0), stop=(n == NST - 1)),
                            reads=[b_idr, b_prod[pq]], writes=[b_pys[c & 1]])
                for c in (2 * half, 2 * half + 1):
                    q.op("vector", lambda e, c=c, xp=xp:
                         e.scalar_tensor_tensor(
                             out=t1, in0=xcm[xp][c], scalar=dp_sb[:, c:c + 1],
                             in1=pys[c & 1], op0=OP.mult, op1=OP.add),
                         reads=[b_xcm[xp][c], b_pys[c & 1]], sreads=[b_dp],
                         writes=[b_t1])
                    q.op("vector", lambda e, c=c, xp=xp: e.tensor_mul(
                        out=y3t, in0=t1, in1=sz[xp][c]),
                        reads=[b_t1, b_sz[xp][c]], writes=[b_y3t])
                    q.dma("sync", lambda e, c=c, s=s: e.dma_start(
                        out=agy_in[s // 2, c, :,
                                   (s % 2) * ST:(s % 2) * ST + ST],
                        in_=y3t),
                        reads=[b_y3t], writes=[b_agyin], stream="y3")

        coll_ops = []

        def emit_ag(g):
            q.lbl = "p1_ag"
            o = q.op("gpsimd", lambda e, g=g: e.collective_compute(
                "AllGather", OP.bypass,
                replica_groups=[[0, 1], [2, 3], [4, 5], [6, 7]],
                ins=[agy_in[g]],
                outs=[agy_out[g]]),
                reads=[b_agyin], writes=[b_agyout])
            coll_ops.append(o)

        def emit_post(ps):
            q.lbl = "p1_post"
            agdep = coll_ops[ps]
            for tt in range(4):
                q.dma("sync", lambda e, ps=ps, tt=tt: e.dma_start(
                    out=ye.rearrange("p (k f) -> p k f", f=128),
                    in_=agy_out[ps, :, :, :, tt * 128:tt * 128 + 128]
                    .rearrange("s c p f -> p (s c) f")),
                    deps=[agdep], writes=[b_ye], stream="yeo")
                q.dma("sync", lambda e, ps=ps, tt=tt: e.dma_start(
                    out=yo.rearrange("p (k f) -> p k f", f=128),
                    in_=agy_out[ps, :, :, :,
                                512 + tt * 128:640 + tt * 128]
                    .rearrange("s c p f -> p (s c) f")),
                    deps=[agdep], writes=[b_yo], stream="yeo")
                q.op("vector", lambda e: e.tensor_scalar(
                    out=yb, in0=yo, scalar1=blm[:, 0:1], scalar2=None,
                    op0=OP.mult), reads=[b_yo], sreads=[b_blm],
                    writes=[b_yb])
                q.op("vector", lambda e: e.scalar_tensor_tensor(
                    out=yb, in0=ye, scalar=blm[:, 1:2], in1=yb,
                    op0=OP.mult, op1=OP.add),
                    reads=[b_ye, b_yb], sreads=[b_blm], writes=[b_yb])
                r0 = ps * ST + tt * 128
                q.dma("sync", lambda e, r0=r0: e.dma_start(
                    out=x0p, in_=x0h[r0:r0 + 128, :]), writes=[b_x0p],
                    stream="x0p")
                lnstats(x0p, b_x0p, stat2, b_stat2, mvt2, b_mvt2,
                        small2[:, 0:1], small2[:, 1:2],
                        [small2[:, 4:5], small2[:, 5:6], small2[:, 6:7],
                         small2[:, 7:8]], b_small2)
                q.op("vector", lambda e: e.tensor_scalar(
                    out=pa, in0=x0p, scalar1=small2[:, 0:1],
                    scalar2=small2[:, 1:2], op0=OP.subtract, op1=OP.mult),
                    reads=[b_x0p], sreads=[b_small2], writes=[b_pa])
                lnstats(pa, b_pa, stat2, b_stat2, mvt2, b_mvt2,
                        small2[:, 2:3], small2[:, 3:4],
                        [small2[:, 4:5], small2[:, 5:6], small2[:, 6:7],
                         small2[:, 7:8]], b_small2)
                q.op("vector", lambda e: e.tensor_scalar(
                    out=pb, in0=pa, scalar1=small2[:, 2:3],
                    scalar2=small2[:, 3:4], op0=OP.subtract, op1=OP.mult),
                    reads=[b_pa], sreads=[b_small2], writes=[b_pb])
                q.op("gpsimd", lambda e: e.tensor_mul(
                    out=pc, in0=pb, in1=g1rep),
                    reads=[b_pb, b_g1], writes=[b_pc])
                q.op("gpsimd", lambda e: e.tensor_add(
                    out=pb, in0=pc, in1=b1rep),
                    reads=[b_pc, b_b1], writes=[b_pb])
                q.op("gpsimd", lambda e: e.tensor_add(
                    out=pc, in0=pa, in1=pb),
                    reads=[b_pa, b_pb], writes=[b_pc])
                for hh in range(2):
                    for k in range(NCH):
                        q.op("tensor", lambda e, k=k, hh=hh: e.matmul(
                            pop[:, hh * 512:(hh + 1) * 512],
                            lhsT=yb[:, k * 128:(k + 1) * 128],
                            rhs=w_op[k][:, hh * 512:(hh + 1) * 512],
                            start=(k == 0), stop=(k == NCH - 1)),
                            reads=[b_yb, b_wop[k]], writes=[b_pop])
                q.op("vector", lambda e: e.tensor_add(
                    out=xar, in0=pc, in1=pop),
                    reads=[b_pc, b_pop], writes=[b_x0p])
                q.dma("sync", lambda e, r0=r0: e.dma_start(
                    out=xa_out[r0:r0 + 128, :], in_=xar),
                    reads=[b_x0p], stream="xaout")

        def emit_debug0():
            q.dma("sync", lambda e: e.dma_start(out=dbg[0], in_=aint),
                  reads=[b_aint])
            q.dma("sync", lambda e: e.dma_start(out=dbg[1], in_=ln1t),
                  reads=[b_ln1t])
            q.dma("sync", lambda e: e.dma_start(out=dbg[2], in_=rmst),
                  reads=[b_rmst])
            q.dma("gpsimd", lambda e: e.dma_start(
                out=dbg[3][:, 0:515], in_=xi[0]), reads=[b_xi[0]])
            q.dma("sync", lambda e: e.dma_start(
                out=dbg[4][:, 0:512], in_=xcm[0][0]), reads=[b_xcm[0][0]])
            q.dma("gpsimd", lambda e: e.dma_start(
                out=dbg[5][:, 0:512], in_=xcp[0]), reads=[b_xcp[0]])
            q.dma("sync", lambda e: e.dma_start(
                out=dbg[6][:, 0:512], in_=delta[0][0]), reads=[b_delta[0][0]])
            q.dma("gpsimd", lambda e: e.dma_start(
                out=dbg[7][:, 0:512], in_=sz[0][0]), reads=[b_sz[0][0]])
            q.dma("sync", lambda e: e.dma_start(
                out=dbg[8][0:32, 0:512], in_=bc_sb), reads=[b_bcsb])
            q.dma("sync", lambda e: e.dma_start(
                out=dbg[8][:, 600:606], in_=mvt), reads=[b_mvt])
            q.dma("sync", lambda e: e.dma_start(
                out=dbg[8][:, 610:618], in_=smallt), reads=[b_small])

        def emit_debug1():
            q.dma("sync", lambda e: e.dma_start(
                out=dbg[9][:, 0:64], in_=state), reads=[b_state])
            q.dma("gpsimd", lambda e: e.dma_start(
                out=dbg[10][:, 0:512], in_=y3t), reads=[b_y3t])
            q.dma("sync", lambda e: e.dma_start(
                out=dbg[11][:, 0:512], in_=t1), reads=[b_t1])
            q.dma("sync", lambda e: e.dma_start(
                out=dbg[12][:, 0:512], in_=hsc[1]), reads=[b_h[1]])
            q.dma("sync", lambda e: e.dma_start(
                out=dbg[13][:, 0:512], in_=da[1]), reads=[b_da[1]])
            q.dma("sync", lambda e: e.dma_start(
                out=dbg[14][:, 0:512], in_=bxt[1]), reads=[b_bx[1]])
            q.dma("sync", lambda e: e.dma_start(
                out=dbg[15][:, 0:64], in_=a_sb), reads=[b_a])

        for s in range(NSLAB):
            emit_prep(s)
            if debug and s == 0:
                emit_debug0()
            if s > 0:
                emit_scan(s - 1)
                if debug and s == 1:
                    emit_debug1()
                if s % 2 == 0:
                    emit_ag(s // 2 - 1)
                    emit_post(s // 2 - 1)
        emit_scan(NSLAB - 1)
        emit_ag(NSLAB // 2 - 1)
        emit_post(NSLAB // 2 - 1)



def build_p1(xa_kind="ExternalOutput", debug=False, serial=False,
             noskip=False):
    nc = bass.Bass("TRN2", target_bir_lowering=False)
    q = Prog(nc)
    X = p1_drams(nc, xa_kind, debug)
    emit_p1(nc, q, X, debug=debug)
    with ExitStack() as fst:
        q.finalize(fst, serial=serial, noskip=noskip)
    return nc


def stage_p1_inputs(ins, core):
    b, ch = core >> 1, core & 1
    x = np.asarray(ins["L0_attn_x"])
    mine = slice(ch * EH, (ch + 1) * EH)
    other = slice((1 - ch) * EH, (2 - ch) * EH)
    perm = np.r_[np.arange(ch * EH, (ch + 1) * EH),
                 np.arange((1 - ch) * EH, (2 - ch) * EH)]
    inpw = np.asarray(ins["L0_attn_in_proj_w"])
    d = {
        "xb": np.ascontiguousarray(x[b]),
        "inpw_x": np.ascontiguousarray(inpw[:, perm]),
        "inpw_z": np.ascontiguousarray(inpw[:, 1024 + ch * EH:1024 + (ch + 1) * EH]),
        "ln1_g": np.ascontiguousarray(ins["L0_attn_ln1_g"]),
        "ln1_b": np.ascontiguousarray(ins["L0_attn_ln1_b"]),
        "rms_w": np.ascontiguousarray(ins["L0_attn_rms_w"]),
        "conv_w": np.ascontiguousarray(np.asarray(ins["L0_attn_conv_w"])[perm]),
        "conv_b": np.ascontiguousarray(np.asarray(ins["L0_attn_conv_b"])[perm]),
        "xprojw": np.ascontiguousarray(np.asarray(ins["L0_attn_x_proj_w"])[perm]),
        "dtw_h": np.ascontiguousarray(np.asarray(ins["L0_attn_dt_proj_w"])[:, mine]),
        "dtb_h": np.ascontiguousarray(np.asarray(ins["L0_attn_dt_proj_b"])[mine]),
        "alog_h": np.ascontiguousarray(np.asarray(ins["L0_attn_A_log"])[mine]),
        "dp_h": np.ascontiguousarray(np.asarray(ins["L0_attn_D_param"])[mine]),
        "opw": np.ascontiguousarray(ins["L0_attn_out_proj_w"]),
        "x0h": np.ascontiguousarray(
            x[b].reshape(NSLAB, ST, D)[ch::2].reshape(T // 2, D)),
        "blendm": np.array([float(ch)], np.float32),
    }
    return {k: np.asarray(v, dtype=np.float32) for k, v in d.items()}




def p2_drams(nc, xa_kind="ExternalInput", x1_kind="ExternalOutput",
             x0h=None, xa_in=None):
    dram = lambda n, s, dt=F32, k="ExternalInput", **kw: nc.dram_tensor(
        n, s, dt, kind=k, **kw).ap()
    X = {}
    X["xa_in"] = xa_in if xa_in is not None else \
        dram("xa_in", [T // 2, D], k=xa_kind)
    X["x0h"] = x0h if x0h is not None else dram("x0h", [T // 2, D])
    X["w1d"] = dram("w1d", [D, 4 * D])
    X["b1d"] = dram("b1d", [4 * D])
    X["w2d"] = dram("w2d", [4 * D, D])
    X["b2d"] = dram("b2d", [D])
    X["g2d"] = dram("g2d", [D])
    X["bt2d"] = dram("bt2d", [D])
    X["x1_out"] = dram("x1_out", [T // 2, D], k=x1_kind)
    return X


def emit_p2(nc, q, X):
    (xa_in, x0h, w1d, b1d, w2d, b2d, g2d, bt2d, x1_out) = (X[k] for k in (
        "xa_in", "x0h", "w1d", "b1d", "w2d", "b2d", "g2d", "bt2d", "x1_out"))
    ST2 = 256
    NS2 = (T // 2) // ST2

    st = ExitStack()
    sb = lambda name, shape, dt=F32: st.enter_context(
        nc.sbuf_tensor("B_" + name, shape, dt)).ap()
    mkpsum = lambda name, shape: st.enter_context(
        nc.psum_tensor("B_" + name, shape, F32)).ap()
    B = Buf
    with st:
        w1 = [sb(f"w1_{k}", [128, 4 * D], BF16) for k in range(NCH)]
        w2 = [sb(f"w2_{k}", [128, D], BF16) for k in range(32)]
        b1c = sb("b1c", [128, 32])
        b2row = sb("b2row", [1, D], BF16)
        onesb = sb("onesb", [1, 128], BF16)
        g2c = sb("g2c", [128, NCH])
        bt2c = sb("bt2c", [128, NCH])
        identb = sb("identb", [128, 128], BF16)
        ident = sb("ident", [128, 128])
        eps_sb = sb("eps_sb", [128, 1])
        row_t = sb("row_t", [1, 1024])
        stage = sb("stage", [128, 2048])
        b1row = sb("b1row", [1, 4 * D])
        xat = sb("xat", [128, D])
        x0t = sb("x0t", [128, D])
        ln2t = sb("ln2t", [128, D])
        ln2fm = [sb(f"ln2fm{k}", [128, ST2], BF16) for k in range(NCH)]
        g1 = [sb(f"g1_{dc}", [128, ST2], BF16) for dc in range(32)]
        x1t = sb("x1t", [128, D])
        stat = sb("stat", [128, 12])
        mvt = sb("mvt", [128, 6])
        smallt = sb("smallt", [128, 8])
        pm = [mkpsum(f"pm{i}", [128, 512]) for i in range(2)]
        pt = mkpsum("pt", [128, 512])
        pop = mkpsum("pop", [128, D])
        pb1 = mkpsum("pb1", [128, 512])

        b_w1 = [B() for _ in range(NCH)]
        b_w2 = [B() for _ in range(32)]
        b_b1c = B(); b_b2r = B(); b_ones = B(); b_g2 = B(); b_bt2 = B()
        b_idb = B(); b_id = B(); b_eps = B(); b_row = B(); b_stage = B()
        b_b1row = B()
        b_xat = B(); b_x0t = B(); b_ln2t = B()
        b_ln2fm = [B() for _ in range(NCH)]
        b_g1 = [B() for _ in range(32)]
        b_x1t = B(); b_stat = B(); b_mvt = B(); b_small = B()
        b_pm = [B(), B()]; b_pt = B(); b_pop = B(); b_pb1 = B()

        def colload(vec, n, out_ap, out_buf):
            q.dma("sync", lambda e: e.dma_start(out=row_t[:, 0:n * 128],
                                                in_=vec[None, :]),
                  writes=[b_row], private=True)
            for c in range(n):
                q.op("tensor", lambda e, c=c: e.transpose(
                    pt[:, c:c + 1], in_=row_t[:, c * 128:(c + 1) * 128],
                    identity=ident[0:1, 0:1]),
                    reads=[b_row, b_id], writes=[b_pt])
            q.op("scalar", lambda e: e.activation(
                out=out_ap, in_=pt[:, 0:n], func=AF.Copy),
                reads=[b_pt], writes=[out_buf])

        def lnstats(xap, xbuf, mu_sl, r_sl):
            for i in range(2):
                q.op("vector", lambda e, i=i: e.bn_stats(
                    out=stat[:, 6 * i:6 * i + 6],
                    in_=xap[:, 512 * i:512 * (i + 1)]),
                    reads=[xbuf], writes=[b_stat])
            q.op("vector", lambda e: e.bn_aggr(out=mvt[:, 0:2],
                                               in_=stat[:, 0:6]),
                 reads=[b_stat], writes=[b_mvt])
            q.op("vector", lambda e: e.bn_aggr(out=mvt[:, 2:4],
                                               in_=stat[:, 6:12]),
                 reads=[b_stat], writes=[b_mvt])
            q.op("vector", lambda e: e.tensor_scalar(
                out=mu_sl, in0=mvt[:, 0:1], scalar1=mvt[:, 2:3],
                scalar2=0.5, op0=OP.add, op1=OP.mult),
                reads=[b_mvt], sreads=[b_mvt], writes=[b_small])
            q.op("vector", lambda e: e.tensor_scalar(
                out=smallt[:, 4:5], in0=mvt[:, 0:1], scalar1=mvt[:, 2:3],
                scalar2=None, op0=OP.subtract),
                reads=[b_mvt], sreads=[b_mvt], writes=[b_small])
            q.op("scalar", lambda e: e.activation(
                out=smallt[:, 5:6], in_=smallt[:, 4:5], func=AF.Square,
                scale=0.5), reads=[b_small], writes=[b_small])
            q.op("vector", lambda e: e.tensor_scalar(
                out=smallt[:, 6:7], in0=mvt[:, 1:2], scalar1=mvt[:, 3:4],
                scalar2=0.5, op0=OP.add, op1=OP.mult),
                reads=[b_mvt], sreads=[b_mvt], writes=[b_small])
            q.op("vector", lambda e: e.tensor_scalar(
                out=smallt[:, 6:7], in0=smallt[:, 6:7],
                scalar1=smallt[:, 5:6], scalar2=None, op0=OP.add),
                reads=[b_small], sreads=[b_small], writes=[b_small])
            q.op("scalar", lambda e: e.activation(
                out=smallt[:, 7:8], in_=smallt[:, 6:7], func=AF.Sqrt,
                bias=eps_sb),
                reads=[b_small], sreads=[b_eps], writes=[b_small])
            q.op("vector", lambda e: e.reciprocal(out=r_sl,
                                                  in_=smallt[:, 7:8]),
                 reads=[b_small], writes=[b_small])

        q.lbl = "p2_wprep"
        q.op("vector", lambda e: e.memset(eps_sb, EPS), writes=[b_eps])
        q.op("gpsimd", lambda e: e.memset(ident, 0.0), writes=[b_id])
        q.op("gpsimd", lambda e: e.affine_select(
            out=ident, in_=ident, compare_op=OP.not_equal, fill=1.0, base=0,
            pattern=[[-1, 128]], channel_multiplier=1),
            reads=[b_id], writes=[b_id])
        q.op("vector", lambda e: e.tensor_copy(out=identb, in_=ident),
             reads=[b_id], writes=[b_idb])
        q.op("vector", lambda e: e.memset(onesb, 1.0), writes=[b_ones])
        colload(g2d, NCH, g2c, b_g2)
        colload(bt2d, NCH, bt2c, b_bt2)
        q.dma("gpsimd", lambda e: e.dma_start(out=b2row, in_=b2d[None, :]),
              writes=[b_b2r], private=True)
        q.dma("sync", lambda e: e.dma_start(out=b1row, in_=b1d[None, :]),
              writes=[b_b1row], private=True)
        for k in range(NCH):
            for h in range(2):
                q.dma("sync", lambda e, k=k, h=h: e.dma_start(
                    out=stage, in_=w1d[k * 128:(k + 1) * 128,
                                       h * 2048:(h + 1) * 2048]),
                    writes=[b_stage])
                q.op("vector", lambda e, k=k, h=h: e.tensor_scalar(
                    out=w1[k][:, h * 2048:(h + 1) * 2048], in0=stage,
                    scalar1=g2c[:, k:k + 1], scalar2=None, op0=OP.mult),
                    reads=[b_stage], sreads=[b_g2], writes=[b_w1[k]])
        for g in range(8):
            for k in range(NCH):
                q.dma("sync", lambda e, k=k, g=g: e.dma_start(
                    out=stage[:, 0:512],
                    in_=w1d[k * 128:(k + 1) * 128, g * 512:(g + 1) * 512]),
                    writes=[b_stage])
                q.op("tensor", lambda e, k=k, g=g: e.matmul(
                    pb1[0:1, :], lhsT=bt2c[:, k:k + 1], rhs=stage[:, 0:512],
                    start=(k == 0), stop=(k == NCH - 1)),
                    reads=[b_stage, b_bt2], writes=[b_pb1])
            q.op("vector", lambda e, g=g: e.tensor_tensor(
                out=row_t[0:1, 0:512], in0=pb1[0:1, 0:512],
                in1=b1row[0:1, g * 512:(g + 1) * 512], op=OP.add),
                reads=[b_pb1, b_b1row], writes=[b_row])
            for c2 in range(4):
                q.op("tensor", lambda e, g=g, c2=c2: e.transpose(
                    pt[:, g * 4 + c2:g * 4 + c2 + 1],
                    in_=row_t[0:1, c2 * 128:(c2 + 1) * 128],
                    identity=ident[0:1, 0:1]),
                    reads=[b_row, b_id], writes=[b_pt])
        q.op("scalar", lambda e: e.activation(
            out=b1c, in_=pt[:, 0:32], func=AF.Copy),
            reads=[b_pt], writes=[b_b1c])
        for kc in range(32):
            q.dma("sync", lambda e, kc=kc: e.dma_start(
                out=stage[:, 0:1024], in_=w2d[kc * 128:(kc + 1) * 128, :]),
                writes=[b_stage])
            q.op("scalar", lambda e, kc=kc: e.activation(
                out=w2[kc], in_=stage[:, 0:1024], func=AF.Copy),
                reads=[b_stage], writes=[b_w2[kc]])

        q.lbl = "p2_slab"
        for s2 in range(NS2):
            for tt in range(2):
                r0 = s2 * ST2 + tt * 128
                q.dma("sync", lambda e, r0=r0: e.dma_start(
                    out=xat, in_=xa_in[r0:r0 + 128, :]), writes=[b_xat],
                    stream="xain")
                lnstats(xat, b_xat, smallt[:, 0:1], smallt[:, 1:2])
                q.op("vector", lambda e: e.tensor_scalar(
                    out=ln2t, in0=xat, scalar1=smallt[:, 0:1],
                    scalar2=smallt[:, 1:2], op0=OP.subtract, op1=OP.mult),
                    reads=[b_xat], sreads=[b_small], writes=[b_ln2t])
                for k in range(NCH):
                    q.op("tensor", lambda e, k=k: e.transpose(
                        pt[:, 0:128], in_=ln2t[:, k * 128:(k + 1) * 128],
                        identity=ident),
                        reads=[b_ln2t, b_id], writes=[b_pt])
                    q.op("scalar", lambda e, k=k, tt=tt: e.activation(
                        out=ln2fm[k][:, tt * 128:(tt + 1) * 128],
                        in_=pt[:, 0:128], func=AF.Copy),
                        reads=[b_pt], writes=[b_ln2fm[k]])
            for dc in range(32):
                pmx = dc % 2
                for k in range(NCH):
                    q.op("tensor", lambda e, dc=dc, k=k, pmx=pmx: e.matmul(
                        pm[pmx][:, 0:ST2],
                        lhsT=w1[k][:, dc * 128:(dc + 1) * 128],
                        rhs=ln2fm[k], start=(k == 0), stop=(k == NCH - 1)),
                        reads=[b_w1[k], b_ln2fm[k]], writes=[b_pm[pmx]])
                q.op("scalar", lambda e, dc=dc, pmx=pmx: e.activation(
                    out=g1[dc], in_=pm[pmx][:, 0:ST2], func=AF.Relu,
                    bias=b1c[:, dc:dc + 1]),
                    reads=[b_pm[pmx]], sreads=[b_b1c], writes=[b_g1[dc]])
            for tt in range(2):
                r0 = s2 * ST2 + tt * 128
                for hh in range(2):
                    for kc in range(32):
                        q.op("tensor", lambda e, kc=kc, tt=tt, hh=hh: e.matmul(
                            pop[:, hh * 512:(hh + 1) * 512],
                            lhsT=g1[kc][:, tt * 128:(tt + 1) * 128],
                            rhs=w2[kc][:, hh * 512:(hh + 1) * 512],
                            start=(kc == 0), stop=False),
                            reads=[b_g1[kc], b_w2[kc]], writes=[b_pop])
                    q.op("tensor", lambda e, tt=tt, hh=hh: e.matmul(
                        pop[:, hh * 512:(hh + 1) * 512], lhsT=onesb,
                        rhs=b2row[0:1, hh * 512:(hh + 1) * 512],
                        start=False, stop=True),
                        reads=[b_ones, b_b2r], writes=[b_pop])
                q.dma("sync", lambda e, r0=r0: e.dma_start(
                    out=xat, in_=xa_in[r0:r0 + 128, :]), writes=[b_xat],
                    stream="xain")
                q.dma("sync", lambda e, r0=r0: e.dma_start(
                    out=x0t, in_=x0h[r0:r0 + 128, :]), writes=[b_x0t],
                    stream="x0p2")
                q.op("vector", lambda e: e.tensor_add(
                    out=x1t, in0=xat, in1=pop),
                    reads=[b_xat, b_pop], writes=[b_x1t])
                q.op("vector", lambda e: e.tensor_add(
                    out=x1t, in0=x1t, in1=x0t),
                    reads=[b_x1t, b_x0t], writes=[b_x1t])
                q.dma("sync", lambda e, r0=r0: e.dma_start(
                    out=x1_out[r0:r0 + 128, :], in_=x1t),
                    reads=[b_x1t], stream="x1out")


def build_p2(noskip=True):
    nc = bass.Bass("TRN2", target_bir_lowering=False)
    q = Prog(nc)
    X = p2_drams(nc)
    emit_p2(nc, q, X)
    with ExitStack() as fst:
        q.finalize(fst, noskip=noskip)
    return nc


def stage_p2_inputs(ins, core, xa=None):
    b, ch = core >> 1, core & 1
    x = np.asarray(ins["L0_attn_x"])
    d = {
        "x0h": np.ascontiguousarray(
            x[b].reshape(NSLAB, ST, D)[ch::2].reshape(T // 2, D)),
        "w1d": np.ascontiguousarray(ins["L0_attn_ffn_w1"]),
        "b1d": np.ascontiguousarray(ins["L0_attn_ffn_b1"]),
        "w2d": np.ascontiguousarray(ins["L0_attn_ffn_w2"]),
        "b2d": np.ascontiguousarray(ins["L0_attn_ffn_b2"]),
        "g2d": np.ascontiguousarray(ins["L0_attn_ln2_g"]),
        "bt2d": np.ascontiguousarray(ins["L0_attn_ln2_b"]),
    }
    if xa is not None:
        d["xa_in"] = np.ascontiguousarray(xa)
    return {k: np.asarray(v, dtype=np.float32) for k, v in d.items()}




IC3 = [(0, 512), (512, 512), (1024, 512), (1536, 512), (2048, 512),
       (2560, 256)]


def p3_drams(nc, x1_kind="ExternalInput", out_kind="ExternalOutput",
             x1_in=None):
    dram = lambda n, s, dt=F32, k="ExternalInput", **kw: nc.dram_tensor(
        n, s, dt, kind=k, **kw).ap()
    X = {}
    X["x1_in"] = x1_in if x1_in is not None else \
        dram("x1_in", [T // 2, D], k=x1_kind)
    X["wg_sl"] = dram("wg_sl", [3 * 128, D])
    X["wu_sl"] = dram("wu_sl", [3 * 128, D])
    X["wd_sl"] = dram("wd_sl", [128, 2816])
    X["out"] = dram("out", [T // 2, D], k=out_kind)
    X["agw1_in"] = dram("agw1_in", [6, 128, D], BF16, k="Internal")
    X["agw1_out"] = dram("agw1_out", [8, 6, 128, D], BF16, k="Internal",
                         addr_space="Local")
    X["agw2_in"] = dram("agw2_in", [128, 2816], BF16, k="Internal")
    X["agw2_out"] = dram("agw2_out", [8, 128, 2816], BF16, k="Internal",
                         addr_space="Local")
    return X


def emit_p3(nc, q, X):
    (x1_in, wg_sl, wu_sl, wd_sl, out_d, agw1_in, agw1_out, agw2_in,
     agw2_out) = (X[k] for k in (
        "x1_in", "wg_sl", "wu_sl", "wd_sl", "out", "agw1_in", "agw1_out",
        "agw2_in", "agw2_out"))
    ST3 = 128
    NS3 = (T // 2) // ST3

    st = ExitStack()
    sb = lambda name, shape, dt=F32: st.enter_context(
        nc.sbuf_tensor("C_" + name, shape, dt)).ap()
    mkpsum = lambda name, shape: st.enter_context(
        nc.psum_tensor("C_" + name, shape, F32)).ap()
    B = Buf
    with st:
        wgT = [sb(f"wgT{k}", [128, 2816], BF16) for k in range(NCH)]
        wuT = [sb(f"wuT{k}", [128, 2816], BF16) for k in range(NCH)]
        wdT = [sb(f"wdT{k}", [128, D], BF16) for k in range(22)]
        ident = sb("ident", [128, 128])
        identb = sb("identb", [128, 128], BF16)
        eps_sb = sb("eps_sb", [128, 1])
        stage = sb("stage", [128, 2816])
        stage_b = sb("stage_b", [128, 2816], BF16)
        qa = sb("qa", [128, 1024])
        qxn = sb("qxn", [128, 1024])
        qb2 = sb("qb2", [128, 1024])
        qc = sb("qc", [128, 1024])
        qp3 = sb("qp3", [128, 1024])
        qsg = sb("qsg", [128, 1024], I32)
        qam = sb("qam", [128, 64])
        qr6 = sb("qr6", [128, 64])
        qs = sb("qs", [128, 64])
        x1t = sb("x1t", [128, D])
        ft = sb("ft", [128, D])
        fq = sb("fq", [128, D])
        fqfm = [sb(f"fqfm{k}", [128, ST3], BF16) for k in range(NCH)]
        ht = sb("ht", [128, 2816])
        hq = stage
        sg_t = sb("sg_t", [128, 512])
        hqfm = [sb(f"hqfm{k}", [128, ST3], BF16) for k in range(22)]
        x2t = ft
        stat = sb("stat", [128, 12])
        mvt = sb("mvt", [128, 6])
        smallt = sb("smallt", [128, 8])
        pg = mkpsum("pg", [128, 512])
        pu = mkpsum("pu", [128, 512])
        pt = mkpsum("pt", [128, 512])
        pd2 = mkpsum("pd2", [128, D])

        b_wgT = [B() for _ in range(NCH)]
        b_wuT = [B() for _ in range(NCH)]
        b_wdT = [B() for _ in range(22)]
        b_id = B(); b_idb = B(); b_eps = B(); b_stage = B(); b_stageb = B()
        b_qa = B(); b_qxn = B(); b_qb2 = B(); b_qc = B(); b_qp3 = B()
        b_qsg = B(); b_qam = B(); b_qr6 = B(); b_qs = B()
        b_x1t = B(); b_ft = B(); b_fq = B()
        b_fqfm = [B() for _ in range(NCH)]
        b_ht = B(); b_sg = B()
        b_hq = b_stage
        b_hqfm = [B() for _ in range(22)]
        b_stat = B(); b_mvt = B(); b_small = B()
        b_x2t = b_ft
        b_pg = B(); b_pu = B(); b_pt = B(); b_pd2 = B()
        b_agw1i = B(); b_agw1o = B(); b_agw2i = B(); b_agw2o = B()

        def quant(x_ap, x_buf, out_ap, out_buf, nblk, gq="gpsimd"):
            scr = dict(a=(qa, b_qa), xn=(qxn, b_qxn), b2=(qb2, b_qb2),
                       c=(qc, b_qc), p3=(qp3, b_qp3), sg=(qsg, b_qsg),
                       am=(qam, b_qam), r6=(qr6, b_qr6), s=(qs, b_qs))
            n16 = nblk * 16
            q.op("vector", lambda e: e.tensor_scalar(
                out=qsg[:, 0:n16], in0=x_ap.bitcast(I32),
                scalar1=-0x80000000, scalar2=None, op0=OP.bitwise_and),
                reads=[x_buf], writes=[b_qsg])
            q.op("vector", lambda e: e.tensor_scalar(
                out=qa.bitcast(I32)[:, 0:n16], in0=x_ap.bitcast(I32),
                scalar1=0x7FFFFFFF, scalar2=None, op0=OP.bitwise_and),
                reads=[x_buf], writes=[b_qa])
            q.op("vector", lambda e: e.tensor_reduce(
                out=qam[:, 0:nblk],
                in_=qa[:, 0:n16].rearrange("p (b k) -> p b k", k=16),
                axis=AX.X, op=OP.max), reads=[b_qa], writes=[b_qam])
            q.op("vector", lambda e: e.reciprocal(
                out=qr6[:, 0:nblk], in_=qam[:, 0:nblk]),
                sreads=[b_qam], writes=[b_qr6])
            q.op("vector", lambda e: e.tensor_single_scalar(
                out=qr6[:, 0:nblk], in_=qr6[:, 0:nblk], scalar=6.0,
                op=OP.mult), sreads=[b_qr6], writes=[b_qr6])
            q.op("vector", lambda e: e.tensor_single_scalar(
                out=qs[:, 0:nblk], in_=qam[:, 0:nblk], scalar=1.0 / 6.0,
                op=OP.mult), sreads=[b_qam], writes=[b_qs])
            q.op(gq, lambda e: e.tensor_tensor(
                out=qxn[:, 0:n16].rearrange("p (b k) -> p b k", k=16),
                in0=qa[:, 0:n16].rearrange("p (b k) -> p b k", k=16),
                in1=qr6[:, 0:nblk].unsqueeze(2).broadcast_to([128, nblk, 16]),
                op=OP.mult), reads=[b_qa, b_qr6], writes=[b_qxn])
            q.op("vector", lambda e: e.tensor_scalar(
                out=qb2[:, 0:n16], in0=qxn[:, 0:n16], scalar1=2.0,
                scalar2=MAGIC, op0=OP.mult, op1=OP.add),
                reads=[b_qxn], writes=[b_qb2])
            q.op("vector", lambda e: e.tensor_scalar(
                out=qb2[:, 0:n16], in0=qb2[:, 0:n16], scalar1=MAGIC,
                scalar2=0.5, op0=OP.subtract, op1=OP.mult),
                reads=[b_qb2], writes=[b_qb2])
            q.op("vector", lambda e: e.tensor_single_scalar(
                out=qb2[:, 0:n16], in_=qb2[:, 0:n16], scalar=2.0, op=OP.min),
                reads=[b_qb2], writes=[b_qb2])
            q.op("vector", lambda e: e.tensor_scalar(
                out=qc[:, 0:n16], in0=qxn[:, 0:n16], scalar1=MAGIC,
                scalar2=MAGIC, op0=OP.add, op1=OP.subtract),
                reads=[b_qxn], writes=[b_qc])
            q.op("vector", lambda e: e.tensor_scalar(
                out=qc[:, 0:n16], in0=qc[:, 0:n16], scalar1=4.0, scalar2=2.0,
                op0=OP.min, op1=OP.subtract), reads=[b_qc], writes=[b_qc])
            q.op("vector", lambda e: e.scalar_tensor_tensor(
                out=qc[:, 0:n16], in0=qc[:, 0:n16], scalar=0.0,
                in1=qb2[:, 0:n16], op0=OP.max, op1=OP.add),
                reads=[b_qc, b_qb2], writes=[b_qc])
            q.op("vector", lambda e: e.tensor_scalar(
                out=qp3[:, 0:n16], in0=qxn[:, 0:n16], scalar1=5.0,
                scalar2=2.0, op0=OP.is_gt, op1=OP.mult),
                reads=[b_qxn], writes=[b_qp3])
            q.op(gq, lambda e: e.tensor_tensor(
                out=qc[:, 0:n16], in0=qc[:, 0:n16], in1=qp3[:, 0:n16],
                op=OP.add), reads=[b_qc, b_qp3], writes=[b_qc])
            q.op(gq, lambda e: e.tensor_tensor(
                out=qxn[:, 0:n16].rearrange("p (b k) -> p b k", k=16),
                in0=qc[:, 0:n16].rearrange("p (b k) -> p b k", k=16),
                in1=qs[:, 0:nblk].unsqueeze(2).broadcast_to([128, nblk, 16]),
                op=OP.mult), reads=[b_qc, b_qs], writes=[b_qxn])
            q.op("vector", lambda e: e.tensor_tensor(
                out=out_ap.bitcast(I32), in0=qxn.bitcast(I32)[:, 0:n16],
                in1=qsg[:, 0:n16], op=OP.bitwise_or),
                reads=[b_qxn, b_qsg], writes=[out_buf])

        def lnstats(xap, xbuf, mu_sl, r_sl):
            for i in range(2):
                q.op("vector", lambda e, i=i: e.bn_stats(
                    out=stat[:, 6 * i:6 * i + 6],
                    in_=xap[:, 512 * i:512 * (i + 1)]),
                    reads=[xbuf], writes=[b_stat])
            q.op("vector", lambda e: e.bn_aggr(out=mvt[:, 0:2],
                                               in_=stat[:, 0:6]),
                 reads=[b_stat], writes=[b_mvt])
            q.op("vector", lambda e: e.bn_aggr(out=mvt[:, 2:4],
                                               in_=stat[:, 6:12]),
                 reads=[b_stat], writes=[b_mvt])
            q.op("vector", lambda e: e.tensor_scalar(
                out=mu_sl, in0=mvt[:, 0:1], scalar1=mvt[:, 2:3],
                scalar2=0.5, op0=OP.add, op1=OP.mult),
                reads=[b_mvt], sreads=[b_mvt], writes=[b_small])
            q.op("vector", lambda e: e.tensor_scalar(
                out=smallt[:, 4:5], in0=mvt[:, 0:1], scalar1=mvt[:, 2:3],
                scalar2=None, op0=OP.subtract),
                reads=[b_mvt], sreads=[b_mvt], writes=[b_small])
            q.op("scalar", lambda e: e.activation(
                out=smallt[:, 5:6], in_=smallt[:, 4:5], func=AF.Square,
                scale=0.5), reads=[b_small], writes=[b_small])
            q.op("vector", lambda e: e.tensor_scalar(
                out=smallt[:, 6:7], in0=mvt[:, 1:2], scalar1=mvt[:, 3:4],
                scalar2=0.5, op0=OP.add, op1=OP.mult),
                reads=[b_mvt], sreads=[b_mvt], writes=[b_small])
            q.op("vector", lambda e: e.tensor_scalar(
                out=smallt[:, 6:7], in0=smallt[:, 6:7],
                scalar1=smallt[:, 5:6], scalar2=None, op0=OP.add),
                reads=[b_small], sreads=[b_small], writes=[b_small])
            q.op("scalar", lambda e: e.activation(
                out=smallt[:, 7:8], in_=smallt[:, 6:7], func=AF.Sqrt,
                bias=eps_sb),
                reads=[b_small], sreads=[b_eps], writes=[b_small])
            q.op("vector", lambda e: e.reciprocal(out=r_sl,
                                                  in_=smallt[:, 7:8]),
                 reads=[b_small], writes=[b_small])

        q.lbl = "p3_wprep"
        q.op("vector", lambda e: e.memset(eps_sb, EPS), writes=[b_eps])
        q.op("gpsimd", lambda e: e.memset(ident, 0.0), writes=[b_id])
        q.op("gpsimd", lambda e: e.affine_select(
            out=ident, in_=ident, compare_op=OP.not_equal, fill=1.0, base=0,
            pattern=[[-1, 128]], channel_multiplier=1),
            reads=[b_id], writes=[b_id])
        q.op("vector", lambda e: e.tensor_copy(out=identb, in_=ident),
             reads=[b_id], writes=[b_idb])
        for sl in range(3):
            for wsrc, slot in ((wg_sl, sl), (wu_sl, 3 + sl)):
                q.dma("sync", lambda e, wsrc=wsrc, sl=sl: e.dma_start(
                    out=stage[:, 0:D], in_=wsrc[sl * 128:(sl + 1) * 128, :]),
                    writes=[b_stage])
                quant(stage[:, 0:D], b_stage, qp3[:, 0:D], b_qp3, 64)
                q.op("scalar", lambda e: e.activation(
                    out=stage_b[:, 0:D], in_=qp3[:, 0:D], func=AF.Copy),
                    reads=[b_qp3], writes=[b_stageb])
                q.dma("sync", lambda e, slot=slot: e.dma_start(
                    out=agw1_in[slot], in_=stage_b[:, 0:D]),
                    reads=[b_stageb], writes=[b_agw1i])
        q.dma("sync", lambda e: e.dma_start(out=stage, in_=wd_sl),
              writes=[b_stage])
        for off, w in ((0, 1024), (1024, 1024), (2048, 768)):
            quant(stage[:, off:off + w], b_stage, qp3[:, 0:w], b_qp3, w // 16)
            q.op("scalar", lambda e, off=off, w=w: e.activation(
                out=stage_b[:, off:off + w], in_=qp3[:, 0:w], func=AF.Copy),
                reads=[b_qp3], writes=[b_stageb])
        q.dma("sync", lambda e: e.dma_start(out=agw2_in, in_=stage_b),
              reads=[b_stageb], writes=[b_agw2i])
        ag1 = q.op("gpsimd", lambda e: e.collective_compute(
            "AllGather", OP.bypass,
            replica_groups=[list(range(8))],
            ins=[agw1_in], outs=[agw1_out]),
            reads=[b_agw1i], writes=[b_agw1o])
        ag2 = q.op("gpsimd", lambda e: e.collective_compute(
            "AllGather", OP.bypass,
            replica_groups=[list(range(8))],
            ins=[agw2_in], outs=[agw2_out]),
            reads=[b_agw2i], writes=[b_agw2o])
        for j in range(22):
            rk, sl = j % 8, j // 8
            for which, wt, bt, slot in (("g", wgT, b_wgT, sl),
                                        ("u", wuT, b_wuT, 3 + sl)):
                q.dma("sync", lambda e, rk=rk, slot=slot: e.dma_start(
                    out=stage_b[:, 0:D], in_=agw1_out[rk, slot]),
                    deps=[ag1], reads=[b_agw1o], writes=[b_stageb])
                for k in range(NCH):
                    q.op("tensor", lambda e, k=k: e.transpose(
                        pt[:, 0:128].bitcast(BF16)[:, 0:128],
                        in_=stage_b[:, k * 128:(k + 1) * 128],
                        identity=identb),
                        reads=[b_stageb, b_idb], writes=[b_pt])
                    q.op("scalar", lambda e, k=k, j=j, wt=wt, bt=bt:
                         e.activation(
                             out=wt[k][:, j * 128:(j + 1) * 128],
                             in_=pt[:, 0:128].bitcast(BF16)[:, 0:128],
                             func=AF.Copy),
                         reads=[b_pt], writes=[bt[k]])
        for j in range(8):
            q.dma("sync", lambda e, j=j: e.dma_start(
                out=stage_b, in_=agw2_out[j]),
                deps=[ag2], reads=[b_agw2o], writes=[b_stageb])
            for kc in range(22):
                q.op("tensor", lambda e, kc=kc: e.transpose(
                    pt[:, 0:128].bitcast(BF16)[:, 0:128],
                    in_=stage_b[:, kc * 128:(kc + 1) * 128],
                    identity=identb),
                    reads=[b_stageb, b_idb], writes=[b_pt])
                q.op("scalar", lambda e, kc=kc, j=j: e.activation(
                    out=wdT[kc][:, j * 128:(j + 1) * 128],
                    in_=pt[:, 0:128].bitcast(BF16)[:, 0:128],
                    func=AF.Copy),
                    reads=[b_pt], writes=[b_wdT[kc]])

        q.lbl = "p3_slab"
        for s3 in range(NS3):
            r0 = s3 * ST3
            q.dma("sync", lambda e, r0=r0: e.dma_start(
                out=x1t, in_=x1_in[r0:r0 + 128, :]), writes=[b_x1t],
                stream="x1in")
            lnstats(x1t, b_x1t, smallt[:, 0:1], smallt[:, 1:2])
            q.op("vector", lambda e: e.tensor_scalar(
                out=ft, in0=x1t, scalar1=smallt[:, 0:1],
                scalar2=smallt[:, 1:2], op0=OP.subtract, op1=OP.mult),
                reads=[b_x1t], sreads=[b_small], writes=[b_ft])
            quant(ft, b_ft, fq, b_fq, 64)
            for k in range(NCH):
                q.op("tensor", lambda e, k=k: e.transpose(
                    pt[:, 0:128], in_=fq[:, k * 128:(k + 1) * 128],
                    identity=ident),
                    reads=[b_fq, b_id], writes=[b_pt])
                q.op("scalar", lambda e, k=k: e.activation(
                    out=fqfm[k], in_=pt[:, 0:128], func=AF.Copy),
                    reads=[b_pt], writes=[b_fqfm[k]])
            for ic, (off, w) in enumerate(IC3):
                for k in range(NCH):
                    q.op("tensor", lambda e, k=k, off=off, w=w: e.matmul(
                        pg[:, 0:w], lhsT=fqfm[k],
                        rhs=wgT[k][:, off:off + w],
                        start=(k == 0), stop=(k == NCH - 1)),
                        reads=[b_fqfm[k], b_wgT[k]], writes=[b_pg])
                for k in range(NCH):
                    q.op("tensor", lambda e, k=k, off=off, w=w: e.matmul(
                        pu[:, 0:w], lhsT=fqfm[k],
                        rhs=wuT[k][:, off:off + w],
                        start=(k == 0), stop=(k == NCH - 1)),
                        reads=[b_fqfm[k], b_wuT[k]], writes=[b_pu])
                q.op("scalar", lambda e, off=off, w=w: e.activation(
                    out=sg_t[:, 0:w], in_=pg[:, 0:w], func=AF.Silu),
                    reads=[b_pg], writes=[b_sg])
                q.op("vector", lambda e, off=off, w=w: e.tensor_tensor(
                    out=ht[:, off:off + w], in0=sg_t[:, 0:w],
                    in1=pu[:, 0:w], op=OP.mult),
                    reads=[b_sg, b_pu], writes=[b_ht])
            for off, w in ((0, 1024), (1024, 1024), (2048, 768)):
                quant(ht[:, off:off + w], b_ht, hq[:, off:off + w], b_hq,
                      w // 16)
            for kc in range(22):
                q.op("tensor", lambda e, kc=kc: e.transpose(
                    pt[:, 0:128], in_=hq[:, kc * 128:(kc + 1) * 128],
                    identity=ident),
                    reads=[b_hq, b_id], writes=[b_pt])
                q.op("scalar", lambda e, kc=kc: e.activation(
                    out=hqfm[kc], in_=pt[:, 0:128], func=AF.Copy),
                    reads=[b_pt], writes=[b_hqfm[kc]])
            for hh in range(2):
                for kc in range(22):
                    q.op("tensor", lambda e, kc=kc, hh=hh: e.matmul(
                        pd2[:, hh * 512:(hh + 1) * 512], lhsT=hqfm[kc],
                        rhs=wdT[kc][:, hh * 512:(hh + 1) * 512],
                        start=(kc == 0), stop=(kc == 21)),
                        reads=[b_hqfm[kc], b_wdT[kc]], writes=[b_pd2])
            q.op("vector", lambda e: e.tensor_add(
                out=x2t, in0=x1t, in1=pd2),
                reads=[b_x1t, b_pd2], writes=[b_x2t])
            q.dma("sync", lambda e, r0=r0: e.dma_start(
                out=out_d[r0:r0 + 128, :], in_=x2t),
                reads=[b_x2t], stream="outd")


def build_p3(noskip=True):
    nc = bass.Bass("TRN2", target_bir_lowering=False)
    q = Prog(nc)
    X = p3_drams(nc)
    emit_p3(nc, q, X)
    with ExitStack() as fst:
        q.finalize(fst, noskip=noskip)
    return nc


def stage_p3_inputs(ins, core, x1=None):
    wg = np.asarray(ins["L0_ffn_w_gate"])
    wu = np.asarray(ins["L0_ffn_w_up"])
    wd = np.asarray(ins["L0_ffn_w_down"])
    c = core
    gsl = [min(c + 8 * i, 21) for i in range(3)]
    d = {
        "wg_sl": np.ascontiguousarray(np.concatenate(
            [wg[j * 128:(j + 1) * 128] for j in gsl], axis=0)),
        "wu_sl": np.ascontiguousarray(np.concatenate(
            [wu[j * 128:(j + 1) * 128] for j in gsl], axis=0)),
        "wd_sl": np.ascontiguousarray(wd[c * 128:(c + 1) * 128]),
    }
    if x1 is not None:
        d["x1_in"] = np.ascontiguousarray(x1)
    return {k: np.asarray(v, dtype=np.float32) for k, v in d.items()}


def build_fused(noskip=True, want_q=False):
    nc = bass.Bass("TRN2", target_bir_lowering=False)
    q = Prog(nc)
    X1 = p1_drams(nc, xa_kind="Internal")
    emit_p1(nc, q, X1)
    q.barrier()
    X2 = p2_drams(nc, x1_kind="Internal", x0h=X1["x0h"],
                  xa_in=X1["xa_out"])
    emit_p2(nc, q, X2)
    q.barrier()
    X3 = p3_drams(nc, x1_in=X2["x1_out"])
    emit_p3(nc, q, X3)
    with ExitStack() as fst:
        q.finalize(fst, noskip=noskip)
    return (nc, q) if want_q else nc


def stage_fused_inputs(ins, core):
    d = stage_p1_inputs(ins, core)
    d.update(stage_p2_inputs(ins, core))
    d.update(stage_p3_inputs(ins, core))
    return d


_BUILD_CACHE = {}


def _get(name, fn):
    if name not in _BUILD_CACHE:
        _BUILD_CACHE[name] = fn()
    return _BUILD_CACHE[name]


def kernel(**inputs):
    cores = list(range(8))
    nc = _get("fused", lambda: build_fused(noskip=False))
    maps = [stage_fused_inputs(inputs, c) for c in cores]
    r = run_bass_kernel_spmd(nc, maps, core_ids=cores)
    out = np.empty((4, T, D), np.float32)
    for c in cores:
        b, ch = c >> 1, c & 1
        out[b].reshape(NSLAB, ST, D)[ch::2] = \
            np.asarray(r.results[c]["out"]).reshape(NSLAB // 2, ST, D)
    return out
